# version 05
# baseline (speedup 1.0000x reference)
import numpy as np
import ml_dtypes
from contextlib import ExitStack

import concourse.bass as bass
import concourse.mybir as mybir
from concourse.bass_utils import run_bass_kernel_spmd

F32 = mybir.dt.float32
BF16 = mybir.dt.bfloat16
AF = mybir.ActivationFunctionType
ALU = mybir.AluOpType

NCORES = 8
T = 2048
D = 2048
KC = 16
TBW = 512
NTB = T // TBW
NG = KC * NTB
LN_EPS = 1e-5
ATT_SCALE = 1.0 / float(np.sqrt(np.float32(2048.0)))

DEBUG_TAPS = False
LAST_RESULTS = None
RACE_DETECT = False
N_SUB = 4


class Ctr:
    def __init__(self, sem):
        self.sem = sem
        self.n = 0

    def inc(self, inst, k=1):
        inst.then_inc(self.sem, k)
        self.n += k
        return self.n


def build_program():
    nc = bass.Bass("TRN2", target_bir_lowering=False, detect_race_conditions=RACE_DETECT)
    tap = "ExternalOutput" if DEBUG_TAPS else "Internal"

    x_in = nc.dram_tensor("x_in", [T, D], F32, kind="ExternalInput").ap()
    flag_in = nc.dram_tensor("flag", [128, 1], F32, kind="ExternalInput").ap()
    ones_in = nc.dram_tensor("ones_lhsT", [1, 128], BF16, kind="ExternalInput").ap()
    wq_in = nc.dram_tensor("wq_t", [KC, 128, D], F32, kind="ExternalInput").ap()
    wk_in = nc.dram_tensor("wk_t", [KC, 128, D], F32, kind="ExternalInput").ap()
    wv_in = nc.dram_tensor("wv_t", [KC, 128, D], F32, kind="ExternalInput").ap()
    wqa_in = nc.dram_tensor("wqa_t", [KC, 128, D], F32, kind="ExternalInput").ap()
    wka_in = nc.dram_tensor("wka_t", [KC, 128, D], F32, kind="ExternalInput").ap()
    bq_in = nc.dram_tensor("bq_c", [128, KC], F32, kind="ExternalInput").ap()
    bk_in = nc.dram_tensor("bk_c", [128, KC], F32, kind="ExternalInput").ap()
    bv_in = nc.dram_tensor("bv_c", [128, KC], F32, kind="ExternalInput").ap()
    bqa_in = nc.dram_tensor("bqa_c", [128, KC], F32, kind="ExternalInput").ap()
    bka_in = nc.dram_tensor("bka_c", [128, KC], F32, kind="ExternalInput").ap()
    g_in = nc.dram_tensor("g_row", [1, D], F32, kind="ExternalInput").ap()
    bln_in = nc.dram_tensor("b_row", [1, D], F32, kind="ExternalInput").ap()
    wbase_in = nc.dram_tensor("wbase_t", [2, 128, KC * 1024], F32, kind="ExternalInput").ap()
    rat_in = nc.dram_tensor("rat", [128, KC, 72], F32, kind="ExternalInput").ap()
    bt_in = nc.dram_tensor("bt", [64, D], F32, kind="ExternalInput").ap()
    bb_in = nc.dram_tensor("bb_in", [1, D], F32, kind="ExternalInput").ap()
    idn_in = nc.dram_tensor("idn", [128, 128], BF16, kind="ExternalInput").ap()
    qkvw_in = nc.dram_tensor("qkvw_t", [3, 2, 128, KC * 1024], F32, kind="ExternalInput").ap()
    wgate_in = nc.dram_tensor("wgate_t", [2, 128, KC * 1024], F32, kind="ExternalInput").ap()
    wout_in = nc.dram_tensor("wout_t", [2, 128, KC * 1024], F32, kind="ExternalInput").ap()
    slope_in = nc.dram_tensor("slope_col", [128, 16], F32, kind="ExternalInput").ap()
    iota1_in = nc.dram_tensor("iota1", [128, 2], F32, kind="ExternalInput").ap()
    iota2_in = nc.dram_tensor("iota2", [128, 2], F32, kind="ExternalInput").ap()
    c256_in = nc.dram_tensor("c256", [128, 1], F32, kind="ExternalInput").ap()
    cm_in = nc.dram_tensor("cm", [128, 2 * 256], F32, kind="ExternalInput").ap()
    iotabj_in = nc.dram_tensor("iota_bj", [128, 8], F32, kind="ExternalInput").ap()
    ind_in = nc.dram_tensor("ind_c", [128, 512], BF16, kind="ExternalInput").ap()
    nw_in = nc.dram_tensor("nw_in_row", [1, D], F32, kind="ExternalInput").ap()

    out_ap = nc.dram_tensor("out", [T, D], F32, kind="ExternalOutput").ap()

    u_sp = nc.dram_tensor("u_sp", [T, D], BF16, kind=tap).ap()
    q_sp = nc.dram_tensor("q_sp", [D, T], BF16, kind=tap).ap()
    k_sp = nc.dram_tensor("k_sp", [D, T], BF16, kind=tap).ap()
    v_sp = nc.dram_tensor("v_sp", [D, T], BF16, kind=tap).ap()
    w_sp = nc.dram_tensor("w_sp", [D, T], BF16, kind=tap).ap()
    hi_sp = nc.dram_tensor("hi_sp", [D, T], BF16, kind=tap).ap()
    lo_sp = nc.dram_tensor("lo_sp", [D, T], BF16, kind=tap).ap()
    x1_sp = nc.dram_tensor("x1_sp", [T, D], F32, kind="Internal").ap()
    x2_sp = nc.dram_tensor("x2_sp", [T, D], F32, kind="Internal").ap()
    x3_sp = nc.dram_tensor("x3_sp", [T, D], F32, kind="Internal").ap()
    dbg1 = nc.dram_tensor("dbg1", [128, D], BF16, kind=tap).ap()
    dbg2 = nc.dram_tensor("dbg2", [128, D], BF16, kind=tap).ap()
    dbg3 = nc.dram_tensor("dbg3", [128, 32], F32, kind=tap).ap()
    dbg4 = nc.dram_tensor("dbg4", [128, D], F32, kind=tap).ap()
    dbg5 = nc.dram_tensor("dbg5", [128, D], F32, kind=tap).ap()
    cc_in = nc.dram_tensor("cc_in", [128, KC], F32, kind="Internal")
    cc_out = nc.dram_tensor("cc_out", [256, KC], F32, kind="Internal")
    bq_sp = nc.dram_tensor("bq_sp", [T, D], BF16, kind="Internal").ap()
    bk_sp = nc.dram_tensor("bk_sp", [T, D], BF16, kind="Internal").ap()
    bv_sp = nc.dram_tensor("bv_sp", [T, D], BF16, kind="Internal").ap()
    gate_sp = nc.dram_tensor("gate_sp", [T, D], BF16, kind="Internal").ap()
    ao_sp = nc.dram_tensor("ao_sp", [T, D], BF16, kind="Internal").ap()
    gn_sp = nc.dram_tensor("gn_sp", [T, D], BF16, kind="Internal").ap()
    cc2_in = nc.dram_tensor("cc2_in", [128, D], F32, kind="Internal")
    cc2_out = nc.dram_tensor("cc2_out", [256, D], F32, kind="Internal")

    with ExitStack() as stk:
        e = stk.enter_context

        cPE = Ctr(e(nc.semaphore("pe")))
        cACT = Ctr(e(nc.semaphore("act")))
        cDVE = Ctr(e(nc.semaphore("dve")))
        cPOOL = Ctr(e(nc.semaphore("pool")))
        cCC = Ctr(e(nc.semaphore("cc")))
        cCI = Ctr(e(nc.semaphore("cc_in_st")))
        cGA = Ctr(e(nc.semaphore("gath_ld")))
        sIN = Ctr(e(nc.semaphore("s_init")))
        sXL = [Ctr(e(nc.semaphore(f"s_xl{i}"))) for i in range(2)]
        sST = [Ctr(e(nc.semaphore(f"s_st{i}"))) for i in range(2)]
        sWS = [Ctr(e(nc.semaphore(f"s_ws{i}"))) for i in range(2)]
        sRG = [Ctr(e(nc.semaphore(f"s_rg{i}"))) for i in range(2)]
        sCH = [Ctr(e(nc.semaphore(f"s_ch{i}"))) for i in range(2)]
        sSP = [Ctr(e(nc.semaphore(f"s_sp{i}"))) for i in range(2)]
        sXB = [Ctr(e(nc.semaphore(f"s_xb{i}"))) for i in range(2)]
        cQR = Ctr(e(nc.semaphore("s_qr")))
        sING = Ctr(e(nc.semaphore("s_init_g")))

        flag_t = e(nc.sbuf_tensor("flag_t", [128, 1], F32))
        ones_t = e(nc.sbuf_tensor("ones_t", [1, 128], BF16))
        bias_c = e(nc.sbuf_tensor("bias_c", [128, 5 * KC], F32))
        g_bc = e(nc.sbuf_tensor("g_bc", [128, D], BF16))
        b_bc = e(nc.sbuf_tensor("b_bc", [128, D], BF16))
        zeros_bf = e(nc.sbuf_tensor("zeros_bf", [128, D], BF16))
        stt = e(nc.sbuf_tensor("stt", [128, 32], F32))
        colA = e(nc.sbuf_tensor("colA", [128, KC, NTB], F32))
        colS = e(nc.sbuf_tensor("colS", [128, KC], F32))
        gath = e(nc.sbuf_tensor("gath", [128, KC], F32))
        ceff = e(nc.sbuf_tensor("ceff", [128, 4 * KC], F32))
        eps_t = e(nc.sbuf_tensor("eps_t", [128, 1], F32))
        eps2_t = e(nc.sbuf_tensor("eps2_t", [128, 1], F32))

        x_t2 = e(nc.sbuf_tensor("x_t2", [128, 2, D], F32))
        mvA = e(nc.sbuf_tensor("mvA", [128, KC, T], BF16))
        ch_re = e(nc.sbuf_tensor("ch_re", [128, 2, T], BF16))
        colT = e(nc.sbuf_tensor("colT", [128, KC], F32))
        scr1 = e(nc.sbuf_tensor("scr1", [128, 1], F32))
        out_t2 = e(nc.sbuf_tensor("out_t2", [128, 2, T], F32))
        astk = ExitStack()
        ae = astk.enter_context
        u1_scr = ae(nc.sbuf_tensor("u1_scr", [128, D], F32))
        h1_scr = ae(nc.sbuf_tensor("h1_scr", [128, 2, D], BF16))
        wstrip = ae(nc.sbuf_tensor("wstrip", [128, 2, KC, 128], BF16))
        kv_stage = ae(nc.sbuf_tensor("kv_stage", [128, 2, TBW], BF16))
        ringA = ae(nc.sbuf_tensor("ringA", [128, 2, T], BF16))
        tmp_f = ae(nc.sbuf_tensor("tmp_f", [128, 2, T], F32))
        hi_st = ae(nc.sbuf_tensor("hi_st", [128, 2, T], BF16))
        lo_st = ae(nc.sbuf_tensor("lo_st", [128, 2, T], BF16))
        hi_tm = ae(nc.sbuf_tensor("hi_tm", [128, 2, D], BF16))
        lo_tm = ae(nc.sbuf_tensor("lo_tm", [128, 2, D], BF16))

        pbank = [e(nc.psum_tensor(f"pb{i}", [128, TBW], F32)) for i in range(7)]
        pbank_bf = e(nc.psum_tensor("pbbf", [128, 1024], BF16))
        NB = 7
        bank_last = [(cACT, 0)] * NB
        bf_last = [cACT, 0]
        wres_pe = {"n": 0}

        all_ctrs = ([cPE, cACT, cDVE, cPOOL, cCC, cCI, cGA, sIN, cQR, sING]
                    + sXL + sST + sWS + sRG + sCH + sSP + sXB)
        with nc.Block() as blk:
            @blk.gpsimd
            def _(gp):
                for ctr in all_ctrs:
                    gp.sem_clear(ctr.sem)
        nc.all_engine_barrier()

        with ExitStack() as istk:
            grow_b = istk.enter_context(nc.sbuf_tensor("grow_b", [1, D], BF16))
            brow_b = istk.enter_context(nc.sbuf_tensor("brow_b", [1, D], BF16))

            with nc.Block() as blk:
                @blk.sync
                def _(sync):
                    for dst, src in ((flag_t[:, :], flag_in), (ones_t[:, :], ones_in),
                                     (bias_c[:, 0:KC], bq_in), (bias_c[:, KC:2 * KC], bk_in),
                                     (bias_c[:, 2 * KC:3 * KC], bv_in),
                                     (bias_c[:, 3 * KC:4 * KC], bqa_in),
                                     (bias_c[:, 4 * KC:5 * KC], bka_in)):
                        sIN.inc(sync.dma_start(out=dst, in_=src), 16)

                @blk.gpsimd
                def _(gp):
                    sING.inc(gp.dma_start(out=grow_b[:, :], in_=g_in), 16)
                    sING.inc(gp.dma_start(out=brow_b[:, :], in_=bln_in), 16)
                init_in = sIN.n
                init_ing = sING.n

                @blk.vector
                def _(v):
                    v.memset(zeros_bf[:, :], 0.0)
                    v.memset(eps_t[:, :], LN_EPS)
                    v.memset(eps2_t[:, :], LN_EPS * LN_EPS)
                    v.wait_ge(sIN.sem, init_in)
                    v.wait_ge(sING.sem, init_ing)
                    v.tensor_scalar_mul(bias_c[:, 3 * KC:4 * KC], bias_c[:, 3 * KC:4 * KC], ATT_SCALE)
                    cDVE.inc(v.tensor_scalar_mul(bias_c[:, 4 * KC:5 * KC], bias_c[:, 4 * KC:5 * KC],
                                                 ATT_SCALE))

                @blk.tensor
                def _(t):
                    t.wait_ge(sIN.sem, init_in)
                    t.wait_ge(sING.sem, init_ing)
                    inst = None
                    for j in range(NTB):
                        inst = t.matmul(pbank[j][:, :], ones_t[0:1, :],
                                        grow_b[0:1, j * TBW:(j + 1) * TBW], start=True, stop=True)
                    pe_g = cPE.inc(inst)
                    t.wait_ge(cACT.sem, cACT.n + 1)
                    inst = None
                    for j in range(NTB):
                        inst = t.matmul(pbank[j][:, :], ones_t[0:1, :],
                                        brow_b[0:1, j * TBW:(j + 1) * TBW], start=True, stop=True)
                    cPE.inc(inst)

                @blk.scalar
                def _(s):
                    s.wait_ge(cPE.sem, cPE.n - 1)
                    inst = None
                    for j in range(NTB):
                        inst = s.copy(g_bc[:, j * TBW:(j + 1) * TBW], pbank[j][:, :])
                    cACT.inc(inst)
                    s.wait_ge(cPE.sem, cPE.n)
                    inst = None
                    for j in range(NTB):
                        inst = s.copy(b_bc[:, j * TBW:(j + 1) * TBW], pbank[j][:, :])
                    cACT.inc(inst)
                for b_ in range(NB):
                    bank_last[b_] = (cACT, cACT.n)

        if DEBUG_TAPS:
            with nc.Block() as blk:
                @blk.sync
                def _(sync):
                    sync.wait_ge(cACT.sem, cACT.n)
                    sIN.inc(sync.dma_start(out=dbg1, in_=g_bc[:, :]), 16)
                    sIN.inc(sync.dma_start(out=dbg2, in_=b_bc[:, :]), 16)

        with nc.Block() as blk:
            v0, a0, p0 = cDVE.n, cACT.n, cPOOL.n
            xl0 = [sXL[0].n, sXL[1].n]
            st0 = [sST[0].n, sST[1].n]

            @blk.sync
            def _(sync):
                def store(j):
                    sync.wait_ge(cPOOL.sem, p0 + j + 1)
                    sST[j % 2].inc(sync.dma_start(out=u_sp[j * 128:(j + 1) * 128, :],
                                                  in_=h1_scr[:, j % 2]), 16)
                for i in range(KC):
                    if i >= 2:
                        sync.wait_ge(cDVE.sem, v0 + (i - 2) * 9 + 8)
                    sXL[i % 2].inc(sync.dma_start(out=x_t2[:, i % 2],
                                                  in_=x_in[i * 128:(i + 1) * 128, :]), 16)
                    if i >= 1:
                        store(i - 1)
                store(KC - 1)

            @blk.vector
            def _(v):
                for i in range(KC):
                    pi = (i % 2) * 16
                    xt = x_t2[:, i % 2]
                    hs = h1_scr[:, i % 2]
                    v.wait_ge(sXL[i % 2].sem, xl0[i % 2] + (i // 2 + 1) * 16)
                    if i >= 2:
                        v.wait_ge(cACT.sem, a0 + (i - 2) * 2 + 2)
                        v.wait_ge(cPOOL.sem, p0 + i - 1)
                        v.wait_ge(sST[i % 2].sem, st0[i % 2] + ((i - 2) // 2 + 1) * 16)
                    n = {}
                    def op(key, inst):
                        n[key] = cDVE.inc(inst)
                    op(1, v.tensor_scalar(hs, xt, 1.0, 0.0, ALU.mult, ALU.add,
                                          accum_out=stt[:, pi + 0:pi + 1]))
                    op(2, v.scalar_tensor_tensor(ch_re[:, 0, 0:D], xt, 1.0, xt, ALU.mult, ALU.mult,
                                                 accum_out=stt[:, pi + 1:pi + 2]))
                    v.wait_ge(cDVE.sem, n[1])
                    op(3, v.tensor_scalar_mul(stt[:, pi + 2:pi + 3], stt[:, pi + 0:pi + 1], 1.0 / D))
                    v.wait_ge(cDVE.sem, n[2])
                    op(4, v.tensor_scalar_mul(stt[:, pi + 3:pi + 4], stt[:, pi + 1:pi + 2], 1.0 / D))
                    v.wait_ge(cDVE.sem, n[3])
                    op(5, v.tensor_tensor(stt[:, pi + 4:pi + 5], stt[:, pi + 2:pi + 3],
                                          stt[:, pi + 2:pi + 3], ALU.mult))
                    v.wait_ge(cDVE.sem, n[5])
                    op(6, v.tensor_tensor(stt[:, pi + 5:pi + 6], stt[:, pi + 3:pi + 4],
                                          stt[:, pi + 4:pi + 5], ALU.subtract))
                    v.wait_ge(cDVE.sem, n[6])
                    op(7, v.tensor_scalar_mul(stt[:, pi + 8:pi + 9], stt[:, pi + 5:pi + 6],
                                              1.0 + LN_EPS))
                    v.wait_ge(cACT.sem, a0 + i * 2 + 2)
                    op(8, v.tensor_scalar(u1_scr[:, :], xt, stt[:, pi + 2:pi + 3], stt[:, pi + 6:pi + 7],
                                          ALU.subtract, ALU.mult))
                    v.wait_ge(cDVE.sem, n[8])
                    op(9, v.scalar_tensor_tensor(hs, u1_scr[:, :], 1.0, g_bc[:, :],
                                                 ALU.mult, ALU.mult))

            @blk.scalar
            def _(s):
                for i in range(KC):
                    pi = (i % 2) * 16
                    s.wait_ge(cDVE.sem, v0 + i * 9 + 7)
                    na = cACT.inc(s.activation(stt[:, pi + 7:pi + 8], stt[:, pi + 8:pi + 9], AF.Ln,
                                               bias=eps2_t[:, 0:1]))
                    s.wait_ge(cACT.sem, na)
                    cACT.inc(s.activation(stt[:, pi + 6:pi + 7], stt[:, pi + 7:pi + 8], AF.Exp, scale=-0.5))

            @blk.gpsimd
            def _(gp):
                for i in range(KC):
                    gp.wait_ge(cDVE.sem, v0 + i * 9 + 9)
                    if i >= 2:
                        gp.wait_ge(sST[i % 2].sem, st0[i % 2] + ((i - 2) // 2 + 1) * 16)
                    cPOOL.inc(gp.tensor_tensor(h1_scr[:, i % 2], h1_scr[:, i % 2], b_bc[:, :], ALU.add))

        u_stores = (sST[0].n, sST[1].n)

        def proj_B(w_dram, mov, evac, mov_ready=(), extra_stage_wait=None, post_sync=None):
            pe0, act0 = cPE.n, cACT.n
            ws0 = [sWS[0].n, sWS[1].n]
            prev_bank = list(bank_last)

            with nc.Block() as blk:
                @blk.gpsimd
                def _(gp):
                    for oc in range(KC):
                        if oc >= 2:
                            gp.wait_ge(cPE.sem, pe0 + (oc - 1) * NTB)
                        else:
                            gp.wait_ge(cPE.sem, pe0)
                        sWS[oc % 2].inc(gp.dma_start(
                            out=wstrip[:, oc % 2, :, :].rearrange("p k o -> p (k o)"),
                            in_=w_dram[oc]), 16)

                @blk.tensor
                def _(t):
                    for sem, val in mov_ready:
                        t.wait_ge(sem, val)
                    for oc in range(KC):
                        t.wait_ge(sWS[oc % 2].sem, ws0[oc % 2] + (oc // 2 + 1) * 16)
                        for tb in range(NTB):
                            g = oc * NTB + tb
                            if g >= NB:
                                t.wait_ge(cACT.sem, act0 + g - NB + 1)
                            else:
                                bc, bval = prev_bank[g % NB]
                                t.wait_ge(bc.sem, bval)
                            inst = None
                            for k in range(KC):
                                inst = t.matmul(pbank[g % NB][:, :], wstrip[:, oc % 2, k, :],
                                                mov[:, k, tb * TBW:(tb + 1) * TBW],
                                                start=(k == 0), stop=(k == KC - 1))
                            cPE.inc(inst)

                @blk.scalar
                def _(s):
                    for oc in range(KC):
                        for tb in range(NTB):
                            g = oc * NTB + tb
                            s.wait_ge(cPE.sem, pe0 + g + 1)
                            if extra_stage_wait is not None:
                                extra_stage_wait(s, g)
                            cACT.inc(evac(s, oc, tb, pbank[g % NB]))
                            bank_last[g % NB] = (cACT, cACT.n)

                if post_sync is not None:
                    @blk.sync
                    def _(sync):
                        post_sync(sync)

        def proj_spill(w_dram, mov, spill, bias_off, func, scale, accum, mov_ready=()):
            act0_local = cACT.n
            sp0 = [sSP[0].n, sSP[1].n]

            def evac(s, oc, tb, bank):
                g = oc * NTB + tb
                kw = {}
                if accum:
                    kw["accum_out"] = colA[:, oc, tb:tb + 1]
                return s.activation(kv_stage[:, g % 2], bank[:, :], func,
                                    bias=bias_c[:, bias_off + oc:bias_off + oc + 1],
                                    scale=scale, **kw)

            def stage_wait(s, g):
                if g >= 2:
                    s.wait_ge(sSP[g % 2].sem, sp0[g % 2] + ((g - 2) // 2 + 1) * 16)
                else:
                    s.wait_ge(sSP[g % 2].sem, sp0[g % 2])

            def post_sync(sync):
                for oc in range(KC):
                    for tb in range(NTB):
                        g = oc * NTB + tb
                        sync.wait_ge(cACT.sem, act0_local + g + 1)
                        sSP[g % 2].inc(sync.dma_start(
                            out=spill[oc * 128:(oc + 1) * 128, tb * TBW:(tb + 1) * TBW],
                            in_=kv_stage[:, g % 2]), 16)

            proj_B(w_dram, mov, evac, mov_ready=mov_ready,
                   extra_stage_wait=stage_wait, post_sync=post_sync)
            return (sSP[0].n, sSP[1].n)

        with nc.Block() as blk:
            @blk.scalar
            def _(s):
                s.wait_ge(sST[0].sem, u_stores[0])
                s.wait_ge(sST[1].sem, u_stores[1])
                for c in range(KC):
                    sXB[c % 2].inc(s.dma_start(out=mvA[:, c, :],
                                               in_=u_sp[:, c * 128:(c + 1) * 128], transpose=True), 16)
        h_ready = (sXB[0].n, sXB[1].n)

        hrdy = [(sXB[0].sem, h_ready[0]), (sXB[1].sem, h_ready[1])]
        q_stores = proj_spill(wq_in, mvA, q_sp, 0, AF.Identity, 1.0, False, mov_ready=hrdy)
        k_stores = proj_spill(wk_in, mvA, k_sp, KC, AF.Identity, 1.0, False)
        v_stores = proj_spill(wv_in, mvA, v_sp, 2 * KC, AF.Identity, 1.0, False)
        v_pe_done = cPE.n

        with nc.Block() as blk:
            @blk.sync
            def _(sync):
                sync.wait_ge(cPE.sem, v_pe_done)
                sync.wait_ge(sSP[0].sem, sSP[0].n)
                sync.wait_ge(sSP[1].sem, sSP[1].n)
                for c in range(KC):
                    cQR.inc(sync.dma_start(out=mvA[:, c, :], in_=q_sp[c * 128:(c + 1) * 128, :]), 16)
        q_loaded = cQR.n

        qw_stores = proj_spill(wqa_in, mvA, w_sp, 3 * KC, AF.Exp, ATT_SCALE, True,
                               mov_ready=[(cQR.sem, q_loaded)])
        a2_act = cACT.n

        def exchange(slot, wait_sem_val, from_colA=True):
            with nc.Block() as blk:
                if from_colA:
                    @blk.vector
                    def _(v):
                        v.wait_ge(*wait_sem_val)
                        cDVE.inc(v.tensor_tensor(colS[:, :], colA[:, :, 0], colA[:, :, 1], ALU.add))
                        nx_ = cDVE.inc(v.tensor_tensor(colT[:, :], colA[:, :, 2], colA[:, :, 3], ALU.add))
                        v.wait_ge(cDVE.sem, nx_)
                        cDVE.inc(v.tensor_tensor(colS[:, :], colS[:, :], colT[:, :], ALU.add))
                dve_sum = cDVE.n

                @blk.sync
                def _(sync):
                    if from_colA:
                        sync.wait_ge(cDVE.sem, dve_sum)
                    else:
                        sync.wait_ge(*wait_sem_val)
                    if slot > 0:
                        sync.wait_ge(cCC.sem, slot)
                    cCI.inc(sync.dma_start(out=cc_in[:, :], in_=colS[:, :]), 16)
                ci_val = cCI.n

                @blk.gpsimd
                def _(gp):
                    gp.wait_ge(cCI.sem, ci_val)
                    if slot > 0:
                        gp.wait_ge(cGA.sem, slot * 16)
                    cCC.inc(gp.collective_compute(
                        "AllGather", ALU.bypass,
                        replica_groups=[[0, 1], [2, 3], [4, 5], [6, 7]],
                        ins=[cc_in.ap().opt()], outs=[cc_out.ap().opt()]))
                    gp.wait_ge(cCC.sem, cCC.n)
                    cGA.inc(gp.dma_start(out=gath[:, :], in_=cc_out[0:128, :]), 16)
                ga_val = cGA.n

                @blk.vector
                def _(v):
                    v.wait_ge(cGA.sem, ga_val)
                    cDVE.inc(v.tensor_scalar_mul(ceff[:, slot * KC:(slot + 1) * KC],
                                                 gath[:, :], flag_t[:, 0:1]))
            return cDVE.n

        e1a_dve = exchange(0, (cACT.sem, a2_act))

        def scan_pass(src_sp, src_stores, ceff_slot, start_dve):
            with nc.Block() as blk:
                v0 = cDVE.n
                rg0 = [sRG[0].n, sRG[1].n]

                sp_tot = (sSP[0].n, sSP[1].n)

                @blk.sync
                def _(sync):
                    sync.wait_ge(sSP[0].sem, sp_tot[0])
                    sync.wait_ge(sSP[1].sem, sp_tot[1])
                    for c in range(KC):
                        p = c // 2
                        if p >= 1:
                            sync.wait_ge(cDVE.sem, v0 + p * 8)
                        sRG[c % 2].inc(sync.dma_start(out=ringA[:, c % 2],
                                                      in_=src_sp[c * 128:(c + 1) * 128, :]), 16)

                @blk.vector
                def _(v):
                    v.wait_ge(cDVE.sem, start_dve)
                    v.engine_nop()
                    prev_mul = {}
                    for cp in range(0, KC, 2):
                        ns = {}
                        for c in (cp, cp + 1):
                            v.wait_ge(sRG[c % 2].sem, rg0[c % 2] + (c // 2 + 1) * 16)
                            if c % 2 in prev_mul:
                                v.wait_ge(cDVE.sem, prev_mul[c % 2])
                            ns[("s", c)] = cDVE.inc(v.tensor_tensor_scan(
                                tmp_f[:, c % 2], ringA[:, c % 2], zeros_bf[:, :],
                                ceff[:, ceff_slot * KC + c:ceff_slot * KC + c + 1], ALU.add, ALU.add))
                        for c in (cp, cp + 1):
                            v.wait_ge(cDVE.sem, ns[("s", c)])
                            ns[("r", c)] = cDVE.inc(v.reciprocal(tmp_f[:, c % 2], tmp_f[:, c % 2]))
                        for c in (cp, cp + 1):
                            v.wait_ge(cDVE.sem, ns[("r", c)])
                            ns[("m", c)] = cDVE.inc(v.tensor_tensor(ringA[:, c % 2], ringA[:, c % 2],
                                                                    tmp_f[:, c % 2], ALU.mult))
                            prev_mul[c % 2] = ns[("m", c)]
                        for c in (cp, cp + 1):
                            v.wait_ge(cDVE.sem, ns[("m", c)])
                            cDVE.inc(v.scalar_tensor_tensor(mvA[:, c, :], ringA[:, c % 2], 1.0, mvA[:, c, :],
                                                            ALU.mult, ALU.mult, accum_out=colS[:, c:c + 1]))
            return cDVE.n

        a3_dve = scan_pass(w_sp, qw_stores, 0, e1a_dve)

        e1b_dve = exchange(1, (cDVE.sem, a3_dve), from_colA=False)

        def scan_mul_pass(src_sp, src_stores, ceff_slot, start_dve):
            with nc.Block() as blk:
                v0 = cDVE.n
                ch0 = [sCH[0].n, sCH[1].n]

                sp_tot = (sSP[0].n, sSP[1].n)

                @blk.sync
                def _(sync):
                    sync.wait_ge(sSP[0].sem, sp_tot[0])
                    sync.wait_ge(sSP[1].sem, sp_tot[1])
                    for c in range(KC):
                        p = c // 2
                        if p >= 1:
                            sync.wait_ge(cDVE.sem, v0 + p * 4)
                        sCH[c % 2].inc(sync.dma_start(out=ch_re[:, c % 2],
                                                      in_=src_sp[c * 128:(c + 1) * 128, :]), 16)

                @blk.vector
                def _(v):
                    v.wait_ge(cDVE.sem, start_dve)
                    v.engine_nop()
                    prev_mul = {}
                    for cp in range(0, KC, 2):
                        ns = {}
                        for c in (cp, cp + 1):
                            if c % 2 in prev_mul:
                                v.wait_ge(cDVE.sem, prev_mul[c % 2])
                            ns[c] = cDVE.inc(v.tensor_tensor_scan(
                                tmp_f[:, c % 2], mvA[:, c, :], zeros_bf[:, :],
                                ceff[:, ceff_slot * KC + c:ceff_slot * KC + c + 1], ALU.add, ALU.add))
                        for c in (cp, cp + 1):
                            v.wait_ge(sCH[c % 2].sem, ch0[c % 2] + (c // 2 + 1) * 16)
                            v.wait_ge(cDVE.sem, ns[c])
                            prev_mul[c % 2] = cDVE.inc(v.tensor_tensor(mvA[:, c, :], tmp_f[:, c % 2],
                                                                       ch_re[:, c % 2], ALU.mult))
            return cDVE.n

        a3b_dve = scan_mul_pass(k_sp, k_stores, 1, e1b_dve)

        kw_stores = proj_spill(wka_in, mvA, w_sp, 4 * KC, AF.Exp, ATT_SCALE, True,
                               mov_ready=[(cDVE.sem, a3b_dve)])
        a4_act = cACT.n

        e2a_dve = exchange(2, (cACT.sem, a4_act))

        a5a_dve = scan_pass(w_sp, kw_stores, 2, e2a_dve)

        e2b_dve = exchange(3, (cDVE.sem, a5a_dve), from_colA=False)

        with nc.Block() as blk:
            v0 = cDVE.n
            a0_ = cACT.n
            ch0 = [sCH[0].n, sCH[1].n]
            st0 = [sST[0].n, sST[1].n]

            @blk.sync
            def _(sync):
                def store(j):
                    sync.wait_ge(cACT.sem, a0_ + j + 1)
                    sync.wait_ge(cDVE.sem, v0 + (j // 2) * 6 + 5 + (j % 2))
                    sST[j % 2].inc(sync.dma_start(out=hi_sp[j * 128:(j + 1) * 128, :],
                                                  in_=hi_st[:, j % 2]), 16)
                    sST[j % 2].inc(sync.dma_start(out=lo_sp[j * 128:(j + 1) * 128, :],
                                                  in_=lo_st[:, j % 2]), 16)
                sync.wait_ge(sSP[0].sem, sSP[0].n)
                sync.wait_ge(sSP[1].sem, sSP[1].n)
                for c in range(KC):
                    p = c // 2
                    if p >= 1:
                        sync.wait_ge(cDVE.sem, v0 + p * 6)
                    sCH[c % 2].inc(sync.dma_start(out=ch_re[:, c % 2],
                                                  in_=v_sp[c * 128:(c + 1) * 128, :]), 16)
                    if c >= 1:
                        store(c - 1)
                store(KC - 1)

            @blk.vector
            def _(v):
                v.wait_ge(cDVE.sem, e2b_dve)
                v.engine_nop()
                prev_out = {}
                for cp in range(0, KC, 2):
                    ns = {}
                    for c in (cp, cp + 1):
                        if c % 2 in prev_out:
                            v.wait_ge(cDVE.sem, prev_out[c % 2])
                        ns[("s", c)] = cDVE.inc(v.tensor_tensor_scan(
                            tmp_f[:, c % 2], mvA[:, c, :], zeros_bf[:, :],
                            ceff[:, 3 * KC + c:3 * KC + c + 1], ALU.add, ALU.add))
                    for c in (cp, cp + 1):
                        v.wait_ge(sCH[c % 2].sem, ch0[c % 2] + (c // 2 + 1) * 16)
                        if c >= 2:
                            v.wait_ge(cACT.sem, a0_ + c - 1)
                        v.wait_ge(cDVE.sem, ns[("s", c)])
                        ns[("o", c)] = cDVE.inc(v.tensor_tensor(out_t2[:, c % 2], tmp_f[:, c % 2],
                                                                ch_re[:, c % 2], ALU.mult))
                        prev_out[c % 2] = ns[("o", c)]
                    for c in (cp, cp + 1):
                        v.wait_ge(cACT.sem, a0_ + c + 1)
                        if c >= 2:
                            v.wait_ge(sST[c % 2].sem, st0[c % 2] + ((c - 2) // 2 + 1) * 32)
                        v.wait_ge(cDVE.sem, ns[("o", c)])
                        cDVE.inc(v.tensor_tensor(lo_st[:, c % 2], out_t2[:, c % 2], hi_st[:, c % 2],
                                                 ALU.subtract))

            @blk.scalar
            def _(s):
                for c in range(KC):
                    s.wait_ge(cDVE.sem, v0 + (c // 2) * 6 + 3 + (c % 2))
                    if c >= 2:
                        s.wait_ge(sST[c % 2].sem, st0[c % 2] + ((c - 2) // 2 + 1) * 32)
                    cACT.inc(s.activation(hi_st[:, c % 2], out_t2[:, c % 2], AF.Copy))
        hilo_stores = (sST[0].n, sST[1].n)

        with nc.Block() as blk:
            v0 = cDVE.n
            xl0 = [sXL[0].n, sXL[1].n]
            st0 = [sST[0].n, sST[1].n]
            xb0 = [sXB[0].n, sXB[1].n]

            @blk.scalar
            def _(s):
                s.wait_ge(sST[0].sem, hilo_stores[0])
                s.wait_ge(sST[1].sem, hilo_stores[1])
                for i in range(KC):
                    if i >= 2:
                        s.wait_ge(cDVE.sem, v0 + (i - 2) * 2 + 1)
                    sXB[i % 2].inc(s.dma_start(out=hi_tm[:, i % 2],
                                               in_=hi_sp[:, i * 128:(i + 1) * 128], transpose=True), 16)
                    sXB[i % 2].inc(s.dma_start(out=lo_tm[:, i % 2],
                                               in_=lo_sp[:, i * 128:(i + 1) * 128], transpose=True), 16)

            @blk.sync
            def _(sync):
                def store(j):
                    sync.wait_ge(cDVE.sem, v0 + j * 2 + 2)
                    sST[j % 2].inc(sync.dma_start(out=x1_sp[j * 128:(j + 1) * 128, :],
                                                  in_=out_t2[:, j % 2]), 16)
                    if N_SUB == 1:
                        sST[j % 2].inc(sync.dma_start(out=out_ap[j * 128:(j + 1) * 128, :],
                                                      in_=out_t2[:, j % 2]), 16)
                for i in range(KC):
                    if i >= 2:
                        sync.wait_ge(cDVE.sem, v0 + (i - 2) * 2 + 2)
                    sXL[i % 2].inc(sync.dma_start(out=x_t2[:, i % 2],
                                                  in_=x_in[i * 128:(i + 1) * 128, :]), 16)
                    if i >= 1:
                        store(i - 1)
                store(KC - 1)
                sync.wait_ge(sST[0].sem, sST[0].n)
                sync.wait_ge(sST[1].sem, sST[1].n)

            @blk.vector
            def _(v):
                spt6 = 32 if N_SUB == 1 else 16
                for i in range(KC):
                    v.wait_ge(sXB[i % 2].sem, xb0[i % 2] + (i // 2 + 1) * 32)
                    v.wait_ge(sXL[i % 2].sem, xl0[i % 2] + (i // 2 + 1) * 16)
                    if i >= 2:
                        v.wait_ge(sST[i % 2].sem, st0[i % 2] + ((i - 2) // 2 + 1) * spt6)
                    nn_ = cDVE.inc(v.scalar_tensor_tensor(out_t2[:, i % 2], hi_tm[:, i % 2], 1.0,
                                                          lo_tm[:, i % 2], ALU.mult, ALU.add))
                    v.wait_ge(cDVE.sem, nn_)
                    cDVE.inc(v.tensor_tensor(out_t2[:, i % 2], out_t2[:, i % 2], x_t2[:, i % 2], ALU.add))

        astk.close()

        wres = e(nc.sbuf_tensor("wres", [128, KC, 1024], BF16))
        raf = e(nc.sbuf_tensor("raf", [128, KC, 72], F32))
        hamf = e(nc.sbuf_tensor("hamf", [64, T], BF16))
        bt_res = e(nc.sbuf_tensor("bt_res", [64, D], BF16))
        rat_t = e(nc.sbuf_tensor("rat_t", [128, KC, 72], BF16))
        bb_row = e(nc.sbuf_tensor("bb_row", [1, D], BF16))
        idn_t = e(nc.sbuf_tensor("idn_t", [128, 128], BF16))
        rsc = e(nc.sbuf_tensor("rsc", [128, 64], F32))
        ham_st = e(nc.sbuf_tensor("ham_st", [128, 64], BF16))
        hb_k = e(nc.sbuf_tensor("hb_k", [128, 2, 2, 128], BF16))
        hb_v = e(nc.sbuf_tensor("hb_v", [128, 2, 2, 128], BF16))
        hb_kf = e(nc.sbuf_tensor("hb_kf", [128, 2, 256], BF16))
        mskd2 = e(nc.sbuf_tensor("mskd2", [128, 2, 512], BF16))
        kv_sb = e(nc.sbuf_tensor("kv_sb", [128, 128], F32))
        kv_bf2 = e(nc.sbuf_tensor("kv_bf2", [128, 2, 128], BF16))
        kv_all = e(nc.sbuf_tensor("kv_all", [128, D], F32))
        mask2 = e(nc.sbuf_tensor("mask2", [128, 2, 256], F32))
        cm_t = e(nc.sbuf_tensor("cm_t", [128, 2 * 256], F32))
        dk_t = e(nc.sbuf_tensor("dk_t", [128, 8], F32))
        slope_f = e(nc.sbuf_tensor("slope_f", [128, 16], F32))
        nw_row = e(nc.sbuf_tensor("nw_row", [1, D], BF16))
        iota_t = e(nc.sbuf_tensor("iota_t", [128, 5], F32))
        ao_stage = e(nc.sbuf_tensor("ao_stage", [128, 2, 2, 128], BF16))
        qd_all = e(nc.sbuf_tensor("qd_all", [128, 32], F32))
        kd_all = e(nc.sbuf_tensor("kd_all", [128, 32], F32))
        md_all = e(nc.sbuf_tensor("md_all", [128, 32], F32))
        ind_t = e(nc.sbuf_tensor("ind_t", [128, 512], BF16))
        bdj = e(nc.sbuf_tensor("bdj", [128, 128], F32))
        iota_bj_t = e(nc.sbuf_tensor("iota_bj_t", [128, 8], F32))
        qs_stg = e(nc.sbuf_tensor("qs_stg", [128, 2, D], BF16))

        with nc.Block() as blk:
            @blk.gpsimd
            def _(gp):
                sING.inc(gp.dma_start(out=rat_t[:, :, :], in_=rat_in), 16)
                sING.inc(gp.dma_start(out=bt_res[:, :], in_=bt_in), 16)
                sING.inc(gp.dma_start(out=bb_row[:, :], in_=bb_in), 16)

            @blk.sync
            def _(sync):
                sIN.inc(sync.dma_start(out=idn_t[:, :], in_=idn_in), 16)
        ffn_consts = (sING.n, sIN.n)

        def emit_ln(x_src):
            with nc.Block() as blk:
                v0, a0 = cDVE.n, cACT.n
                xl0 = [sXL[0].n, sXL[1].n]
                st0 = [sST[0].n, sST[1].n]

                @blk.sync
                def _(sync):
                    sync.wait_ge(sST[0].sem, st0[0])
                    sync.wait_ge(sST[1].sem, st0[1])
                    for i in range(KC):
                        if i >= 2:
                            sync.wait_ge(cDVE.sem, v0 + (i - 2) * 7 + 7)
                        else:
                            sync.wait_ge(cDVE.sem, v0)
                        sXL[i % 2].inc(sync.dma_start(out=x_t2[:, i % 2],
                                                      in_=x_src[i * 128:(i + 1) * 128, :]), 16)
                        if i >= 1:
                            j = i - 1
                            sync.wait_ge(cDVE.sem, v0 + j * 7 + 7)
                            sST[j % 2].inc(sync.dma_start(out=u_sp[j * 128:(j + 1) * 128, :],
                                                          in_=ch_re[:, j % 2, 0:D]), 16)
                    j = KC - 1
                    sync.wait_ge(cDVE.sem, v0 + j * 7 + 7)
                    sST[j % 2].inc(sync.dma_start(out=u_sp[j * 128:(j + 1) * 128, :],
                                                  in_=ch_re[:, j % 2, 0:D]), 16)

                @blk.vector
                def _(v):
                    v.wait_ge(sST[0].sem, st0[0])
                    v.wait_ge(sST[1].sem, st0[1])
                    for i in range(KC):
                        pi = (i % 2) * 16
                        xt = x_t2[:, i % 2]
                        ut = ch_re[:, i % 2, 0:D]
                        v.wait_ge(sXL[i % 2].sem, xl0[i % 2] + (i // 2 + 1) * 16)
                        if i >= 2:
                            v.wait_ge(cACT.sem, a0 + (i - 2) * 2 + 2)
                            v.wait_ge(sST[i % 2].sem, st0[i % 2] + ((i - 2) // 2 + 1) * 16)
                        n = {}
                        def op(key, inst):
                            n[key] = cDVE.inc(inst)
                        op(1, v.tensor_scalar(out_t2[:, i % 2], xt, 1.0, 0.0, ALU.mult, ALU.add,
                                              accum_out=stt[:, pi + 0:pi + 1]))
                        op(2, v.scalar_tensor_tensor(out_t2[:, (i + 1) % 2], xt, 1.0, xt,
                                                     ALU.mult, ALU.mult,
                                                     accum_out=stt[:, pi + 1:pi + 2]))
                        v.wait_ge(cDVE.sem, n[1])
                        op(3, v.tensor_scalar_mul(stt[:, pi + 2:pi + 3], stt[:, pi + 0:pi + 1], 1.0 / D))
                        v.wait_ge(cDVE.sem, n[2])
                        op(4, v.tensor_scalar_mul(stt[:, pi + 3:pi + 4], stt[:, pi + 1:pi + 2], 1.0 / D))
                        v.wait_ge(cDVE.sem, n[3])
                        op(5, v.tensor_tensor(stt[:, pi + 4:pi + 5], stt[:, pi + 2:pi + 3],
                                              stt[:, pi + 2:pi + 3], ALU.mult))
                        v.wait_ge(cDVE.sem, n[5])
                        op(6, v.tensor_tensor(stt[:, pi + 5:pi + 6], stt[:, pi + 3:pi + 4],
                                              stt[:, pi + 4:pi + 5], ALU.subtract))
                        v.wait_ge(cACT.sem, a0 + i * 2 + 2)
                        op(7, v.tensor_scalar(ut, xt, stt[:, pi + 2:pi + 3], stt[:, pi + 6:pi + 7],
                                              ALU.subtract, ALU.mult))

                @blk.scalar
                def _(s):
                    for i in range(KC):
                        pi = (i % 2) * 16
                        s.wait_ge(cDVE.sem, v0 + i * 7 + 6)
                        na = cACT.inc(s.activation(stt[:, pi + 7:pi + 8], stt[:, pi + 5:pi + 6],
                                                   AF.Ln, bias=eps_t[:, 0:1]))
                        s.wait_ge(cACT.sem, na)
                        cACT.inc(s.activation(stt[:, pi + 6:pi + 7], stt[:, pi + 7:pi + 8],
                                              AF.Exp, scale=-0.5))
            return (sST[0].n, sST[1].n)

        def emit_xbar(st_tot):
            with nc.Block() as blk:
                @blk.scalar
                def _(s):
                    s.wait_ge(sST[0].sem, st_tot[0])
                    s.wait_ge(sST[1].sem, st_tot[1])
                    for c in range(KC):
                        sXB[c % 2].inc(s.dma_start(out=mvA[:, c, :],
                                                   in_=u_sp[:, c * 128:(c + 1) * 128],
                                                   transpose=True), 16)
            return (sXB[0].n, sXB[1].n)

        def emit_ffn(x_src, x_dst, store_out):
            st_tot = emit_ln(x_src)
            xb_tot = emit_xbar(st_tot)

            with nc.Block() as blk:
                pe0 = cPE.n
                v0 = cDVE.n
                a0 = cACT.n
                prev_bank = list(bank_last)
                mm_n = {}
                tp_n = {}
                dve_evac = {tr: v0 + tr * 19 + 1 for tr in range(KC)}
                dve_ham = {tr: v0 + tr * 19 + 19 for tr in range(KC)}
                dve_m2 = {tr: v0 + tr * 19 + 6 for tr in range(KC)}
                dve_r24 = {tr: v0 + tr * 19 + 10 for tr in range(KC)}
                act_d = {tr: a0 + (2 if tr == 0 else 3 * tr + 1) for tr in range(KC)}
                act_evac = {tr: a0 + (3 * (tr + 1) + 2 if tr < KC - 1 else 3 * KC) for tr in range(KC)}

                @blk.tensor
                def _(t):
                    t.wait_ge(sXB[0].sem, xb_tot[0])
                    t.wait_ge(sXB[1].sem, xb_tot[1])
                    t.wait_ge(sING.sem, ffn_consts[0])
                    t.wait_ge(sIN.sem, ffn_consts[1])

                    def mm(tr):
                        if tr >= NB:
                            t.wait_ge(cDVE.sem, dve_evac[tr - NB])
                        else:
                            bc, bval = prev_bank[tr % NB]
                            t.wait_ge(bc.sem, bval)
                        inst = None
                        for k in range(KC):
                            inst = t.matmul(pbank[tr % NB][:, 0:72],
                                            mvA[:, k, tr * 128:(tr + 1) * 128],
                                            rat_t[:, k, :], start=(k == 0), stop=(k == KC - 1))
                        mm_n[tr] = cPE.inc(inst)

                    def tp(tr):
                        t.wait_ge(cDVE.sem, dve_ham[tr])
                        if tr >= 1:
                            t.wait_ge(cACT.sem, act_evac[tr - 1])
                        tp_n[tr] = cPE.inc(t.transpose(pbank_bf[0:64, 0:128], ham_st[:, :],
                                                       idn_t[:, :]))

                    for tr in range(NB):
                        mm(tr)
                    for tr in range(KC - NB):
                        tp(tr)
                        mm(tr + NB)
                    for tr in range(KC - NB, KC):
                        tp(tr)

                @blk.vector
                def _(v):
                    for tr in range(KC):
                        n = {}
                        def op(key, inst):
                            n[key] = cDVE.inc(inst)
                        v.wait_ge(cPE.sem, pe0 + tr + 1 if tr < NB else mm_n[tr])
                        op(1, v.tensor_copy(raf[:, tr, :], pbank[tr % NB][:, 0:72]))
                        lg = raf[:, tr, 0:8]
                        v.wait_ge(cDVE.sem, n[1])
                        op(2, v.reduce_max(rsc[:, 0:1], lg, mybir.AxisListType.X))
                        v.wait_ge(cDVE.sem, n[2])
                        op(3, v.tensor_scalar_mul(rsc[:, 1:2], rsc[:, 0:1], -1.0))
                        op(4, v.tensor_single_scalar(rsc[:, 8:16], lg, rsc[:, 0:1], ALU.is_ge))
                        v.wait_ge(cDVE.sem, n[4])
                        op(5, v.scalar_tensor_tensor(rsc[:, 16:24], rsc[:, 8:16], -1e30, lg,
                                                     ALU.mult, ALU.add))
                        v.wait_ge(cDVE.sem, n[5])
                        op(6, v.reduce_max(rsc[:, 2:3], rsc[:, 16:24], mybir.AxisListType.X))
                        v.wait_ge(cACT.sem, act_d[tr])
                        op(7, v.tensor_scalar(rsc[:, 4:5], rsc[:, 3:4], 1.0, 0.5, ALU.add, ALU.mult))
                        v.wait_ge(cDVE.sem, n[7])
                        op(8, v.reciprocal(rsc[:, 5:6], rsc[:, 4:5]))
                        op(9, v.tensor_single_scalar(rsc[:, 32:40], lg, rsc[:, 2:3], ALU.is_ge))
                        v.wait_ge(cDVE.sem, n[9])
                        op(10, v.tensor_tensor(rsc[:, 40:48], rsc[:, 24:32], rsc[:, 32:40], ALU.mult))
                        v.wait_ge(cDVE.sem, n[10])
                        op(11, v.tensor_single_scalar(rsc[:, 48:56], rsc[:, 40:48], rsc[:, 5:6],
                                                      ALU.mult))
                        v.wait_ge(cDVE.sem, n[11])
                        if tr >= 1:
                            v.wait_ge(cPE.sem, tp_n[tr - 1])
                        for ei in range(8):
                            cDVE.inc(v.tensor_single_scalar(ham_st[:, ei * 8:(ei + 1) * 8],
                                                            raf[:, tr, 8 + ei * 8:8 + (ei + 1) * 8],
                                                            rsc[:, 48 + ei:49 + ei], ALU.mult))

                @blk.scalar
                def _(s):
                    for tr in range(KC):
                        s.wait_ge(cDVE.sem, dve_m2[tr])
                        if tr >= 1:
                            s.wait_ge(cDVE.sem, dve_r24[tr - 1])
                        cACT.inc(s.activation(rsc[:, 24:32], raf[:, tr, 0:8], AF.Exp,
                                              bias=rsc[:, 1:2]))
                        cACT.inc(s.activation(rsc[:, 3:4], rsc[:, 2:3], AF.Exp, bias=rsc[:, 1:2]))
                        if tr >= 1:
                            s.wait_ge(cPE.sem, tp_n[tr - 1])
                            cACT.inc(s.copy(hamf[0:64, (tr - 1) * 128:tr * 128],
                                            pbank_bf[0:64, 0:128]))
                    s.wait_ge(cPE.sem, tp_n[KC - 1])
                    cACT.inc(s.copy(hamf[0:64, (KC - 1) * 128:KC * 128], pbank_bf[0:64, 0:128]))
                for b_ in range(NB):
                    bank_last[b_] = (cDVE, cDVE.n)
                bf_last[0], bf_last[1] = cACT, cACT.n
            hamf_act = cACT.n

            with nc.Block() as blk:
                pe0 = cPE.n
                v0 = cDVE.n
                ws_0 = sWS[0].n
                xl0 = [sXL[0].n, sXL[1].n]
                st0 = [sST[0].n, sST[1].n]
                prev_bank = list(bank_last)

                @blk.gpsimd
                def _(gp):
                    for oh in range(2):
                        if oh >= 1:
                            gp.wait_ge(cPE.sem, pe0 + 32)
                        elif wres_pe["n"] > 0:
                            gp.wait_ge(cPE.sem, wres_pe["n"])
                        sWS[0].inc(gp.dma_start(out=wres[:, :, :].rearrange("p k o -> p (k o)"),
                                                in_=wbase_in[oh]), 16)

                @blk.tensor
                def _(t):
                    t.wait_ge(cACT.sem, hamf_act)
                    for oh in range(2):
                        t.wait_ge(sWS[0].sem, ws_0 + (oh + 1) * 16)
                        for tr in range(KC):
                            for oc in range(2):
                                g = oh * 32 + tr * 2 + oc
                                if g >= NB:
                                    t.wait_ge(cDVE.sem, v0 + (g - NB) + 1)
                                else:
                                    bc, bval = prev_bank[g % NB]
                                    t.wait_ge(bc.sem, bval)
                                bank = pbank[g % NB]
                                inst = None
                                for k in range(KC):
                                    inst = t.matmul(bank[:, :], mvA[:, k, tr * 128:(tr + 1) * 128],
                                                    wres[:, k, oc * TBW:(oc + 1) * TBW],
                                                    start=(k == 0), stop=False)
                                t.matmul(bank[:, :], ones_t[0:1, :],
                                         bb_row[0:1, oh * 1024 + oc * TBW:oh * 1024 + (oc + 1) * TBW],
                                         start=False, stop=False)
                                inst = t.matmul(bank[:, :], hamf[0:64, tr * 128:(tr + 1) * 128],
                                                bt_res[0:64, oh * 1024 + oc * TBW:oh * 1024 + (oc + 1) * TBW],
                                                start=False, stop=True)
                                cPE.inc(inst)

                @blk.sync
                def _(sync):
                    for oh in range(2):
                        for tr in range(KC):
                            g2 = oh * 16 + tr
                            if g2 >= 2:
                                sync.wait_ge(cDVE.sem, v0 + (g2 - 2) * 2 + 2)
                            else:
                                sync.wait_ge(cDVE.sem, v0)
                            sXL[tr % 2].inc(sync.dma_start(
                                out=x_t2[:, tr % 2, 0:1024],
                                in_=x_src[tr * 128:(tr + 1) * 128, oh * 1024:(oh + 1) * 1024]), 16)
                            if g2 >= 1:
                                j2 = g2 - 1
                                joh, jtr = j2 // 16, j2 % 16
                                sync.wait_ge(cDVE.sem, v0 + j2 * 2 + 2)
                                sST[jtr % 2].inc(sync.dma_start(
                                    out=x_dst[jtr * 128:(jtr + 1) * 128, joh * 1024:(joh + 1) * 1024],
                                    in_=out_t2[:, jtr % 2, 0:1024]), 16)
                                if store_out:
                                    sST[jtr % 2].inc(sync.dma_start(
                                        out=out_ap[jtr * 128:(jtr + 1) * 128, joh * 1024:(joh + 1) * 1024],
                                        in_=out_t2[:, jtr % 2, 0:1024]), 16)
                    j2 = 31
                    joh, jtr = j2 // 16, j2 % 16
                    sync.wait_ge(cDVE.sem, v0 + j2 * 2 + 2)
                    sST[jtr % 2].inc(sync.dma_start(
                        out=x_dst[jtr * 128:(jtr + 1) * 128, joh * 1024:(joh + 1) * 1024],
                        in_=out_t2[:, jtr % 2, 0:1024]), 16)
                    if store_out:
                        sST[jtr % 2].inc(sync.dma_start(
                            out=out_ap[jtr * 128:(jtr + 1) * 128, joh * 1024:(joh + 1) * 1024],
                            in_=out_t2[:, jtr % 2, 0:1024]), 16)
                    if store_out:
                        sync.wait_ge(sST[0].sem, sST[0].n)
                        sync.wait_ge(sST[1].sem, sST[1].n)

                @blk.vector
                def _(v):
                    spt = 32 if store_out else 16
                    for oh in range(2):
                        for tr in range(KC):
                            g2 = oh * 16 + tr
                            v.wait_ge(sXL[tr % 2].sem, xl0[tr % 2] + (g2 // 2 + 1) * 16)
                            if g2 >= 2:
                                v.wait_ge(sST[tr % 2].sem, st0[tr % 2] + ((g2 - 2) // 2 + 1) * spt)
                            nprev = None
                            for oc in range(2):
                                g = oh * 32 + tr * 2 + oc
                                v.wait_ge(cPE.sem, pe0 + g + 1)
                                if nprev is not None:
                                    v.wait_ge(cDVE.sem, nprev)
                                nprev = cDVE.inc(v.scalar_tensor_tensor(
                                    out_t2[:, tr % 2, oc * TBW:(oc + 1) * TBW],
                                    pbank[g % NB][:, :], 1.0,
                                    x_t2[:, tr % 2, oc * TBW:(oc + 1) * TBW],
                                    ALU.mult, ALU.add))
                for b_ in range(NB):
                    bank_last[b_] = (cDVE, cDVE.n)
                wres_pe["n"] = cPE.n

        def emit_optA(wt2, dst_sp, func=None, x_src=None, x_dst=None, store_out=False,
                      extra_ready=()):
            with nc.Block() as blk:
                pe0 = cPE.n
                v0 = cDVE.n
                a0 = cACT.n
                ws_0 = sWS[0].n
                xl0 = [sXL[0].n, sXL[1].n]
                st0 = [sST[0].n, sST[1].n]
                prev_bank = list(bank_last)
                evac_ctr = cACT if func is not None else cDVE

                @blk.gpsimd
                def _(gp):
                    for oh in range(2):
                        if oh >= 1:
                            gp.wait_ge(cPE.sem, pe0 + 32)
                        else:
                            gp.wait_ge(cPE.sem, pe0)
                        sWS[0].inc(gp.dma_start(out=wres[:, :, :].rearrange("p k o -> p (k o)"),
                                                in_=wt2[oh]), 16)

                @blk.tensor
                def _(t):
                    for sem, val in extra_ready:
                        t.wait_ge(sem, val)
                    for oh in range(2):
                        t.wait_ge(sWS[0].sem, ws_0 + (oh + 1) * 16)
                        for tr in range(KC):
                            for oc in range(2):
                                g = oh * 32 + tr * 2 + oc
                                if g >= NB:
                                    t.wait_ge(evac_ctr.sem, v0 + (g - NB) + 1 if func is None
                                              else a0 + (g - NB) + 1)
                                else:
                                    bc, bval = prev_bank[g % NB]
                                    t.wait_ge(bc.sem, bval)
                                bank = pbank[g % NB]
                                inst = None
                                for k in range(KC):
                                    inst = t.matmul(bank[:, :], mvA[:, k, tr * 128:(tr + 1) * 128],
                                                    wres[:, k, oc * TBW:(oc + 1) * TBW],
                                                    start=(k == 0), stop=(k == KC - 1))
                                cPE.inc(inst)

                if func is not None:
                    @blk.scalar
                    def _(s):
                        for oh in range(2):
                            for tr in range(KC):
                                g2 = oh * 16 + tr
                                if g2 >= 2:
                                    s.wait_ge(sST[tr % 2].sem,
                                              st0[tr % 2] + ((g2 - 2) // 2 + 1) * 16)
                                else:
                                    s.wait_ge(sST[tr % 2].sem, st0[tr % 2])
                                for oc in range(2):
                                    g = oh * 32 + tr * 2 + oc
                                    s.wait_ge(cPE.sem, pe0 + g + 1)
                                    cACT.inc(s.activation(
                                        ch_re[:, tr % 2, oc * TBW:(oc + 1) * TBW],
                                        pbank[g % NB][:, :], func))

                    @blk.sync
                    def _(sync):
                        for oh in range(2):
                            for tr in range(KC):
                                g2 = oh * 16 + tr
                                sync.wait_ge(cACT.sem, a0 + g2 * 2 + 2)
                                sST[tr % 2].inc(sync.dma_start(
                                    out=dst_sp[tr * 128:(tr + 1) * 128, oh * 1024:(oh + 1) * 1024],
                                    in_=ch_re[:, tr % 2, 0:1024]), 16)
                    for b_ in range(NB):
                        bank_last[b_] = (cACT, cACT.n)
                else:
                    @blk.sync
                    def _(sync):
                        spt = 32 if store_out else 16
                        nst = 0
                        for oh in range(2):
                            for tr in range(KC):
                                g2 = oh * 16 + tr
                                if g2 >= 2:
                                    sync.wait_ge(cDVE.sem, v0 + (g2 - 2) * 2 + 2)
                                else:
                                    sync.wait_ge(cDVE.sem, v0)
                                sXL[tr % 2].inc(sync.dma_start(
                                    out=x_t2[:, tr % 2, 0:1024],
                                    in_=x_src[tr * 128:(tr + 1) * 128, oh * 1024:(oh + 1) * 1024]), 16)
                                if g2 >= 1:
                                    j2 = g2 - 1
                                    joh, jtr = j2 // 16, j2 % 16
                                    sync.wait_ge(cDVE.sem, v0 + j2 * 2 + 2)
                                    sST[jtr % 2].inc(sync.dma_start(
                                        out=x_dst[jtr * 128:(jtr + 1) * 128,
                                                  joh * 1024:(joh + 1) * 1024],
                                        in_=out_t2[:, jtr % 2, 0:1024]), 16)
                                    if store_out:
                                        sST[jtr % 2].inc(sync.dma_start(
                                            out=out_ap[jtr * 128:(jtr + 1) * 128,
                                                       joh * 1024:(joh + 1) * 1024],
                                            in_=out_t2[:, jtr % 2, 0:1024]), 16)
                        j2 = 31
                        joh, jtr = 1, 15
                        sync.wait_ge(cDVE.sem, v0 + j2 * 2 + 2)
                        sST[jtr % 2].inc(sync.dma_start(
                            out=x_dst[jtr * 128:(jtr + 1) * 128, joh * 1024:(joh + 1) * 1024],
                            in_=out_t2[:, jtr % 2, 0:1024]), 16)
                        if store_out:
                            sST[jtr % 2].inc(sync.dma_start(
                                out=out_ap[jtr * 128:(jtr + 1) * 128, joh * 1024:(joh + 1) * 1024],
                                in_=out_t2[:, jtr % 2, 0:1024]), 16)
                            sync.wait_ge(sST[0].sem, sST[0].n)
                            sync.wait_ge(sST[1].sem, sST[1].n)

                    @blk.vector
                    def _(v):
                        spt = 32 if store_out else 16
                        for oh in range(2):
                            for tr in range(KC):
                                g2 = oh * 16 + tr
                                v.wait_ge(sXL[tr % 2].sem, xl0[tr % 2] + (g2 // 2 + 1) * 16)
                                if g2 >= 2:
                                    v.wait_ge(sST[tr % 2].sem,
                                              st0[tr % 2] + ((g2 - 2) // 2 + 1) * spt)
                                else:
                                    v.wait_ge(sST[tr % 2].sem, st0[tr % 2])
                                nprev = None
                                for oc in range(2):
                                    g = oh * 32 + tr * 2 + oc
                                    v.wait_ge(cPE.sem, pe0 + g + 1)
                                    if nprev is not None:
                                        v.wait_ge(cDVE.sem, nprev)
                                    nprev = cDVE.inc(v.scalar_tensor_tensor(
                                        out_t2[:, tr % 2, oc * TBW:(oc + 1) * TBW],
                                        pbank[g % NB][:, :], 1.0,
                                        x_t2[:, tr % 2, oc * TBW:(oc + 1) * TBW],
                                        ALU.mult, ALU.add))
                    for b_ in range(NB):
                        bank_last[b_] = (cDVE, cDVE.n)

        def emit_attn_b(x_src, x_dst, store_out):
            st_tot = emit_ln(x_src)
            xb_tot = emit_xbar(st_tot)

            with nc.Block() as blk:
                @blk.sync
                def _(sync):
                    sIN.inc(sync.dma_start(out=cm_t[:, :], in_=cm_in), 16)
                    sIN.inc(sync.dma_start(out=iota_t[:, 0:2], in_=iota1_in), 16)
                    sIN.inc(sync.dma_start(out=iota_t[:, 2:4], in_=iota2_in), 16)
                    sIN.inc(sync.dma_start(out=iota_t[:, 4:5], in_=c256_in), 16)
                    sIN.inc(sync.dma_start(out=slope_f[:, :], in_=slope_in), 16)
                    sIN.inc(sync.dma_start(out=iota_bj_t[:, :], in_=iotabj_in), 16)
                    sIN.inc(sync.dma_start(out=ind_t[:, :], in_=ind_in), 16)

                @blk.gpsimd
                def _(gp):
                    sING.inc(gp.dma_start(out=nw_row[:, :], in_=nw_in), 16)
                binit_in = (sIN.n, sING.n)

                @blk.tensor
                def _(t):
                    t.wait_ge(sING.sem, binit_in[1])
                    t.wait_ge(cACT.sem, cACT.n)
                    inst = None
                    for j in range(NTB):
                        inst = t.matmul(pbank[j % NB][:, :], ones_t[0:1, :],
                                        nw_row[0:1, j * TBW:(j + 1) * TBW], start=True, stop=True)
                    cPE.inc(inst)
                nwpe = cPE.n

                @blk.scalar
                def _(s):
                    s.wait_ge(cPE.sem, nwpe)
                    inst = None
                    for j in range(NTB):
                        inst = s.copy(g_bc[:, j * TBW:(j + 1) * TBW], pbank[j % NB][:, :])
                    cACT.inc(inst)

                @blk.vector
                def _(v):
                    cDVE.inc(v.memset(dk_t[:, 7:8], 1e-6))
            binit_act = cACT.n
            binit_dve = cDVE.n
            for b_ in range(NTB):
                bank_last[b_] = (cACT, binit_act)

            NH, NJ = 16, 8

            with nc.Block() as blk:
                v0p, a0p = cDVE.n, cACT.n

                @blk.vector
                def _(v):
                    v.wait_ge(cDVE.sem, binit_dve)
                    v.wait_ge(sIN.sem, binit_in[0])
                    for h in range(NH):
                        if h >= 1:
                            v.wait_ge(cACT.sem, a0p + 4 * h)
                        cDVE.inc(v.tensor_scalar_mul(dk_t[:, 6:7], slope_f[:, h:h + 1], -1.0))

                @blk.scalar
                def _(s):
                    for h in range(NH):
                        s.wait_ge(cDVE.sem, v0p + h + 1)
                        cACT.inc(s.activation(qd_all[:, h * 2:h * 2 + 2], iota_t[:, 0:2],
                                              AF.Exp, scale=dk_t[:, 6:7]))
                        cACT.inc(s.activation(kd_all[:, h * 2:h * 2 + 2], iota_t[:, 2:4],
                                              AF.Exp, scale=dk_t[:, 6:7]))
                        cACT.inc(s.activation(bdj[:, h * 8:(h + 1) * 8], iota_bj_t[:, :],
                                              AF.Exp, scale=dk_t[:, 6:7]))
                        cACT.inc(s.activation(md_all[:, h * 2:h * 2 + 2], iota_t[:, 0:2],
                                              AF.Exp, scale=slope_f[:, h:h + 1]))
            p0_act = cACT.n

            xbrdy = [(sXB[0].sem, xb_tot[0]), (sXB[1].sem, xb_tot[1]),
                     (sIN.sem, binit_in[0])]
            emit_optA(qkvw_in[0], bq_sp, func=AF.Silu, extra_ready=xbrdy)
            bq_st = (sST[0].n, sST[1].n)
            emit_optA(qkvw_in[1], bk_sp, func=AF.Silu)
            bk_st = (sST[0].n, sST[1].n)
            emit_optA(qkvw_in[2], bv_sp, func=AF.Silu)
            emit_optA(wgate_in, gate_sp, func=AF.Sigmoid)
            b1_st = (sST[0].n, sST[1].n)
            b1_pe = cPE.n

            with nc.Block() as blk:
                v0 = cDVE.n
                rg0 = [sRG[0].n, sRG[1].n]
                sp0 = [sSP[0].n, sSP[1].n]
                NT1 = 3 * KC

                @blk.sync
                def _(sync):
                    for g in range(NT1):
                        i = g % KC
                        src = bq_sp if g < KC else bk_sp
                        gate = bq_st if g < KC else bk_st
                        sync.wait_ge(sST[0].sem, gate[0])
                        sync.wait_ge(sST[1].sem, gate[1])
                        if g >= 2:
                            sync.wait_ge(sSP[g % 2].sem, sp0[g % 2] + ((g - 2) // 2 + 1) * 16)
                        sRG[g % 2].inc(sync.dma_start(out=qs_stg[:, g % 2],
                                                      in_=src[i * 128:(i + 1) * 128, :]), 16)

                @blk.vector
                def _(v):
                    v.wait_ge(cACT.sem, p0_act)
                    for g in range(NT1):
                        i = g % KC
                        tab = qd_all if g < KC else (kd_all if g < 2 * KC else md_all)
                        v.wait_ge(sRG[g % 2].sem, rg0[g % 2] + (g // 2 + 1) * 16)
                        if g >= 2:
                            v.wait_ge(sSP[g % 2].sem, sp0[g % 2] + ((g - 2) // 2 + 1) * 16)
                        for h in range(NH):
                            cDVE.inc(v.tensor_single_scalar(
                                qs_stg[:, g % 2, h * 128:(h + 1) * 128],
                                qs_stg[:, g % 2, h * 128:(h + 1) * 128],
                                tab[:, h * 2 + (i % 2):h * 2 + (i % 2) + 1], ALU.mult))

                @blk.gpsimd
                def _(gp):
                    for g in range(NT1):
                        i = g % KC
                        dst = u_sp if g < KC else (k_sp if g < 2 * KC else q_sp)
                        gp.wait_ge(cDVE.sem, v0 + (g + 1) * NH)
                        sSP[g % 2].inc(gp.dma_start(out=dst[i * 128:(i + 1) * 128, :],
                                                    in_=qs_stg[:, g % 2]), 16)
            p1_sp = (sSP[0].n, sSP[1].n)

            with nc.Block() as blk:
                @blk.scalar
                def _(s):
                    s.wait_ge(sSP[0].sem, p1_sp[0])
                    s.wait_ge(sSP[1].sem, p1_sp[1])
                    s.wait_ge(cPE.sem, b1_pe)
                    for h in range(NH):
                        sXB[h % 2].inc(s.dma_start(out=mvA[:, h, :],
                                                   in_=u_sp[:, h * 128:(h + 1) * 128],
                                                   transpose=True), 16)
            qs_xb = (sXB[0].n, sXB[1].n)

            NS = NH * NJ
            with nc.Block() as blk:
                pe0u, a0u = cPE.n, cACT.n
                ch0 = [sCH[0].n, sCH[1].n]
                xl0 = [sXL[0].n, sXL[1].n]
                prev_bank = list(bank_last)
                pe_u = {s_: pe0u + s_ + 1 for s_ in range(NS)}
                act_uev = {s_: a0u + s_ + 1 for s_ in range(NS)}

                @blk.sync
                def _(sync):
                    sync.wait_ge(sST[0].sem, b1_st[0])
                    sync.wait_ge(sST[1].sem, b1_st[1])
                    for s_ in range(NS):
                        h, j = divmod(s_, NJ)
                        p = s_ % 2
                        if s_ >= 2:
                            sync.wait_ge(cPE.sem, pe_u[s_ - 2])
                        sXL[p].inc(sync.dma_start(
                            out=hb_v[:, p, :, :].rearrange("a b c -> a (b c)"),
                            in_=bv_sp[j * 256:(j + 1) * 256, h * 128:(h + 1) * 128]
                            .rearrange("(s p) d -> p s d", s=2)), 16)

                @blk.tensor
                def _(t):
                    for s_ in range(NS):
                        p = s_ % 2
                        t.wait_ge(sCH[p].sem, ch0[p] + (s_ // 2 + 1) * 16)
                        t.wait_ge(sXL[p].sem, xl0[p] + (s_ // 2 + 1) * 16)
                        if s_ >= NB:
                            t.wait_ge(cACT.sem, act_uev[s_ - NB])
                        else:
                            bc, bval = prev_bank[s_ % NB]
                            t.wait_ge(bc.sem, bval)
                        inst = None
                        for sub in range(2):
                            inst = t.matmul(pbank[s_ % NB][:, 0:128], hb_k[:, p, sub, :],
                                            hb_v[:, p, sub, :], start=(sub == 0), stop=(sub == 1))
                        cPE.inc(inst)

                @blk.scalar
                def _(s):
                    s.wait_ge(sSP[0].sem, p1_sp[0])
                    s.wait_ge(sSP[1].sem, p1_sp[1])
                    for s_ in range(NS):
                        h, j = divmod(s_, NJ)
                        p = s_ % 2
                        if s_ >= 2:
                            s.wait_ge(cPE.sem, pe_u[s_ - 2])
                        sCH[p].inc(s.dma_start(
                            out=hb_k[:, p, :, :].rearrange("a b c -> a (b c)"),
                            in_=k_sp[j * 256:(j + 1) * 256, h * 128:(h + 1) * 128]
                            .rearrange("(s p) d -> p s d", s=2)), 16)
                        if s_ >= 1:
                            s.wait_ge(cPE.sem, pe_u[s_ - 1])
                            cACT.inc(s.copy(
                                wres[:, (s_ - 1) // NJ,
                                     ((s_ - 1) % NJ) * 128:((s_ - 1) % NJ + 1) * 128],
                                pbank[(s_ - 1) % NB][:, 0:128]))
                    s.wait_ge(cPE.sem, pe_u[NS - 1])
                    cACT.inc(s.copy(wres[:, (NS - 1) // NJ,
                                         ((NS - 1) % NJ) * 128:((NS - 1) % NJ + 1) * 128],
                                    pbank[(NS - 1) % NB][:, 0:128]))
                for b_ in range(NB):
                    bank_last[b_] = (cACT, cACT.n)
            p3_pe = cPE.n
            p3_act = cACT.n

            with nc.Block() as blk:
                @blk.vector
                def _(v):
                    for h in range(NH):
                        v.wait_ge(cACT.sem, p3_act - (NH - 1 - h) * NJ)
                        nprev = cDVE.inc(v.tensor_copy(kv_sb[:, :], wres[:, h, 0:128]))
                        for j in range(1, NJ):
                            v.wait_ge(cDVE.sem, nprev)
                            nc_ = cDVE.inc(v.tensor_copy(wres[:, h, (j - 1) * 128:j * 128],
                                                         kv_sb[:, :]))
                            v.wait_ge(cDVE.sem, nc_)
                            n1 = cDVE.inc(v.tensor_single_scalar(kv_sb[:, :], kv_sb[:, :],
                                                                 bdj[:, h * 8 + 1:h * 8 + 2],
                                                                 ALU.mult))
                            v.wait_ge(cDVE.sem, n1)
                            nprev = cDVE.inc(v.tensor_tensor(kv_sb[:, :], kv_sb[:, :],
                                                             wres[:, h, j * 128:(j + 1) * 128],
                                                             ALU.add))
                        v.wait_ge(cDVE.sem, nprev)
                        cDVE.inc(v.tensor_copy(kv_all[:, h * 128:(h + 1) * 128], kv_sb[:, :]))
            p4_dve = cDVE.n

            with nc.Block() as blk:
                @blk.sync
                def _(sync):
                    sync.wait_ge(cDVE.sem, p4_dve)
                    sync.wait_ge(cCC.sem, 4)
                    cCI.inc(sync.dma_start(out=cc2_in[:, :], in_=kv_all[:, :]), 16)
                ci_val = cCI.n

                @blk.gpsimd
                def _(gp):
                    gp.wait_ge(cCI.sem, ci_val)
                    gp.wait_ge(cGA.sem, 4 * 16)
                    cCC.inc(gp.collective_compute(
                        "AllGather", ALU.bypass,
                        replica_groups=[[0, 1], [2, 3], [4, 5], [6, 7]],
                        ins=[cc2_in.ap().opt()], outs=[cc2_out.ap().opt()]))
                    gp.wait_ge(cCC.sem, cCC.n)
                    cGA.inc(gp.dma_start(out=kv_all[:, :], in_=cc2_out[0:128, :]), 16)
                ga_val = cGA.n

                @blk.vector
                def _(v):
                    v.wait_ge(cGA.sem, ga_val)
                    cDVE.inc(v.tensor_single_scalar(kv_all[:, :], kv_all[:, :],
                                                    flag_t[:, 0:1], ALU.mult))
            e3_dve = cDVE.n

            with nc.Block() as blk:
                v0, a0, pe0 = cDVE.n, cACT.n, cPE.n
                xl0 = [sXL[0].n, sXL[1].n]
                xb0 = [sXB[0].n, sXB[1].n]
                st0 = [sST[0].n, sST[1].n]
                prev_bank = list(bank_last)
                dve_fix, dve_msk = {}, {}
                c = v0
                for s_ in range(NS):
                    c += (1 if s_ < NH else 2); dve_fix[s_] = c
                    c += 1; dve_msk[s_] = c
                b2b_dve_end = c
                act_ev = {s_: a0 + s_ + 1 for s_ in range(NS)}
                b2b_act_end = a0 + NS
                pe_qk, pe_out = {}, {}
                c = pe0
                for s_ in range(NS):
                    c += 2; pe_qk[s_] = c
                    c += 1; pe_out[s_] = c
                b2b_pe_end = c

                @blk.sync
                def _(sync):
                    for j in range(NJ):
                        if j >= 1:
                            sync.wait_ge(cPE.sem, pe_out[j * NH - 1])
                        else:
                            sync.wait_ge(sSP[0].sem, p1_sp[0])
                            sync.wait_ge(sSP[1].sem, p1_sp[1])
                        sXL[j % 2].inc(sync.dma_start(
                            out=qs_stg[:, :, :].rearrange("p s d -> p (s d)"),
                            in_=bv_sp[j * 256:(j + 1) * 256, :]
                            .rearrange("(s p) d -> p s d", s=2)), 16)
                        for h in range(NH):
                            s_ = j * NH + h
                            if s_ >= 1:
                                sm1 = s_ - 1
                                hh, jj = sm1 % NH, sm1 // NH
                                pp = sm1 % 2
                                sync.wait_ge(cACT.sem, act_ev[sm1])
                                sST[pp].inc(sync.dma_start(
                                    out=v_sp[hh * 128:(hh + 1) * 128,
                                             jj * 256:(jj + 1) * 256],
                                    in_=ao_stage[:, pp, :, :]
                                    .rearrange("a b c -> a (b c)")), 16)
                    s_ = NS - 1
                    hh, jj = s_ % NH, s_ // NH
                    pp = s_ % 2
                    sync.wait_ge(cACT.sem, act_ev[s_])
                    sST[pp].inc(sync.dma_start(
                        out=v_sp[hh * 128:(hh + 1) * 128, jj * 256:(jj + 1) * 256],
                        in_=ao_stage[:, pp, :, :].rearrange("a b c -> a (b c)")), 16)

                @blk.scalar
                def _(s):
                    for s_ in range(NS):
                        h, j = s_ % NH, s_ // NH
                        p = s_ % 2
                        if s_ >= 2:
                            s.wait_ge(cPE.sem, pe_qk[s_ - 2])
                        sXB[p].inc(s.dma_start(
                            out=hb_kf[:, p, :],
                            in_=q_sp[j * 256:(j + 1) * 256, h * 128:(h + 1) * 128],
                            transpose=True), 16)
                        if s_ >= 1:
                            sm1 = s_ - 1
                            s.wait_ge(cPE.sem, pe_out[sm1])
                            if sm1 >= 2:
                                s.wait_ge(sST[sm1 % 2].sem,
                                          st0[sm1 % 2] + ((sm1 - 2) // 2 + 1) * 16)
                            cACT.inc(s.copy(ao_stage[:, sm1 % 2, :, :]
                                            .rearrange("a b c -> a (b c)"),
                                            pbank[4 + sm1 % 3][:, 0:256]))
                    s_ = NS - 1
                    s.wait_ge(cPE.sem, pe_out[s_])
                    s.wait_ge(sST[s_ % 2].sem, st0[s_ % 2] + ((s_ - 2) // 2 + 1) * 16)
                    cACT.inc(s.copy(ao_stage[:, s_ % 2, :, :].rearrange("a b c -> a (b c)"),
                                    pbank[4 + s_ % 3][:, 0:256]))

                @blk.vector
                def _(v):
                    v.wait_ge(cDVE.sem, e3_dve)
                    for s_ in range(NS):
                        h, j = s_ % NH, s_ // NH
                        p = s_ % 2
                        if s_ >= 1:
                            v.wait_ge(cDVE.sem, dve_fix[s_ - 1])
                        if s_ >= 2:
                            v.wait_ge(cPE.sem, pe_out[s_ - 2])
                        if j == 0:
                            cDVE.inc(v.tensor_single_scalar(kv_bf2[:, p, :],
                                                            kv_all[:, h * 128:(h + 1) * 128],
                                                            bdj[:, h * 8:h * 8 + 1], ALU.mult))
                        else:
                            n1 = cDVE.inc(v.tensor_single_scalar(
                                kv_sb[:, :], kv_all[:, h * 128:(h + 1) * 128],
                                bdj[:, h * 8 + j:h * 8 + j + 1], ALU.mult))
                            v.wait_ge(cDVE.sem, n1)
                            cDVE.inc(v.tensor_tensor(kv_bf2[:, p, :], kv_sb[:, :],
                                                     wres[:, h, (j - 1) * 128:j * 128],
                                                     ALU.add))
                        v.wait_ge(cPE.sem, pe_qk[s_])
                        cDVE.inc(v.tensor_tensor(mskd2[:, p, :], pbank[s_ % 4][:, 0:512],
                                                 ind_t[:, :], ALU.mult))

                @blk.tensor
                def _(t):
                    t.wait_ge(sXB[0].sem, qs_xb[0])
                    t.wait_ge(sXB[1].sem, qs_xb[1])
                    for s_ in range(NS):
                        h, j = s_ % NH, s_ // NH
                        p = s_ % 2
                        t.wait_ge(sXB[p].sem, xb0[p] + (s_ // 2 + 1) * 16)
                        if s_ >= 4:
                            t.wait_ge(cDVE.sem, dve_msk[s_ - 4])
                        else:
                            bc, bval = prev_bank[s_ % 4]
                            t.wait_ge(bc.sem, bval)
                        for sub2 in range(2):
                            cPE.inc(t.matmul(pbank[s_ % 4][:, sub2 * 256:(sub2 + 1) * 256],
                                             hb_kf[:, p, sub2 * 128:(sub2 + 1) * 128],
                                             mvA[:, h, j * 256:(j + 1) * 256],
                                             start=True, stop=True))
                        t.wait_ge(cDVE.sem, dve_msk[s_])
                        t.wait_ge(sXL[j % 2].sem, xl0[j % 2] + (j // 2 + 1) * 16)
                        if s_ >= 3:
                            t.wait_ge(cACT.sem, act_ev[s_ - 3])
                        else:
                            bc, bval = prev_bank[4 + s_ % 3]
                            t.wait_ge(bc.sem, bval)
                        ob = pbank[4 + s_ % 3]
                        t.matmul(ob[:, 0:256], kv_bf2[:, p, :],
                                 mvA[:, h, j * 256:(j + 1) * 256], start=True, stop=False)
                        t.matmul(ob[:, 0:256], qs_stg[:, 0, h * 128:(h + 1) * 128],
                                 mskd2[:, p, 0:256], start=False, stop=False)
                        inst = t.matmul(ob[:, 0:256], qs_stg[:, 1, h * 128:(h + 1) * 128],
                                        mskd2[:, p, 256:512], start=False, stop=True)
                        cPE.inc(inst)
                for b_ in range(4):
                    bank_last[b_] = (cDVE, b2b_dve_end)
                for b_ in range(4, NB):
                    bank_last[b_] = (cACT, b2b_act_end)
            assert cDVE.n == b2b_dve_end, (cDVE.n, b2b_dve_end)
            assert cACT.n == b2b_act_end, (cACT.n, b2b_act_end)
            assert cPE.n == b2b_pe_end, (cPE.n, b2b_pe_end)
            b2b_st = (sST[0].n, sST[1].n)

            with nc.Block() as blk:
                v0, a0 = cDVE.n, cACT.n
                xl0 = [sXL[0].n, sXL[1].n]
                rg0 = [sRG[0].n, sRG[1].n]
                st0 = [sST[0].n, sST[1].n]

                @blk.sync
                def _(sync):
                    sync.wait_ge(sST[0].sem, b2b_st[0])
                    sync.wait_ge(sST[1].sem, b2b_st[1])
                    for i in range(KC):
                        if i >= 2:
                            sync.wait_ge(cDVE.sem, v0 + (i - 2) * 4 + 4)
                        else:
                            sync.wait_ge(cDVE.sem, v0)
                        sXL[i % 2].inc(sync.dma_start(out=ch_re[:, i % 2, 0:D],
                                                      in_=v_sp[:, i * 128:(i + 1) * 128],
                                                      transpose=True), 16)
                        sRG[i % 2].inc(sync.dma_start(out=mvA[:, i % 2, 0:D],
                                                      in_=gate_sp[i * 128:(i + 1) * 128, :]), 16)
                        if i >= 1:
                            j = i - 1
                            sync.wait_ge(cDVE.sem, v0 + j * 4 + 4)
                            sST[j % 2].inc(sync.dma_start(out=gn_sp[j * 128:(j + 1) * 128, :],
                                                          in_=mvA[:, 2 + j % 2, 0:D]), 16)
                    j = KC - 1
                    sync.wait_ge(cDVE.sem, v0 + j * 4 + 4)
                    sST[j % 2].inc(sync.dma_start(out=gn_sp[j * 128:(j + 1) * 128, :],
                                                  in_=mvA[:, 2 + j % 2, 0:D]), 16)

                @blk.vector
                def _(v):
                    for i in range(KC):
                        pi = (i % 2) * 16
                        at = ch_re[:, i % 2, 0:D]
                        v.wait_ge(sXL[i % 2].sem, xl0[i % 2] + (i // 2 + 1) * 16)
                        v.wait_ge(sRG[i % 2].sem, rg0[i % 2] + (i // 2 + 1) * 16)
                        if i >= 2:
                            v.wait_ge(cACT.sem, a0 + (i - 2) * 2 + 2)
                            v.wait_ge(sST[i % 2].sem, st0[i % 2] + ((i - 2) // 2 + 1) * 16)
                        n1 = cDVE.inc(v.scalar_tensor_tensor(out_t2[:, i % 2, 0:D], at, 1.0, at,
                                                             ALU.mult, ALU.mult,
                                                             accum_out=stt[:, pi + 0:pi + 1]))
                        v.wait_ge(cDVE.sem, n1)
                        cDVE.inc(v.tensor_scalar_mul(stt[:, pi + 1:pi + 2], stt[:, pi + 0:pi + 1],
                                                     1.0 / D))
                        v.wait_ge(cACT.sem, a0 + i * 2 + 2)
                        n3 = cDVE.inc(v.scalar_tensor_tensor(out_t2[:, i % 2, 0:D], at,
                                                             stt[:, pi + 3:pi + 4], g_bc[:, :],
                                                             ALU.mult, ALU.mult))
                        v.wait_ge(cDVE.sem, n3)
                        cDVE.inc(v.tensor_tensor(mvA[:, 2 + i % 2, 0:D], out_t2[:, i % 2, 0:D],
                                                 mvA[:, i % 2, 0:D], ALU.mult))

                @blk.scalar
                def _(s):
                    for i in range(KC):
                        pi = (i % 2) * 16
                        s.wait_ge(cDVE.sem, v0 + i * 4 + 2)
                        na = cACT.inc(s.activation(stt[:, pi + 2:pi + 3], stt[:, pi + 1:pi + 2],
                                                   AF.Ln, bias=dk_t[:, 7:8]))
                        s.wait_ge(cACT.sem, na)
                        cACT.inc(s.activation(stt[:, pi + 3:pi + 4], stt[:, pi + 2:pi + 3],
                                              AF.Exp, scale=-0.5))
            b3_st = (sST[0].n, sST[1].n)

            with nc.Block() as blk:
                @blk.scalar
                def _(s):
                    s.wait_ge(sST[0].sem, b3_st[0])
                    s.wait_ge(sST[1].sem, b3_st[1])
                    for c2 in range(KC):
                        sXB[c2 % 2].inc(s.dma_start(out=mvA[:, c2, :],
                                                    in_=gn_sp[:, c2 * 128:(c2 + 1) * 128],
                                                    transpose=True), 16)
            gn_xb = [(sXB[0].sem, sXB[0].n), (sXB[1].sem, sXB[1].n)]

            emit_optA(wout_in, None, func=None, x_src=x_src, x_dst=x_dst,
                      store_out=store_out, extra_ready=gn_xb)

        emit_ffn(x1_sp, x2_sp, store_out=(N_SUB == 2))
        if N_SUB >= 3:
            emit_attn_b(x2_sp, x3_sp, store_out=(N_SUB == 3))
        if N_SUB >= 4:
            emit_ffn(x3_sp, x2_sp, store_out=True)

    return nc


def _chunked_vec(v):
    return np.ascontiguousarray(v.reshape(KC, 128).T).astype(np.float32)


def _tile_wT(w):
    wt = np.ascontiguousarray(w.T)
    x = wt.reshape(KC, 128, KC, 128)
    return np.ascontiguousarray(np.transpose(x, (2, 1, 0, 3))).reshape(KC, 128, KC * 128).astype(np.float32)


def prepare_in_maps(inputs):
    x_full = np.asarray(inputs["L0_attn_a_hidden_states"], dtype=np.float32)

    wq = _tile_wT(np.asarray(inputs["L0_attn_a_Wq"], dtype=np.float32))
    wk = _tile_wT(np.asarray(inputs["L0_attn_a_Wk"], dtype=np.float32))
    wv = _tile_wT(np.asarray(inputs["L0_attn_a_Wv"], dtype=np.float32))
    wqa = _tile_wT(np.asarray(inputs["L0_attn_a_Wqa"], dtype=np.float32))
    wka = _tile_wT(np.asarray(inputs["L0_attn_a_Wka"], dtype=np.float32))
    bq = _chunked_vec(np.asarray(inputs["L0_attn_a_bq"], dtype=np.float32))
    bk = _chunked_vec(np.asarray(inputs["L0_attn_a_bk"], dtype=np.float32))
    bv = _chunked_vec(np.asarray(inputs["L0_attn_a_bv"], dtype=np.float32))
    bqa = _chunked_vec(np.asarray(inputs["L0_attn_a_bqa"], dtype=np.float32))
    bka = _chunked_vec(np.asarray(inputs["L0_attn_a_bka"], dtype=np.float32))
    g_row = np.asarray(inputs["L0_attn_a_ln_g"], dtype=np.float32).reshape(1, D)
    b_row = np.asarray(inputs["L0_attn_a_ln_b"], dtype=np.float32).reshape(1, D)
    ones_l = np.ones((1, 128), dtype=ml_dtypes.bfloat16)
    idn = np.eye(128, dtype=np.float32).astype(ml_dtypes.bfloat16)

    wbase_T = np.ascontiguousarray(np.asarray(inputs["L0_ffn_W_base"], dtype=np.float32).T)
    wbase_t = np.ascontiguousarray(
        np.transpose(wbase_T.reshape(KC, 128, 2, 1024), (2, 1, 0, 3))).reshape(2, 128, KC * 1024)
    rt = np.transpose(np.ascontiguousarray(
        np.asarray(inputs["L0_ffn_W_router"], dtype=np.float32).T).reshape(KC, 128, 8), (1, 0, 2))
    a_flat = np.asarray(inputs["L0_ffn_A_w"], dtype=np.float32).reshape(64, D)
    at = np.transpose(np.ascontiguousarray(a_flat.T).reshape(KC, 128, 64), (1, 0, 2))
    rat = np.ascontiguousarray(np.concatenate([rt, at], axis=2)).astype(np.float32)
    bt = np.ascontiguousarray(
        np.transpose(np.asarray(inputs["L0_ffn_B_w"], dtype=np.float32), (0, 2, 1)).reshape(64, D))
    bb = np.asarray(inputs["L0_ffn_b_base"], dtype=np.float32).reshape(1, D)

    def _tile_optA(w):
        wt = np.ascontiguousarray(np.asarray(w, dtype=np.float32).T)
        return np.ascontiguousarray(
            np.transpose(wt.reshape(KC, 128, 2, 1024), (2, 1, 0, 3))).reshape(2, 128, KC * 1024)

    wqkv = np.asarray(inputs["L0_attn_b_w_qkv"], dtype=np.float32)
    qkvw_t = np.stack([_tile_optA(wqkv[s * D:(s + 1) * D, :]) for s in range(3)])
    wgate_t = _tile_optA(inputs["L0_attn_b_w_gate"])
    wout_t = _tile_optA(inputs["L0_attn_b_w_out"])
    slope_col = np.ascontiguousarray(np.repeat(
        np.asarray(inputs["L0_attn_b_slope_rate"], dtype=np.float32).reshape(1, 16), 128, axis=0))
    tloc = (np.arange(2)[None, :] * 128 + np.arange(128)[:, None]).astype(np.float32)
    iota1 = np.ascontiguousarray(tloc + 1.0)
    iota2 = np.ascontiguousarray(255.0 - tloc)
    c256 = np.full((128, 1), 256.0, dtype=np.float32)
    iota_bj = np.ascontiguousarray(
        np.repeat((np.arange(8)[None, :] * 256.0).astype(np.float32), 128, axis=0))
    t2i = (np.arange(2)[:, None, None] * 128 + np.arange(128)[None, :, None])
    t1i = np.arange(256)[None, None, :]
    ind_c = np.ascontiguousarray(
        np.transpose((t1i >= t2i).astype(np.float32), (1, 0, 2)).reshape(128, 512)
    ).astype(ml_dtypes.bfloat16)
    t2g = (np.arange(2)[:, None, None] * 128 + np.arange(128)[None, :, None]).astype(np.float32)
    t1g = np.arange(256)[None, None, :].astype(np.float32)
    cmm = np.where(t1g >= t2g, t2g + 1.0, -1e30).astype(np.float32)
    cm = np.ascontiguousarray(np.transpose(cmm, (1, 0, 2)).reshape(128, 512))
    nw = np.asarray(inputs["L0_attn_b_norm_w"], dtype=np.float32).reshape(1, D)

    in_maps = []
    for c in range(NCORES):
        b, half = c // 2, c % 2
        flag = np.full((128, 1), float(half), dtype=np.float32)
        x_sl = np.ascontiguousarray(x_full[b, half * T:(half + 1) * T, :])
        in_maps.append(dict(
            x_in=x_sl, flag=flag, ones_lhsT=ones_l,
            wq_t=wq, wk_t=wk, wv_t=wv, wqa_t=wqa, wka_t=wka,
            bq_c=bq, bk_c=bk, bv_c=bv, bqa_c=bqa, bka_c=bka,
            g_row=g_row, b_row=b_row,
            wbase_t=wbase_t, rat=rat, bt=bt, bb_in=bb, idn=idn,
            qkvw_t=qkvw_t, wgate_t=wgate_t, wout_t=wout_t, slope_col=slope_col,
            iota1=iota1, iota2=iota2, c256=c256, cm=cm, iota_bj=iota_bj, ind_c=ind_c,
            nw_in_row=nw,
        ))
    return in_maps


def kernel(**inputs):
    in_maps = prepare_in_maps(inputs)
    nc = build_program()
    res = run_bass_kernel_spmd(nc, in_maps, core_ids=list(range(NCORES)))
    global LAST_RESULTS
    LAST_RESULTS = res.results
    out = np.empty((4, 4096, D), dtype=np.float32)
    for c in range(NCORES):
        b, half = c // 2, c % 2
        out[b, half * T:(half + 1) * T, :] = np.asarray(res.results[c]["out"], dtype=np.float32)
    return out


# version 07
# speedup vs baseline: 1.0408x; 1.0408x over previous
import numpy as np
import ml_dtypes
from contextlib import ExitStack

import concourse.bass as bass
import concourse.mybir as mybir
from concourse.bass_utils import run_bass_kernel_spmd

F32 = mybir.dt.float32
BF16 = mybir.dt.bfloat16
AF = mybir.ActivationFunctionType
ALU = mybir.AluOpType

NCORES = 8
T = 2048
D = 2048
KC = 16
TBW = 512
NTB = T // TBW
NG = KC * NTB
LN_EPS = 1e-5
ATT_SCALE = 1.0 / float(np.sqrt(np.float32(2048.0)))

DEBUG_TAPS = False
LAST_RESULTS = None
RACE_DETECT = False
N_SUB = 4


class Ctr:
    def __init__(self, sem):
        self.sem = sem
        self.n = 0

    def inc(self, inst, k=1):
        inst.then_inc(self.sem, k)
        self.n += k
        return self.n


def build_program():
    nc = bass.Bass("TRN2", target_bir_lowering=False, detect_race_conditions=RACE_DETECT)
    tap = "ExternalOutput" if DEBUG_TAPS else "Internal"

    x_in = nc.dram_tensor("x_in", [T, D], F32, kind="ExternalInput").ap()
    flag_in = nc.dram_tensor("flag", [128, 1], F32, kind="ExternalInput").ap()
    ones_in = nc.dram_tensor("ones_lhsT", [1, 128], BF16, kind="ExternalInput").ap()
    wq_in = nc.dram_tensor("wq_t", [KC, 128, D], F32, kind="ExternalInput").ap()
    wk_in = nc.dram_tensor("wk_t", [KC, 128, D], F32, kind="ExternalInput").ap()
    wv_in = nc.dram_tensor("wv_t", [KC, 128, D], F32, kind="ExternalInput").ap()
    wqa_in = nc.dram_tensor("wqa_t", [KC, 128, D], F32, kind="ExternalInput").ap()
    wka_in = nc.dram_tensor("wka_t", [KC, 128, D], F32, kind="ExternalInput").ap()
    bq_in = nc.dram_tensor("bq_c", [128, KC], F32, kind="ExternalInput").ap()
    bk_in = nc.dram_tensor("bk_c", [128, KC], F32, kind="ExternalInput").ap()
    bv_in = nc.dram_tensor("bv_c", [128, KC], F32, kind="ExternalInput").ap()
    bqa_in = nc.dram_tensor("bqa_c", [128, KC], F32, kind="ExternalInput").ap()
    bka_in = nc.dram_tensor("bka_c", [128, KC], F32, kind="ExternalInput").ap()
    g_in = nc.dram_tensor("g_row", [1, D], F32, kind="ExternalInput").ap()
    bln_in = nc.dram_tensor("b_row", [1, D], F32, kind="ExternalInput").ap()
    wbase_in = nc.dram_tensor("wbase_t", [2, 128, KC * 1024], F32, kind="ExternalInput").ap()
    rat_in = nc.dram_tensor("rat", [128, KC, 72], F32, kind="ExternalInput").ap()
    bt_in = nc.dram_tensor("bt", [64, D], F32, kind="ExternalInput").ap()
    bb_in = nc.dram_tensor("bb_in", [1, D], F32, kind="ExternalInput").ap()
    idn_in = nc.dram_tensor("idn", [128, 128], BF16, kind="ExternalInput").ap()
    qkvw_in = nc.dram_tensor("qkvw_t", [3, 2, 128, KC * 1024], F32, kind="ExternalInput").ap()
    wgate_in = nc.dram_tensor("wgate_t", [2, 128, KC * 1024], F32, kind="ExternalInput").ap()
    wout_in = nc.dram_tensor("wout_t", [2, 128, KC * 1024], F32, kind="ExternalInput").ap()
    slope_in = nc.dram_tensor("slope_col", [128, 16], F32, kind="ExternalInput").ap()
    iota1_in = nc.dram_tensor("iota1", [128, 2], F32, kind="ExternalInput").ap()
    iota2_in = nc.dram_tensor("iota2", [128, 2], F32, kind="ExternalInput").ap()
    c256_in = nc.dram_tensor("c256", [128, 1], F32, kind="ExternalInput").ap()
    cm_in = nc.dram_tensor("cm", [128, 2 * 256], F32, kind="ExternalInput").ap()
    iotabj_in = nc.dram_tensor("iota_bj", [128, 8], F32, kind="ExternalInput").ap()
    ind_in = nc.dram_tensor("ind_c", [128, 512], BF16, kind="ExternalInput").ap()
    nw_in = nc.dram_tensor("nw_in_row", [1, D], F32, kind="ExternalInput").ap()

    out_ap = nc.dram_tensor("out", [T, D], F32, kind="ExternalOutput").ap()

    u_sp = nc.dram_tensor("u_sp", [T, D], BF16, kind=tap).ap()
    q_sp = nc.dram_tensor("q_sp", [D, T], BF16, kind=tap).ap()
    k_sp = nc.dram_tensor("k_sp", [D, T], BF16, kind=tap).ap()
    v_sp = nc.dram_tensor("v_sp", [D, T], BF16, kind=tap).ap()
    w_sp = nc.dram_tensor("w_sp", [D, T], BF16, kind=tap).ap()
    hi_sp = nc.dram_tensor("hi_sp", [D, T], BF16, kind=tap).ap()
    lo_sp = nc.dram_tensor("lo_sp", [D, T], BF16, kind=tap).ap()
    x1_sp = nc.dram_tensor("x1_sp", [T, D], F32, kind="Internal").ap()
    x2_sp = nc.dram_tensor("x2_sp", [T, D], F32, kind="Internal").ap()
    x3_sp = nc.dram_tensor("x3_sp", [T, D], F32, kind="Internal").ap()
    dbg1 = nc.dram_tensor("dbg1", [128, D], BF16, kind=tap).ap()
    dbg2 = nc.dram_tensor("dbg2", [128, D], BF16, kind=tap).ap()
    dbg3 = nc.dram_tensor("dbg3", [128, 32], F32, kind=tap).ap()
    dbg4 = nc.dram_tensor("dbg4", [128, D], F32, kind=tap).ap()
    dbg5 = nc.dram_tensor("dbg5", [128, D], F32, kind=tap).ap()
    cc_in = nc.dram_tensor("cc_in", [128, KC], F32, kind="Internal")
    cc_out = nc.dram_tensor("cc_out", [256, KC], F32, kind="Internal")
    bq_sp = nc.dram_tensor("bq_sp", [T, D], BF16, kind="Internal").ap()
    bk_sp = nc.dram_tensor("bk_sp", [T, D], BF16, kind="Internal").ap()
    bv_sp = nc.dram_tensor("bv_sp", [T, D], BF16, kind="Internal").ap()
    gate_sp = nc.dram_tensor("gate_sp", [T, D], BF16, kind="Internal").ap()
    ao_sp = nc.dram_tensor("ao_sp", [T, D], BF16, kind="Internal").ap()
    gn_sp = nc.dram_tensor("gn_sp", [T, D], BF16, kind="Internal").ap()
    cc2_in = nc.dram_tensor("cc2_in", [128, D], F32, kind="Internal")
    cc2_out = nc.dram_tensor("cc2_out", [256, D], F32, kind="Internal")

    with ExitStack() as stk:
        e = stk.enter_context

        cPE = Ctr(e(nc.semaphore("pe")))
        cACT = Ctr(e(nc.semaphore("act")))
        cDVE = Ctr(e(nc.semaphore("dve")))
        cPOOL = Ctr(e(nc.semaphore("pool")))
        cCC = Ctr(e(nc.semaphore("cc")))
        cCI = Ctr(e(nc.semaphore("cc_in_st")))
        cGA = Ctr(e(nc.semaphore("gath_ld")))
        sIN = Ctr(e(nc.semaphore("s_init")))
        sXL = [Ctr(e(nc.semaphore(f"s_xl{i}"))) for i in range(2)]
        sST = [Ctr(e(nc.semaphore(f"s_st{i}"))) for i in range(2)]
        sWS = [Ctr(e(nc.semaphore(f"s_ws{i}"))) for i in range(2)]
        sRG = [Ctr(e(nc.semaphore(f"s_rg{i}"))) for i in range(2)]
        sCH = [Ctr(e(nc.semaphore(f"s_ch{i}"))) for i in range(2)]
        sSP = [Ctr(e(nc.semaphore(f"s_sp{i}"))) for i in range(2)]
        sXB = [Ctr(e(nc.semaphore(f"s_xb{i}"))) for i in range(2)]
        cQR = Ctr(e(nc.semaphore("s_qr")))
        sING = Ctr(e(nc.semaphore("s_init_g")))

        flag_t = e(nc.sbuf_tensor("flag_t", [128, 1], F32))
        ones_t = e(nc.sbuf_tensor("ones_t", [1, 128], BF16))
        bias_c = e(nc.sbuf_tensor("bias_c", [128, 5 * KC], F32))
        g_bc = e(nc.sbuf_tensor("g_bc", [128, D], BF16))
        b_bc = e(nc.sbuf_tensor("b_bc", [128, D], BF16))
        zeros_bf = e(nc.sbuf_tensor("zeros_bf", [128, D], BF16))
        stt = e(nc.sbuf_tensor("stt", [128, 32], F32))
        colA = e(nc.sbuf_tensor("colA", [128, KC, NTB], F32))
        colS = e(nc.sbuf_tensor("colS", [128, KC], F32))
        gath = e(nc.sbuf_tensor("gath", [128, KC], F32))
        ceff = e(nc.sbuf_tensor("ceff", [128, 4 * KC], F32))
        eps_t = e(nc.sbuf_tensor("eps_t", [128, 1], F32))
        eps2_t = e(nc.sbuf_tensor("eps2_t", [128, 1], F32))

        x_t2 = e(nc.sbuf_tensor("x_t2", [128, 2, D], F32))
        mvA = e(nc.sbuf_tensor("mvA", [128, KC, T], BF16))
        ch_re = e(nc.sbuf_tensor("ch_re", [128, 2, T], BF16))
        colT = e(nc.sbuf_tensor("colT", [128, KC], F32))
        scr1 = e(nc.sbuf_tensor("scr1", [128, 1], F32))
        out_t2 = e(nc.sbuf_tensor("out_t2", [128, 2, T], F32))
        astk = ExitStack()
        ae = astk.enter_context
        u1_scr = ae(nc.sbuf_tensor("u1_scr", [128, D], F32))
        h1_scr = ae(nc.sbuf_tensor("h1_scr", [128, 2, D], BF16))
        wstrip = ae(nc.sbuf_tensor("wstrip", [128, 2, KC, 128], BF16))
        kv_stage = ae(nc.sbuf_tensor("kv_stage", [128, 2, TBW], BF16))
        ringA = ae(nc.sbuf_tensor("ringA", [128, 2, T], BF16))
        tmp_f = ae(nc.sbuf_tensor("tmp_f", [128, 2, T], F32))
        hi_st = ae(nc.sbuf_tensor("hi_st", [128, 2, T], BF16))
        lo_st = ae(nc.sbuf_tensor("lo_st", [128, 2, T], BF16))
        hi_tm = ae(nc.sbuf_tensor("hi_tm", [128, 2, D], BF16))
        lo_tm = ae(nc.sbuf_tensor("lo_tm", [128, 2, D], BF16))

        pbank = [e(nc.psum_tensor(f"pb{i}", [128, TBW], F32)) for i in range(7)]
        pbank_bf = e(nc.psum_tensor("pbbf", [128, 1024], BF16))
        NB = 7
        bank_last = [(cACT, 0)] * NB
        bf_last = [cACT, 0]
        wres_pe = {"n": 0}

        all_ctrs = ([cPE, cACT, cDVE, cPOOL, cCC, cCI, cGA, sIN, cQR, sING]
                    + sXL + sST + sWS + sRG + sCH + sSP + sXB)
        with nc.Block() as blk:
            @blk.gpsimd
            def _(gp):
                for ctr in all_ctrs:
                    gp.sem_clear(ctr.sem)
        nc.all_engine_barrier()

        with ExitStack() as istk:
            grow_b = istk.enter_context(nc.sbuf_tensor("grow_b", [1, D], BF16))
            brow_b = istk.enter_context(nc.sbuf_tensor("brow_b", [1, D], BF16))

            with nc.Block() as blk:
                @blk.sync
                def _(sync):
                    for dst, src in ((flag_t[:, :], flag_in), (ones_t[:, :], ones_in),
                                     (bias_c[:, 0:KC], bq_in), (bias_c[:, KC:2 * KC], bk_in),
                                     (bias_c[:, 2 * KC:3 * KC], bv_in),
                                     (bias_c[:, 3 * KC:4 * KC], bqa_in),
                                     (bias_c[:, 4 * KC:5 * KC], bka_in)):
                        sIN.inc(sync.dma_start(out=dst, in_=src), 16)

                @blk.gpsimd
                def _(gp):
                    sING.inc(gp.dma_start(out=grow_b[:, :], in_=g_in), 16)
                    sING.inc(gp.dma_start(out=brow_b[:, :], in_=bln_in), 16)
                init_in = sIN.n
                init_ing = sING.n

                @blk.vector
                def _(v):
                    v.memset(zeros_bf[:, :], 0.0)
                    v.memset(eps_t[:, :], LN_EPS)
                    v.memset(eps2_t[:, :], LN_EPS * LN_EPS)
                    v.wait_ge(sIN.sem, init_in)
                    v.wait_ge(sING.sem, init_ing)
                    v.tensor_scalar_mul(bias_c[:, 3 * KC:4 * KC], bias_c[:, 3 * KC:4 * KC], ATT_SCALE)
                    cDVE.inc(v.tensor_scalar_mul(bias_c[:, 4 * KC:5 * KC], bias_c[:, 4 * KC:5 * KC],
                                                 ATT_SCALE))

                @blk.tensor
                def _(t):
                    t.wait_ge(sIN.sem, init_in)
                    t.wait_ge(sING.sem, init_ing)
                    inst = None
                    for j in range(NTB):
                        inst = t.matmul(pbank[j][:, :], ones_t[0:1, :],
                                        grow_b[0:1, j * TBW:(j + 1) * TBW], start=True, stop=True)
                    pe_g = cPE.inc(inst)
                    t.wait_ge(cACT.sem, cACT.n + 1)
                    inst = None
                    for j in range(NTB):
                        inst = t.matmul(pbank[j][:, :], ones_t[0:1, :],
                                        brow_b[0:1, j * TBW:(j + 1) * TBW], start=True, stop=True)
                    cPE.inc(inst)

                @blk.scalar
                def _(s):
                    s.wait_ge(cPE.sem, cPE.n - 1)
                    inst = None
                    for j in range(NTB):
                        inst = s.copy(g_bc[:, j * TBW:(j + 1) * TBW], pbank[j][:, :])
                    cACT.inc(inst)
                    s.wait_ge(cPE.sem, cPE.n)
                    inst = None
                    for j in range(NTB):
                        inst = s.copy(b_bc[:, j * TBW:(j + 1) * TBW], pbank[j][:, :])
                    cACT.inc(inst)
                for b_ in range(NB):
                    bank_last[b_] = (cACT, cACT.n)

        if DEBUG_TAPS:
            with nc.Block() as blk:
                @blk.sync
                def _(sync):
                    sync.wait_ge(cACT.sem, cACT.n)
                    sIN.inc(sync.dma_start(out=dbg1, in_=g_bc[:, :]), 16)
                    sIN.inc(sync.dma_start(out=dbg2, in_=b_bc[:, :]), 16)

        with nc.Block() as blk:
            v0, a0, p0 = cDVE.n, cACT.n, cPOOL.n
            xl0 = [sXL[0].n, sXL[1].n]
            st0 = [sST[0].n, sST[1].n]

            @blk.sync
            def _(sync):
                def store(j):
                    sync.wait_ge(cPOOL.sem, p0 + j + 1)
                    sST[j % 2].inc(sync.dma_start(out=u_sp[j * 128:(j + 1) * 128, :],
                                                  in_=h1_scr[:, j % 2]), 16)
                for i in range(KC):
                    if i >= 2:
                        sync.wait_ge(cDVE.sem, v0 + (i - 2) * 9 + 8)
                    sXL[i % 2].inc(sync.dma_start(out=x_t2[:, i % 2],
                                                  in_=x_in[i * 128:(i + 1) * 128, :]), 16)
                    if i >= 1:
                        store(i - 1)
                store(KC - 1)

            @blk.vector
            def _(v):
                for i in range(KC):
                    pi = (i % 2) * 16
                    xt = x_t2[:, i % 2]
                    hs = h1_scr[:, i % 2]
                    v.wait_ge(sXL[i % 2].sem, xl0[i % 2] + (i // 2 + 1) * 16)
                    if i >= 2:
                        v.wait_ge(cACT.sem, a0 + (i - 2) * 2 + 2)
                        v.wait_ge(cPOOL.sem, p0 + i - 1)
                        v.wait_ge(sST[i % 2].sem, st0[i % 2] + ((i - 2) // 2 + 1) * 16)
                    n = {}
                    def op(key, inst):
                        n[key] = cDVE.inc(inst)
                    op(1, v.tensor_scalar(hs, xt, 1.0, 0.0, ALU.mult, ALU.add,
                                          accum_out=stt[:, pi + 0:pi + 1]))
                    op(2, v.scalar_tensor_tensor(ch_re[:, 0, 0:D], xt, 1.0, xt, ALU.mult, ALU.mult,
                                                 accum_out=stt[:, pi + 1:pi + 2]))
                    v.wait_ge(cDVE.sem, n[1])
                    op(3, v.tensor_scalar_mul(stt[:, pi + 2:pi + 3], stt[:, pi + 0:pi + 1], 1.0 / D))
                    v.wait_ge(cDVE.sem, n[2])
                    op(4, v.tensor_scalar_mul(stt[:, pi + 3:pi + 4], stt[:, pi + 1:pi + 2], 1.0 / D))
                    v.wait_ge(cDVE.sem, n[3])
                    op(5, v.tensor_tensor(stt[:, pi + 4:pi + 5], stt[:, pi + 2:pi + 3],
                                          stt[:, pi + 2:pi + 3], ALU.mult))
                    v.wait_ge(cDVE.sem, n[5])
                    op(6, v.tensor_tensor(stt[:, pi + 5:pi + 6], stt[:, pi + 3:pi + 4],
                                          stt[:, pi + 4:pi + 5], ALU.subtract))
                    v.wait_ge(cDVE.sem, n[6])
                    op(7, v.tensor_scalar_mul(stt[:, pi + 8:pi + 9], stt[:, pi + 5:pi + 6],
                                              1.0 + LN_EPS))
                    v.wait_ge(cACT.sem, a0 + i * 2 + 2)
                    op(8, v.tensor_scalar(u1_scr[:, :], xt, stt[:, pi + 2:pi + 3], stt[:, pi + 6:pi + 7],
                                          ALU.subtract, ALU.mult))
                    v.wait_ge(cDVE.sem, n[8])
                    op(9, v.scalar_tensor_tensor(hs, u1_scr[:, :], 1.0, g_bc[:, :],
                                                 ALU.mult, ALU.mult))

            @blk.scalar
            def _(s):
                for i in range(KC):
                    pi = (i % 2) * 16
                    s.wait_ge(cDVE.sem, v0 + i * 9 + 7)
                    na = cACT.inc(s.activation(stt[:, pi + 7:pi + 8], stt[:, pi + 8:pi + 9], AF.Ln,
                                               bias=eps2_t[:, 0:1]))
                    s.wait_ge(cACT.sem, na)
                    cACT.inc(s.activation(stt[:, pi + 6:pi + 7], stt[:, pi + 7:pi + 8], AF.Exp, scale=-0.5))

            @blk.gpsimd
            def _(gp):
                for i in range(KC):
                    gp.wait_ge(cDVE.sem, v0 + i * 9 + 9)
                    if i >= 2:
                        gp.wait_ge(sST[i % 2].sem, st0[i % 2] + ((i - 2) // 2 + 1) * 16)
                    cPOOL.inc(gp.tensor_tensor(h1_scr[:, i % 2], h1_scr[:, i % 2], b_bc[:, :], ALU.add))

        u_stores = (sST[0].n, sST[1].n)

        def proj_B(w_dram, mov, evac, mov_ready=(), extra_stage_wait=None, post_sync=None):
            pe0, act0 = cPE.n, cACT.n
            ws0 = [sWS[0].n, sWS[1].n]
            prev_bank = list(bank_last)

            with nc.Block() as blk:
                @blk.gpsimd
                def _(gp):
                    for oc in range(KC):
                        if oc >= 2:
                            gp.wait_ge(cPE.sem, pe0 + (oc - 1) * NTB)
                        else:
                            gp.wait_ge(cPE.sem, pe0)
                        sWS[oc % 2].inc(gp.dma_start(
                            out=wstrip[:, oc % 2, :, :].rearrange("p k o -> p (k o)"),
                            in_=w_dram[oc]), 16)

                @blk.tensor
                def _(t):
                    for sem, val in mov_ready:
                        t.wait_ge(sem, val)
                    for oc in range(KC):
                        t.wait_ge(sWS[oc % 2].sem, ws0[oc % 2] + (oc // 2 + 1) * 16)
                        for tb in range(NTB):
                            g = oc * NTB + tb
                            if g >= NB:
                                t.wait_ge(cACT.sem, act0 + g - NB + 1)
                            else:
                                bc, bval = prev_bank[g % NB]
                                t.wait_ge(bc.sem, bval)
                            inst = None
                            for k in range(KC):
                                inst = t.matmul(pbank[g % NB][:, :], wstrip[:, oc % 2, k, :],
                                                mov[:, k, tb * TBW:(tb + 1) * TBW],
                                                start=(k == 0), stop=(k == KC - 1))
                            cPE.inc(inst)

                @blk.scalar
                def _(s):
                    for oc in range(KC):
                        for tb in range(NTB):
                            g = oc * NTB + tb
                            s.wait_ge(cPE.sem, pe0 + g + 1)
                            if extra_stage_wait is not None:
                                extra_stage_wait(s, g)
                            cACT.inc(evac(s, oc, tb, pbank[g % NB]))
                            bank_last[g % NB] = (cACT, cACT.n)

                if post_sync is not None:
                    @blk.sync
                    def _(sync):
                        post_sync(sync)

        def proj_spill(w_dram, mov, spill, bias_off, func, scale, accum, mov_ready=()):
            act0_local = cACT.n
            sp0 = [sSP[0].n, sSP[1].n]

            def evac(s, oc, tb, bank):
                g = oc * NTB + tb
                kw = {}
                if accum:
                    kw["accum_out"] = colA[:, oc, tb:tb + 1]
                return s.activation(kv_stage[:, g % 2], bank[:, :], func,
                                    bias=bias_c[:, bias_off + oc:bias_off + oc + 1],
                                    scale=scale, **kw)

            def stage_wait(s, g):
                if g >= 2:
                    s.wait_ge(sSP[g % 2].sem, sp0[g % 2] + ((g - 2) // 2 + 1) * 16)
                else:
                    s.wait_ge(sSP[g % 2].sem, sp0[g % 2])

            def post_sync(sync):
                for oc in range(KC):
                    for tb in range(NTB):
                        g = oc * NTB + tb
                        sync.wait_ge(cACT.sem, act0_local + g + 1)
                        sSP[g % 2].inc(sync.dma_start(
                            out=spill[oc * 128:(oc + 1) * 128, tb * TBW:(tb + 1) * TBW],
                            in_=kv_stage[:, g % 2]), 16)

            proj_B(w_dram, mov, evac, mov_ready=mov_ready,
                   extra_stage_wait=stage_wait, post_sync=post_sync)
            return (sSP[0].n, sSP[1].n)

        with nc.Block() as blk:
            @blk.scalar
            def _(s):
                s.wait_ge(sST[0].sem, u_stores[0])
                s.wait_ge(sST[1].sem, u_stores[1])
                for c in range(KC):
                    sXB[c % 2].inc(s.dma_start(out=mvA[:, c, :],
                                               in_=u_sp[:, c * 128:(c + 1) * 128], transpose=True), 16)
        h_ready = (sXB[0].n, sXB[1].n)

        hrdy = [(sXB[0].sem, h_ready[0]), (sXB[1].sem, h_ready[1])]
        q_stores = proj_spill(wq_in, mvA, q_sp, 0, AF.Identity, 1.0, False, mov_ready=hrdy)
        k_stores = proj_spill(wk_in, mvA, k_sp, KC, AF.Identity, 1.0, False)
        v_stores = proj_spill(wv_in, mvA, v_sp, 2 * KC, AF.Identity, 1.0, False)
        v_pe_done = cPE.n

        with nc.Block() as blk:
            @blk.sync
            def _(sync):
                sync.wait_ge(cPE.sem, v_pe_done)
                sync.wait_ge(sSP[0].sem, sSP[0].n)
                sync.wait_ge(sSP[1].sem, sSP[1].n)
                for c in range(KC):
                    cQR.inc(sync.dma_start(out=mvA[:, c, :], in_=q_sp[c * 128:(c + 1) * 128, :]), 16)
        q_loaded = cQR.n

        qw_stores = proj_spill(wqa_in, mvA, w_sp, 3 * KC, AF.Exp, ATT_SCALE, True,
                               mov_ready=[(cQR.sem, q_loaded)])
        a2_act = cACT.n

        def exchange(slot, wait_sem_val, from_colA=True):
            with nc.Block() as blk:
                if from_colA:
                    @blk.vector
                    def _(v):
                        v.wait_ge(*wait_sem_val)
                        cDVE.inc(v.tensor_tensor(colS[:, :], colA[:, :, 0], colA[:, :, 1], ALU.add))
                        nx_ = cDVE.inc(v.tensor_tensor(colT[:, :], colA[:, :, 2], colA[:, :, 3], ALU.add))
                        v.wait_ge(cDVE.sem, nx_)
                        cDVE.inc(v.tensor_tensor(colS[:, :], colS[:, :], colT[:, :], ALU.add))
                dve_sum = cDVE.n

                @blk.sync
                def _(sync):
                    if from_colA:
                        sync.wait_ge(cDVE.sem, dve_sum)
                    else:
                        sync.wait_ge(*wait_sem_val)
                    if slot > 0:
                        sync.wait_ge(cCC.sem, slot)
                    cCI.inc(sync.dma_start(out=cc_in[:, :], in_=colS[:, :]), 16)
                ci_val = cCI.n

                @blk.gpsimd
                def _(gp):
                    gp.wait_ge(cCI.sem, ci_val)
                    if slot > 0:
                        gp.wait_ge(cGA.sem, slot * 16)
                    cCC.inc(gp.collective_compute(
                        "AllGather", ALU.bypass,
                        replica_groups=[[0, 1], [2, 3], [4, 5], [6, 7]],
                        ins=[cc_in.ap().opt()], outs=[cc_out.ap().opt()]))
                    gp.wait_ge(cCC.sem, cCC.n)
                    cGA.inc(gp.dma_start(out=gath[:, :], in_=cc_out[0:128, :]), 16)
                ga_val = cGA.n

                @blk.vector
                def _(v):
                    v.wait_ge(cGA.sem, ga_val)
                    cDVE.inc(v.tensor_scalar_mul(ceff[:, slot * KC:(slot + 1) * KC],
                                                 gath[:, :], flag_t[:, 0:1]))
            return cDVE.n

        e1a_dve = exchange(0, (cACT.sem, a2_act))

        def scan_pass(src_sp, src_stores, ceff_slot, start_dve):
            with nc.Block() as blk:
                v0 = cDVE.n
                rg0 = [sRG[0].n, sRG[1].n]

                sp_tot = (sSP[0].n, sSP[1].n)

                @blk.sync
                def _(sync):
                    sync.wait_ge(sSP[0].sem, sp_tot[0])
                    sync.wait_ge(sSP[1].sem, sp_tot[1])
                    for c in range(KC):
                        p = c // 2
                        if p >= 1:
                            sync.wait_ge(cDVE.sem, v0 + p * 8)
                        sRG[c % 2].inc(sync.dma_start(out=ringA[:, c % 2],
                                                      in_=src_sp[c * 128:(c + 1) * 128, :]), 16)

                @blk.vector
                def _(v):
                    v.wait_ge(cDVE.sem, start_dve)
                    v.engine_nop()
                    prev_mul = {}
                    for cp in range(0, KC, 2):
                        ns = {}
                        for c in (cp, cp + 1):
                            v.wait_ge(sRG[c % 2].sem, rg0[c % 2] + (c // 2 + 1) * 16)
                            if c % 2 in prev_mul:
                                v.wait_ge(cDVE.sem, prev_mul[c % 2])
                            ns[("s", c)] = cDVE.inc(v.tensor_tensor_scan(
                                tmp_f[:, c % 2], ringA[:, c % 2], zeros_bf[:, :],
                                ceff[:, ceff_slot * KC + c:ceff_slot * KC + c + 1], ALU.add, ALU.add))
                        for c in (cp, cp + 1):
                            v.wait_ge(cDVE.sem, ns[("s", c)])
                            ns[("r", c)] = cDVE.inc(v.reciprocal(tmp_f[:, c % 2], tmp_f[:, c % 2]))
                        for c in (cp, cp + 1):
                            v.wait_ge(cDVE.sem, ns[("r", c)])
                            ns[("m", c)] = cDVE.inc(v.tensor_tensor(ringA[:, c % 2], ringA[:, c % 2],
                                                                    tmp_f[:, c % 2], ALU.mult))
                            prev_mul[c % 2] = ns[("m", c)]
                        for c in (cp, cp + 1):
                            v.wait_ge(cDVE.sem, ns[("m", c)])
                            cDVE.inc(v.scalar_tensor_tensor(mvA[:, c, :], ringA[:, c % 2], 1.0, mvA[:, c, :],
                                                            ALU.mult, ALU.mult, accum_out=colS[:, c:c + 1]))
            return cDVE.n

        a3_dve = scan_pass(w_sp, qw_stores, 0, e1a_dve)

        e1b_dve = exchange(1, (cDVE.sem, a3_dve), from_colA=False)

        def scan_mul_pass(src_sp, src_stores, ceff_slot, start_dve):
            with nc.Block() as blk:
                v0 = cDVE.n
                ch0 = [sCH[0].n, sCH[1].n]

                sp_tot = (sSP[0].n, sSP[1].n)

                @blk.sync
                def _(sync):
                    sync.wait_ge(sSP[0].sem, sp_tot[0])
                    sync.wait_ge(sSP[1].sem, sp_tot[1])
                    for c in range(KC):
                        p = c // 2
                        if p >= 1:
                            sync.wait_ge(cDVE.sem, v0 + p * 4)
                        sCH[c % 2].inc(sync.dma_start(out=ch_re[:, c % 2],
                                                      in_=src_sp[c * 128:(c + 1) * 128, :]), 16)

                @blk.vector
                def _(v):
                    v.wait_ge(cDVE.sem, start_dve)
                    v.engine_nop()
                    prev_mul = {}
                    for cp in range(0, KC, 2):
                        ns = {}
                        for c in (cp, cp + 1):
                            if c % 2 in prev_mul:
                                v.wait_ge(cDVE.sem, prev_mul[c % 2])
                            ns[c] = cDVE.inc(v.tensor_tensor_scan(
                                tmp_f[:, c % 2], mvA[:, c, :], zeros_bf[:, :],
                                ceff[:, ceff_slot * KC + c:ceff_slot * KC + c + 1], ALU.add, ALU.add))
                        for c in (cp, cp + 1):
                            v.wait_ge(sCH[c % 2].sem, ch0[c % 2] + (c // 2 + 1) * 16)
                            v.wait_ge(cDVE.sem, ns[c])
                            prev_mul[c % 2] = cDVE.inc(v.tensor_tensor(mvA[:, c, :], tmp_f[:, c % 2],
                                                                       ch_re[:, c % 2], ALU.mult))
            return cDVE.n

        a3b_dve = scan_mul_pass(k_sp, k_stores, 1, e1b_dve)

        kw_stores = proj_spill(wka_in, mvA, w_sp, 4 * KC, AF.Exp, ATT_SCALE, True,
                               mov_ready=[(cDVE.sem, a3b_dve)])
        a4_act = cACT.n

        e2a_dve = exchange(2, (cACT.sem, a4_act))

        a5a_dve = scan_pass(w_sp, kw_stores, 2, e2a_dve)

        e2b_dve = exchange(3, (cDVE.sem, a5a_dve), from_colA=False)

        with nc.Block() as blk:
            v0 = cDVE.n
            a0_ = cACT.n
            ch0 = [sCH[0].n, sCH[1].n]
            st0 = [sST[0].n, sST[1].n]

            @blk.sync
            def _(sync):
                def store(j):
                    sync.wait_ge(cACT.sem, a0_ + j + 1)
                    sync.wait_ge(cDVE.sem, v0 + (j // 2) * 6 + 5 + (j % 2))
                    sST[j % 2].inc(sync.dma_start(out=hi_sp[j * 128:(j + 1) * 128, :],
                                                  in_=hi_st[:, j % 2]), 16)
                    sST[j % 2].inc(sync.dma_start(out=lo_sp[j * 128:(j + 1) * 128, :],
                                                  in_=lo_st[:, j % 2]), 16)
                sync.wait_ge(sSP[0].sem, sSP[0].n)
                sync.wait_ge(sSP[1].sem, sSP[1].n)
                for c in range(KC):
                    p = c // 2
                    if p >= 1:
                        sync.wait_ge(cDVE.sem, v0 + p * 6)
                    sCH[c % 2].inc(sync.dma_start(out=ch_re[:, c % 2],
                                                  in_=v_sp[c * 128:(c + 1) * 128, :]), 16)
                    if c >= 1:
                        store(c - 1)
                store(KC - 1)

            @blk.vector
            def _(v):
                v.wait_ge(cDVE.sem, e2b_dve)
                v.engine_nop()
                prev_out = {}
                for cp in range(0, KC, 2):
                    ns = {}
                    for c in (cp, cp + 1):
                        if c % 2 in prev_out:
                            v.wait_ge(cDVE.sem, prev_out[c % 2])
                        ns[("s", c)] = cDVE.inc(v.tensor_tensor_scan(
                            tmp_f[:, c % 2], mvA[:, c, :], zeros_bf[:, :],
                            ceff[:, 3 * KC + c:3 * KC + c + 1], ALU.add, ALU.add))
                    for c in (cp, cp + 1):
                        v.wait_ge(sCH[c % 2].sem, ch0[c % 2] + (c // 2 + 1) * 16)
                        if c >= 2:
                            v.wait_ge(cACT.sem, a0_ + c - 1)
                        v.wait_ge(cDVE.sem, ns[("s", c)])
                        ns[("o", c)] = cDVE.inc(v.tensor_tensor(out_t2[:, c % 2], tmp_f[:, c % 2],
                                                                ch_re[:, c % 2], ALU.mult))
                        prev_out[c % 2] = ns[("o", c)]
                    for c in (cp, cp + 1):
                        v.wait_ge(cACT.sem, a0_ + c + 1)
                        if c >= 2:
                            v.wait_ge(sST[c % 2].sem, st0[c % 2] + ((c - 2) // 2 + 1) * 32)
                        v.wait_ge(cDVE.sem, ns[("o", c)])
                        cDVE.inc(v.tensor_tensor(lo_st[:, c % 2], out_t2[:, c % 2], hi_st[:, c % 2],
                                                 ALU.subtract))

            @blk.scalar
            def _(s):
                for c in range(KC):
                    s.wait_ge(cDVE.sem, v0 + (c // 2) * 6 + 3 + (c % 2))
                    if c >= 2:
                        s.wait_ge(sST[c % 2].sem, st0[c % 2] + ((c - 2) // 2 + 1) * 32)
                    cACT.inc(s.activation(hi_st[:, c % 2], out_t2[:, c % 2], AF.Copy))
        hilo_stores = (sST[0].n, sST[1].n)

        with nc.Block() as blk:
            v0 = cDVE.n
            xl0 = [sXL[0].n, sXL[1].n]
            st0 = [sST[0].n, sST[1].n]
            xb0 = [sXB[0].n, sXB[1].n]

            @blk.scalar
            def _(s):
                s.wait_ge(sST[0].sem, hilo_stores[0])
                s.wait_ge(sST[1].sem, hilo_stores[1])
                for i in range(KC):
                    if i >= 2:
                        s.wait_ge(cDVE.sem, v0 + (i - 2) * 2 + 1)
                    sXB[i % 2].inc(s.dma_start(out=hi_tm[:, i % 2],
                                               in_=hi_sp[:, i * 128:(i + 1) * 128], transpose=True), 16)
                    sXB[i % 2].inc(s.dma_start(out=lo_tm[:, i % 2],
                                               in_=lo_sp[:, i * 128:(i + 1) * 128], transpose=True), 16)

            @blk.sync
            def _(sync):
                def store(j):
                    sync.wait_ge(cDVE.sem, v0 + j * 2 + 2)
                    sST[j % 2].inc(sync.dma_start(out=x1_sp[j * 128:(j + 1) * 128, :],
                                                  in_=out_t2[:, j % 2]), 16)
                    if N_SUB == 1:
                        sST[j % 2].inc(sync.dma_start(out=out_ap[j * 128:(j + 1) * 128, :],
                                                      in_=out_t2[:, j % 2]), 16)
                for i in range(KC):
                    if i >= 2:
                        sync.wait_ge(cDVE.sem, v0 + (i - 2) * 2 + 2)
                    sXL[i % 2].inc(sync.dma_start(out=x_t2[:, i % 2],
                                                  in_=x_in[i * 128:(i + 1) * 128, :]), 16)
                    if i >= 1:
                        store(i - 1)
                store(KC - 1)
                sync.wait_ge(sST[0].sem, sST[0].n)
                sync.wait_ge(sST[1].sem, sST[1].n)

            @blk.vector
            def _(v):
                spt6 = 32 if N_SUB == 1 else 16
                for i in range(KC):
                    v.wait_ge(sXB[i % 2].sem, xb0[i % 2] + (i // 2 + 1) * 32)
                    v.wait_ge(sXL[i % 2].sem, xl0[i % 2] + (i // 2 + 1) * 16)
                    if i >= 2:
                        v.wait_ge(sST[i % 2].sem, st0[i % 2] + ((i - 2) // 2 + 1) * spt6)
                    nn_ = cDVE.inc(v.scalar_tensor_tensor(out_t2[:, i % 2], hi_tm[:, i % 2], 1.0,
                                                          lo_tm[:, i % 2], ALU.mult, ALU.add))
                    v.wait_ge(cDVE.sem, nn_)
                    cDVE.inc(v.tensor_tensor(out_t2[:, i % 2], out_t2[:, i % 2], x_t2[:, i % 2], ALU.add))

        astk.close()

        wres = e(nc.sbuf_tensor("wres", [128, KC, 1024], BF16))
        raf = e(nc.sbuf_tensor("raf", [128, KC, 72], F32))
        hamf = e(nc.sbuf_tensor("hamf", [64, T], BF16))
        bt_res = e(nc.sbuf_tensor("bt_res", [64, D], BF16))
        rat_t = e(nc.sbuf_tensor("rat_t", [128, KC, 72], BF16))
        bb_row = e(nc.sbuf_tensor("bb_row", [1, D], BF16))
        idn_t = e(nc.sbuf_tensor("idn_t", [128, 128], BF16))
        rsc = e(nc.sbuf_tensor("rsc", [128, 64], F32))
        ham_st = e(nc.sbuf_tensor("ham_st", [128, 64], BF16))
        hb_k = e(nc.sbuf_tensor("hb_k", [128, 2, 2, 128], BF16))
        hb_v = e(nc.sbuf_tensor("hb_v", [128, 2, 2, 128], BF16))
        hb_kf = e(nc.sbuf_tensor("hb_kf", [128, 2, 256], BF16))
        mskd2 = e(nc.sbuf_tensor("mskd2", [128, 2, 512], BF16))
        kv_sb = e(nc.sbuf_tensor("kv_sb", [128, 128], F32))
        kv_bf2 = e(nc.sbuf_tensor("kv_bf2", [128, 2, 128], BF16))
        kv_all = e(nc.sbuf_tensor("kv_all", [128, D], F32))
        mask2 = e(nc.sbuf_tensor("mask2", [128, 2, 256], F32))
        cm_t = e(nc.sbuf_tensor("cm_t", [128, 2 * 256], F32))
        dk_t = e(nc.sbuf_tensor("dk_t", [128, 8], F32))
        slope_f = e(nc.sbuf_tensor("slope_f", [128, 16], F32))
        nw_row = e(nc.sbuf_tensor("nw_row", [1, D], BF16))
        iota_t = e(nc.sbuf_tensor("iota_t", [128, 5], F32))
        ao_stage = e(nc.sbuf_tensor("ao_stage", [128, 2, 2, 128], BF16))
        qd_all = e(nc.sbuf_tensor("qd_all", [128, 32], F32))
        kd_all = e(nc.sbuf_tensor("kd_all", [128, 32], F32))
        md_all = e(nc.sbuf_tensor("md_all", [128, 32], F32))
        ind_t = e(nc.sbuf_tensor("ind_t", [128, 512], BF16))
        bdj = e(nc.sbuf_tensor("bdj", [128, 128], F32))
        iota_bj_t = e(nc.sbuf_tensor("iota_bj_t", [128, 8], F32))
        qs_stg = e(nc.sbuf_tensor("qs_stg", [128, 2, D], BF16))

        with nc.Block() as blk:
            @blk.gpsimd
            def _(gp):
                sING.inc(gp.dma_start(out=rat_t[:, :, :], in_=rat_in), 16)
                sING.inc(gp.dma_start(out=bt_res[:, :], in_=bt_in), 16)
                sING.inc(gp.dma_start(out=bb_row[:, :], in_=bb_in), 16)

            @blk.sync
            def _(sync):
                sIN.inc(sync.dma_start(out=idn_t[:, :], in_=idn_in), 16)
        ffn_consts = (sING.n, sIN.n)

        def emit_ln(x_src):
            with nc.Block() as blk:
                v0, a0 = cDVE.n, cACT.n
                xl0 = [sXL[0].n, sXL[1].n]
                st0 = [sST[0].n, sST[1].n]

                @blk.sync
                def _(sync):
                    sync.wait_ge(sST[0].sem, st0[0])
                    sync.wait_ge(sST[1].sem, st0[1])
                    for i in range(KC):
                        if i >= 2:
                            sync.wait_ge(cDVE.sem, v0 + (i - 2) * 7 + 7)
                        else:
                            sync.wait_ge(cDVE.sem, v0)
                        sXL[i % 2].inc(sync.dma_start(out=x_t2[:, i % 2],
                                                      in_=x_src[i * 128:(i + 1) * 128, :]), 16)
                        if i >= 1:
                            j = i - 1
                            sync.wait_ge(cDVE.sem, v0 + j * 7 + 7)
                            sST[j % 2].inc(sync.dma_start(out=u_sp[j * 128:(j + 1) * 128, :],
                                                          in_=ch_re[:, j % 2, 0:D]), 16)
                    j = KC - 1
                    sync.wait_ge(cDVE.sem, v0 + j * 7 + 7)
                    sST[j % 2].inc(sync.dma_start(out=u_sp[j * 128:(j + 1) * 128, :],
                                                  in_=ch_re[:, j % 2, 0:D]), 16)

                @blk.vector
                def _(v):
                    v.wait_ge(sST[0].sem, st0[0])
                    v.wait_ge(sST[1].sem, st0[1])
                    for i in range(KC):
                        pi = (i % 2) * 16
                        xt = x_t2[:, i % 2]
                        ut = ch_re[:, i % 2, 0:D]
                        v.wait_ge(sXL[i % 2].sem, xl0[i % 2] + (i // 2 + 1) * 16)
                        if i >= 2:
                            v.wait_ge(cACT.sem, a0 + (i - 2) * 2 + 2)
                            v.wait_ge(sST[i % 2].sem, st0[i % 2] + ((i - 2) // 2 + 1) * 16)
                        n = {}
                        def op(key, inst):
                            n[key] = cDVE.inc(inst)
                        op(1, v.tensor_scalar(out_t2[:, i % 2], xt, 1.0, 0.0, ALU.mult, ALU.add,
                                              accum_out=stt[:, pi + 0:pi + 1]))
                        op(2, v.scalar_tensor_tensor(out_t2[:, (i + 1) % 2], xt, 1.0, xt,
                                                     ALU.mult, ALU.mult,
                                                     accum_out=stt[:, pi + 1:pi + 2]))
                        v.wait_ge(cDVE.sem, n[1])
                        op(3, v.tensor_scalar_mul(stt[:, pi + 2:pi + 3], stt[:, pi + 0:pi + 1], 1.0 / D))
                        v.wait_ge(cDVE.sem, n[2])
                        op(4, v.tensor_scalar_mul(stt[:, pi + 3:pi + 4], stt[:, pi + 1:pi + 2], 1.0 / D))
                        v.wait_ge(cDVE.sem, n[3])
                        op(5, v.tensor_tensor(stt[:, pi + 4:pi + 5], stt[:, pi + 2:pi + 3],
                                              stt[:, pi + 2:pi + 3], ALU.mult))
                        v.wait_ge(cDVE.sem, n[5])
                        op(6, v.tensor_tensor(stt[:, pi + 5:pi + 6], stt[:, pi + 3:pi + 4],
                                              stt[:, pi + 4:pi + 5], ALU.subtract))
                        v.wait_ge(cACT.sem, a0 + i * 2 + 2)
                        op(7, v.tensor_scalar(ut, xt, stt[:, pi + 2:pi + 3], stt[:, pi + 6:pi + 7],
                                              ALU.subtract, ALU.mult))

                @blk.scalar
                def _(s):
                    for i in range(KC):
                        pi = (i % 2) * 16
                        s.wait_ge(cDVE.sem, v0 + i * 7 + 6)
                        na = cACT.inc(s.activation(stt[:, pi + 7:pi + 8], stt[:, pi + 5:pi + 6],
                                                   AF.Ln, bias=eps_t[:, 0:1]))
                        s.wait_ge(cACT.sem, na)
                        cACT.inc(s.activation(stt[:, pi + 6:pi + 7], stt[:, pi + 7:pi + 8],
                                              AF.Exp, scale=-0.5))
            return (sST[0].n, sST[1].n)

        def emit_xbar(st_tot):
            with nc.Block() as blk:
                @blk.scalar
                def _(s):
                    s.wait_ge(sST[0].sem, st_tot[0])
                    s.wait_ge(sST[1].sem, st_tot[1])
                    for c in range(KC):
                        sXB[c % 2].inc(s.dma_start(out=mvA[:, c, :],
                                                   in_=u_sp[:, c * 128:(c + 1) * 128],
                                                   transpose=True), 16)
            return (sXB[0].n, sXB[1].n)

        def emit_ffn(x_src, x_dst, store_out):
            st_tot = emit_ln(x_src)
            xb_tot = emit_xbar(st_tot)

            with nc.Block() as blk:
                pe0 = cPE.n
                v0 = cDVE.n
                a0 = cACT.n
                prev_bank = list(bank_last)
                mm_n = {}
                tp_n = {}
                dve_evac = {tr: v0 + tr * 19 + 1 for tr in range(KC)}
                dve_ham = {tr: v0 + tr * 19 + 19 for tr in range(KC)}
                dve_m2 = {tr: v0 + tr * 19 + 6 for tr in range(KC)}
                dve_r24 = {tr: v0 + tr * 19 + 10 for tr in range(KC)}
                act_d = {tr: a0 + (2 if tr == 0 else 3 * tr + 1) for tr in range(KC)}
                act_evac = {tr: a0 + (3 * (tr + 1) + 2 if tr < KC - 1 else 3 * KC) for tr in range(KC)}

                @blk.tensor
                def _(t):
                    t.wait_ge(sXB[0].sem, xb_tot[0])
                    t.wait_ge(sXB[1].sem, xb_tot[1])
                    t.wait_ge(sING.sem, ffn_consts[0])
                    t.wait_ge(sIN.sem, ffn_consts[1])

                    def mm(tr):
                        if tr >= NB:
                            t.wait_ge(cDVE.sem, dve_evac[tr - NB])
                        else:
                            bc, bval = prev_bank[tr % NB]
                            t.wait_ge(bc.sem, bval)
                        inst = None
                        for k in range(KC):
                            inst = t.matmul(pbank[tr % NB][:, 0:72],
                                            mvA[:, k, tr * 128:(tr + 1) * 128],
                                            rat_t[:, k, :], start=(k == 0), stop=(k == KC - 1))
                        mm_n[tr] = cPE.inc(inst)

                    def tp(tr):
                        t.wait_ge(cDVE.sem, dve_ham[tr])
                        if tr >= 1:
                            t.wait_ge(cACT.sem, act_evac[tr - 1])
                        tp_n[tr] = cPE.inc(t.transpose(pbank_bf[0:64, 0:128], ham_st[:, :],
                                                       idn_t[:, :]))

                    for tr in range(NB):
                        mm(tr)
                    for tr in range(KC - NB):
                        tp(tr)
                        mm(tr + NB)
                    for tr in range(KC - NB, KC):
                        tp(tr)

                @blk.vector
                def _(v):
                    for tr in range(KC):
                        n = {}
                        def op(key, inst):
                            n[key] = cDVE.inc(inst)
                        v.wait_ge(cPE.sem, pe0 + tr + 1 if tr < NB else mm_n[tr])
                        op(1, v.tensor_copy(raf[:, tr, :], pbank[tr % NB][:, 0:72]))
                        lg = raf[:, tr, 0:8]
                        v.wait_ge(cDVE.sem, n[1])
                        op(2, v.reduce_max(rsc[:, 0:1], lg, mybir.AxisListType.X))
                        v.wait_ge(cDVE.sem, n[2])
                        op(3, v.tensor_scalar_mul(rsc[:, 1:2], rsc[:, 0:1], -1.0))
                        op(4, v.tensor_single_scalar(rsc[:, 8:16], lg, rsc[:, 0:1], ALU.is_ge))
                        v.wait_ge(cDVE.sem, n[4])
                        op(5, v.scalar_tensor_tensor(rsc[:, 16:24], rsc[:, 8:16], -1e30, lg,
                                                     ALU.mult, ALU.add))
                        v.wait_ge(cDVE.sem, n[5])
                        op(6, v.reduce_max(rsc[:, 2:3], rsc[:, 16:24], mybir.AxisListType.X))
                        v.wait_ge(cACT.sem, act_d[tr])
                        op(7, v.tensor_scalar(rsc[:, 4:5], rsc[:, 3:4], 1.0, 0.5, ALU.add, ALU.mult))
                        v.wait_ge(cDVE.sem, n[7])
                        op(8, v.reciprocal(rsc[:, 5:6], rsc[:, 4:5]))
                        op(9, v.tensor_single_scalar(rsc[:, 32:40], lg, rsc[:, 2:3], ALU.is_ge))
                        v.wait_ge(cDVE.sem, n[9])
                        op(10, v.tensor_tensor(rsc[:, 40:48], rsc[:, 24:32], rsc[:, 32:40], ALU.mult))
                        v.wait_ge(cDVE.sem, n[10])
                        op(11, v.tensor_single_scalar(rsc[:, 48:56], rsc[:, 40:48], rsc[:, 5:6],
                                                      ALU.mult))
                        v.wait_ge(cDVE.sem, n[11])
                        if tr >= 1:
                            v.wait_ge(cPE.sem, tp_n[tr - 1])
                        for ei in range(8):
                            cDVE.inc(v.tensor_single_scalar(ham_st[:, ei * 8:(ei + 1) * 8],
                                                            raf[:, tr, 8 + ei * 8:8 + (ei + 1) * 8],
                                                            rsc[:, 48 + ei:49 + ei], ALU.mult))

                @blk.scalar
                def _(s):
                    for tr in range(KC):
                        s.wait_ge(cDVE.sem, dve_m2[tr])
                        if tr >= 1:
                            s.wait_ge(cDVE.sem, dve_r24[tr - 1])
                        cACT.inc(s.activation(rsc[:, 24:32], raf[:, tr, 0:8], AF.Exp,
                                              bias=rsc[:, 1:2]))
                        cACT.inc(s.activation(rsc[:, 3:4], rsc[:, 2:3], AF.Exp, bias=rsc[:, 1:2]))
                        if tr >= 1:
                            s.wait_ge(cPE.sem, tp_n[tr - 1])
                            cACT.inc(s.copy(hamf[0:64, (tr - 1) * 128:tr * 128],
                                            pbank_bf[0:64, 0:128]))
                    s.wait_ge(cPE.sem, tp_n[KC - 1])
                    cACT.inc(s.copy(hamf[0:64, (KC - 1) * 128:KC * 128], pbank_bf[0:64, 0:128]))
                for b_ in range(NB):
                    bank_last[b_] = (cDVE, cDVE.n)
                bf_last[0], bf_last[1] = cACT, cACT.n
            hamf_act = cACT.n

            with nc.Block() as blk:
                pe0 = cPE.n
                v0 = cDVE.n
                ws_0 = sWS[0].n
                xl0 = [sXL[0].n, sXL[1].n]
                st0 = [sST[0].n, sST[1].n]
                prev_bank = list(bank_last)

                @blk.gpsimd
                def _(gp):
                    for oh in range(2):
                        if oh >= 1:
                            gp.wait_ge(cPE.sem, pe0 + 32)
                        elif wres_pe["n"] > 0:
                            gp.wait_ge(cPE.sem, wres_pe["n"])
                        sWS[0].inc(gp.dma_start(out=wres[:, :, :].rearrange("p k o -> p (k o)"),
                                                in_=wbase_in[oh]), 16)

                @blk.tensor
                def _(t):
                    t.wait_ge(cACT.sem, hamf_act)
                    for oh in range(2):
                        t.wait_ge(sWS[0].sem, ws_0 + (oh + 1) * 16)
                        for tr in range(KC):
                            for oc in range(2):
                                g = oh * 32 + tr * 2 + oc
                                if g >= NB:
                                    t.wait_ge(cDVE.sem, v0 + (g - NB) + 1)
                                else:
                                    bc, bval = prev_bank[g % NB]
                                    t.wait_ge(bc.sem, bval)
                                bank = pbank[g % NB]
                                inst = None
                                for k in range(KC):
                                    inst = t.matmul(bank[:, :], mvA[:, k, tr * 128:(tr + 1) * 128],
                                                    wres[:, k, oc * TBW:(oc + 1) * TBW],
                                                    start=(k == 0), stop=False)
                                t.matmul(bank[:, :], ones_t[0:1, :],
                                         bb_row[0:1, oh * 1024 + oc * TBW:oh * 1024 + (oc + 1) * TBW],
                                         start=False, stop=False)
                                inst = t.matmul(bank[:, :], hamf[0:64, tr * 128:(tr + 1) * 128],
                                                bt_res[0:64, oh * 1024 + oc * TBW:oh * 1024 + (oc + 1) * TBW],
                                                start=False, stop=True)
                                cPE.inc(inst)

                @blk.sync
                def _(sync):
                    for oh in range(2):
                        for tr in range(KC):
                            g2 = oh * 16 + tr
                            if g2 >= 2:
                                sync.wait_ge(cDVE.sem, v0 + (g2 - 2) * 2 + 2)
                            else:
                                sync.wait_ge(cDVE.sem, v0)
                            sXL[tr % 2].inc(sync.dma_start(
                                out=x_t2[:, tr % 2, 0:1024],
                                in_=x_src[tr * 128:(tr + 1) * 128, oh * 1024:(oh + 1) * 1024]), 16)
                            if g2 >= 1:
                                j2 = g2 - 1
                                joh, jtr = j2 // 16, j2 % 16
                                sync.wait_ge(cDVE.sem, v0 + j2 * 2 + 2)
                                sST[jtr % 2].inc(sync.dma_start(
                                    out=x_dst[jtr * 128:(jtr + 1) * 128, joh * 1024:(joh + 1) * 1024],
                                    in_=out_t2[:, jtr % 2, 0:1024]), 16)
                                if store_out:
                                    sST[jtr % 2].inc(sync.dma_start(
                                        out=out_ap[jtr * 128:(jtr + 1) * 128, joh * 1024:(joh + 1) * 1024],
                                        in_=out_t2[:, jtr % 2, 0:1024]), 16)
                    j2 = 31
                    joh, jtr = j2 // 16, j2 % 16
                    sync.wait_ge(cDVE.sem, v0 + j2 * 2 + 2)
                    sST[jtr % 2].inc(sync.dma_start(
                        out=x_dst[jtr * 128:(jtr + 1) * 128, joh * 1024:(joh + 1) * 1024],
                        in_=out_t2[:, jtr % 2, 0:1024]), 16)
                    if store_out:
                        sST[jtr % 2].inc(sync.dma_start(
                            out=out_ap[jtr * 128:(jtr + 1) * 128, joh * 1024:(joh + 1) * 1024],
                            in_=out_t2[:, jtr % 2, 0:1024]), 16)
                    if store_out:
                        sync.wait_ge(sST[0].sem, sST[0].n)
                        sync.wait_ge(sST[1].sem, sST[1].n)

                @blk.vector
                def _(v):
                    spt = 32 if store_out else 16
                    for oh in range(2):
                        for tr in range(KC):
                            g2 = oh * 16 + tr
                            v.wait_ge(sXL[tr % 2].sem, xl0[tr % 2] + (g2 // 2 + 1) * 16)
                            if g2 >= 2:
                                v.wait_ge(sST[tr % 2].sem, st0[tr % 2] + ((g2 - 2) // 2 + 1) * spt)
                            nprev = None
                            for oc in range(2):
                                g = oh * 32 + tr * 2 + oc
                                v.wait_ge(cPE.sem, pe0 + g + 1)
                                if nprev is not None:
                                    v.wait_ge(cDVE.sem, nprev)
                                nprev = cDVE.inc(v.scalar_tensor_tensor(
                                    out_t2[:, tr % 2, oc * TBW:(oc + 1) * TBW],
                                    pbank[g % NB][:, :], 1.0,
                                    x_t2[:, tr % 2, oc * TBW:(oc + 1) * TBW],
                                    ALU.mult, ALU.add))
                for b_ in range(NB):
                    bank_last[b_] = (cDVE, cDVE.n)
                wres_pe["n"] = cPE.n

        def emit_optA(wt2, dst_sp, func=None, x_src=None, x_dst=None, store_out=False,
                      extra_ready=()):
            with nc.Block() as blk:
                pe0 = cPE.n
                v0 = cDVE.n
                a0 = cACT.n
                ws_0 = sWS[0].n
                xl0 = [sXL[0].n, sXL[1].n]
                st0 = [sST[0].n, sST[1].n]
                prev_bank = list(bank_last)
                evac_ctr = cACT if func is not None else cDVE

                @blk.gpsimd
                def _(gp):
                    for oh in range(2):
                        if oh >= 1:
                            gp.wait_ge(cPE.sem, pe0 + 32)
                        else:
                            gp.wait_ge(cPE.sem, pe0)
                        sWS[0].inc(gp.dma_start(out=wres[:, :, :].rearrange("p k o -> p (k o)"),
                                                in_=wt2[oh]), 16)

                @blk.tensor
                def _(t):
                    for sem, val in extra_ready:
                        t.wait_ge(sem, val)
                    for oh in range(2):
                        t.wait_ge(sWS[0].sem, ws_0 + (oh + 1) * 16)
                        for tr in range(KC):
                            for oc in range(2):
                                g = oh * 32 + tr * 2 + oc
                                if g >= NB:
                                    t.wait_ge(evac_ctr.sem, v0 + (g - NB) + 1 if func is None
                                              else a0 + (g - NB) + 1)
                                else:
                                    bc, bval = prev_bank[g % NB]
                                    t.wait_ge(bc.sem, bval)
                                bank = pbank[g % NB]
                                inst = None
                                for k in range(KC):
                                    inst = t.matmul(bank[:, :], mvA[:, k, tr * 128:(tr + 1) * 128],
                                                    wres[:, k, oc * TBW:(oc + 1) * TBW],
                                                    start=(k == 0), stop=(k == KC - 1))
                                cPE.inc(inst)

                if func is not None:
                    @blk.scalar
                    def _(s):
                        for oh in range(2):
                            for tr in range(KC):
                                g2 = oh * 16 + tr
                                if g2 >= 2:
                                    s.wait_ge(sST[tr % 2].sem,
                                              st0[tr % 2] + ((g2 - 2) // 2 + 1) * 16)
                                else:
                                    s.wait_ge(sST[tr % 2].sem, st0[tr % 2])
                                for oc in range(2):
                                    g = oh * 32 + tr * 2 + oc
                                    s.wait_ge(cPE.sem, pe0 + g + 1)
                                    cACT.inc(s.activation(
                                        ch_re[:, tr % 2, oc * TBW:(oc + 1) * TBW],
                                        pbank[g % NB][:, :], func))

                    @blk.sync
                    def _(sync):
                        for oh in range(2):
                            for tr in range(KC):
                                g2 = oh * 16 + tr
                                sync.wait_ge(cACT.sem, a0 + g2 * 2 + 2)
                                sST[tr % 2].inc(sync.dma_start(
                                    out=dst_sp[tr * 128:(tr + 1) * 128, oh * 1024:(oh + 1) * 1024],
                                    in_=ch_re[:, tr % 2, 0:1024]), 16)
                    for b_ in range(NB):
                        bank_last[b_] = (cACT, cACT.n)
                else:
                    @blk.sync
                    def _(sync):
                        spt = 32 if store_out else 16
                        nst = 0
                        for oh in range(2):
                            for tr in range(KC):
                                g2 = oh * 16 + tr
                                if g2 >= 2:
                                    sync.wait_ge(cDVE.sem, v0 + (g2 - 2) * 2 + 2)
                                else:
                                    sync.wait_ge(cDVE.sem, v0)
                                sXL[tr % 2].inc(sync.dma_start(
                                    out=x_t2[:, tr % 2, 0:1024],
                                    in_=x_src[tr * 128:(tr + 1) * 128, oh * 1024:(oh + 1) * 1024]), 16)
                                if g2 >= 1:
                                    j2 = g2 - 1
                                    joh, jtr = j2 // 16, j2 % 16
                                    sync.wait_ge(cDVE.sem, v0 + j2 * 2 + 2)
                                    sST[jtr % 2].inc(sync.dma_start(
                                        out=x_dst[jtr * 128:(jtr + 1) * 128,
                                                  joh * 1024:(joh + 1) * 1024],
                                        in_=out_t2[:, jtr % 2, 0:1024]), 16)
                                    if store_out:
                                        sST[jtr % 2].inc(sync.dma_start(
                                            out=out_ap[jtr * 128:(jtr + 1) * 128,
                                                       joh * 1024:(joh + 1) * 1024],
                                            in_=out_t2[:, jtr % 2, 0:1024]), 16)
                        j2 = 31
                        joh, jtr = 1, 15
                        sync.wait_ge(cDVE.sem, v0 + j2 * 2 + 2)
                        sST[jtr % 2].inc(sync.dma_start(
                            out=x_dst[jtr * 128:(jtr + 1) * 128, joh * 1024:(joh + 1) * 1024],
                            in_=out_t2[:, jtr % 2, 0:1024]), 16)
                        if store_out:
                            sST[jtr % 2].inc(sync.dma_start(
                                out=out_ap[jtr * 128:(jtr + 1) * 128, joh * 1024:(joh + 1) * 1024],
                                in_=out_t2[:, jtr % 2, 0:1024]), 16)
                            sync.wait_ge(sST[0].sem, sST[0].n)
                            sync.wait_ge(sST[1].sem, sST[1].n)

                    @blk.vector
                    def _(v):
                        spt = 32 if store_out else 16
                        for oh in range(2):
                            for tr in range(KC):
                                g2 = oh * 16 + tr
                                v.wait_ge(sXL[tr % 2].sem, xl0[tr % 2] + (g2 // 2 + 1) * 16)
                                if g2 >= 2:
                                    v.wait_ge(sST[tr % 2].sem,
                                              st0[tr % 2] + ((g2 - 2) // 2 + 1) * spt)
                                else:
                                    v.wait_ge(sST[tr % 2].sem, st0[tr % 2])
                                nprev = None
                                for oc in range(2):
                                    g = oh * 32 + tr * 2 + oc
                                    v.wait_ge(cPE.sem, pe0 + g + 1)
                                    if nprev is not None:
                                        v.wait_ge(cDVE.sem, nprev)
                                    nprev = cDVE.inc(v.scalar_tensor_tensor(
                                        out_t2[:, tr % 2, oc * TBW:(oc + 1) * TBW],
                                        pbank[g % NB][:, :], 1.0,
                                        x_t2[:, tr % 2, oc * TBW:(oc + 1) * TBW],
                                        ALU.mult, ALU.add))
                    for b_ in range(NB):
                        bank_last[b_] = (cDVE, cDVE.n)

        def emit_attn_b(x_src, x_dst, store_out):
            st_tot = emit_ln(x_src)
            xb_tot = emit_xbar(st_tot)

            with nc.Block() as blk:
                @blk.sync
                def _(sync):
                    sIN.inc(sync.dma_start(out=cm_t[:, :], in_=cm_in), 16)
                    sIN.inc(sync.dma_start(out=iota_t[:, 0:2], in_=iota1_in), 16)
                    sIN.inc(sync.dma_start(out=iota_t[:, 2:4], in_=iota2_in), 16)
                    sIN.inc(sync.dma_start(out=iota_t[:, 4:5], in_=c256_in), 16)
                    sIN.inc(sync.dma_start(out=slope_f[:, :], in_=slope_in), 16)
                    sIN.inc(sync.dma_start(out=iota_bj_t[:, :], in_=iotabj_in), 16)
                    sIN.inc(sync.dma_start(out=ind_t[:, :], in_=ind_in), 16)

                @blk.gpsimd
                def _(gp):
                    sING.inc(gp.dma_start(out=nw_row[:, :], in_=nw_in), 16)
                binit_in = (sIN.n, sING.n)

                @blk.tensor
                def _(t):
                    t.wait_ge(sING.sem, binit_in[1])
                    t.wait_ge(cACT.sem, cACT.n)
                    inst = None
                    for j in range(NTB):
                        inst = t.matmul(pbank[j % NB][:, :], ones_t[0:1, :],
                                        nw_row[0:1, j * TBW:(j + 1) * TBW], start=True, stop=True)
                    cPE.inc(inst)
                nwpe = cPE.n

                @blk.scalar
                def _(s):
                    s.wait_ge(cPE.sem, nwpe)
                    inst = None
                    for j in range(NTB):
                        inst = s.copy(g_bc[:, j * TBW:(j + 1) * TBW], pbank[j % NB][:, :])
                    cACT.inc(inst)

                @blk.vector
                def _(v):
                    cDVE.inc(v.memset(dk_t[:, 7:8], 1e-6))
            binit_act = cACT.n
            binit_dve = cDVE.n
            for b_ in range(NTB):
                bank_last[b_] = (cACT, binit_act)

            NH, NJ = 16, 8

            with nc.Block() as blk:
                v0p, a0p = cDVE.n, cACT.n

                @blk.vector
                def _(v):
                    v.wait_ge(cDVE.sem, binit_dve)
                    v.wait_ge(sIN.sem, binit_in[0])
                    for h in range(NH):
                        if h >= 1:
                            v.wait_ge(cACT.sem, a0p + 4 * h)
                        cDVE.inc(v.tensor_scalar_mul(dk_t[:, 6:7], slope_f[:, h:h + 1], -1.0))

                @blk.scalar
                def _(s):
                    for h in range(NH):
                        s.wait_ge(cDVE.sem, v0p + h + 1)
                        cACT.inc(s.activation(qd_all[:, h * 2:h * 2 + 2], iota_t[:, 0:2],
                                              AF.Exp, scale=dk_t[:, 6:7]))
                        cACT.inc(s.activation(kd_all[:, h * 2:h * 2 + 2], iota_t[:, 2:4],
                                              AF.Exp, scale=dk_t[:, 6:7]))
                        cACT.inc(s.activation(bdj[:, h * 8:(h + 1) * 8], iota_bj_t[:, :],
                                              AF.Exp, scale=dk_t[:, 6:7]))
                        cACT.inc(s.activation(md_all[:, h * 2:h * 2 + 2], iota_t[:, 0:2],
                                              AF.Exp, scale=slope_f[:, h:h + 1]))
            p0_act = cACT.n

            def emit_p1_pass(src_sp2, gate, tab, dst_sp2):
                with nc.Block() as blk:
                    v0 = cDVE.n
                    rg0 = [sRG[0].n, sRG[1].n]
                    sp0 = [sSP[0].n, sSP[1].n]

                    @blk.sync
                    def _(sync):
                        def store(j):
                            sync.wait_ge(cDVE.sem, v0 + (j + 1) * NH)
                            sSP[j % 2].inc(sync.dma_start(
                                out=dst_sp2[j * 128:(j + 1) * 128, :],
                                in_=qs_stg[:, j % 2]), 16)
                        for i in range(KC):
                            sync.wait_ge(sST[0].sem, gate[0])
                            sync.wait_ge(sST[1].sem, gate[1])
                            if i >= 2:
                                sync.wait_ge(sSP[i % 2].sem,
                                             sp0[i % 2] + ((i - 2) // 2 + 1) * 16)
                            else:
                                sync.wait_ge(sSP[i % 2].sem, sp0[i % 2])
                            sRG[i % 2].inc(sync.dma_start(
                                out=qs_stg[:, i % 2],
                                in_=src_sp2[i * 128:(i + 1) * 128, :]), 16)
                            if i >= 1:
                                store(i - 1)
                        store(KC - 1)

                    @blk.vector
                    def _(v):
                        v.wait_ge(cACT.sem, p0_act)
                        for i in range(KC):
                            v.wait_ge(sRG[i % 2].sem, rg0[i % 2] + (i // 2 + 1) * 16)
                            for h in range(NH):
                                cDVE.inc(v.tensor_single_scalar(
                                    qs_stg[:, i % 2, h * 128:(h + 1) * 128],
                                    qs_stg[:, i % 2, h * 128:(h + 1) * 128],
                                    tab[:, h * 2 + (i % 2):h * 2 + (i % 2) + 1], ALU.mult))

            xbrdy = [(sXB[0].sem, xb_tot[0]), (sXB[1].sem, xb_tot[1]),
                     (sIN.sem, binit_in[0])]
            emit_optA(qkvw_in[0], bq_sp, func=AF.Silu, extra_ready=xbrdy)
            bq_st = (sST[0].n, sST[1].n)
            emit_p1_pass(bq_sp, bq_st, qd_all, u_sp)
            emit_optA(qkvw_in[1], bk_sp, func=AF.Silu)
            bk_st = (sST[0].n, sST[1].n)
            emit_p1_pass(bk_sp, bk_st, kd_all, k_sp)
            emit_p1_pass(bk_sp, bk_st, md_all, q_sp)
            emit_optA(qkvw_in[2], bv_sp, func=AF.Silu)
            emit_optA(wgate_in, gate_sp, func=AF.Sigmoid)
            b1_st = (sST[0].n, sST[1].n)
            b1_pe = cPE.n
            p1_sp = (sSP[0].n, sSP[1].n)


            with nc.Block() as blk:
                @blk.scalar
                def _(s):
                    s.wait_ge(sSP[0].sem, p1_sp[0])
                    s.wait_ge(sSP[1].sem, p1_sp[1])
                    s.wait_ge(cPE.sem, b1_pe)
                    for h in range(NH):
                        sXB[h % 2].inc(s.dma_start(out=mvA[:, h, :],
                                                   in_=u_sp[:, h * 128:(h + 1) * 128],
                                                   transpose=True), 16)
            qs_xb = (sXB[0].n, sXB[1].n)

            NS = NH * NJ
            with nc.Block() as blk:
                pe0u, v0u = cPE.n, cDVE.n
                ch0 = [sCH[0].n, sCH[1].n]
                xl0 = [sXL[0].n, sXL[1].n]
                prev_bank = list(bank_last)
                pe_u = {s_: pe0u + s_ + 1 for s_ in range(NS)}
                dve_uev = {s_: v0u + s_ + 1 for s_ in range(NS)}

                @blk.sync
                def _(sync):
                    sync.wait_ge(sST[0].sem, b1_st[0])
                    sync.wait_ge(sST[1].sem, b1_st[1])
                    for s_ in range(NS):
                        h, j = divmod(s_, NJ)
                        p = s_ % 2
                        if s_ >= 2:
                            sync.wait_ge(cPE.sem, pe_u[s_ - 2])
                        sXL[p].inc(sync.dma_start(
                            out=hb_v[:, p, :, :].rearrange("a b c -> a (b c)"),
                            in_=bv_sp[j * 256:(j + 1) * 256, h * 128:(h + 1) * 128]
                            .rearrange("(s p) d -> p s d", s=2)), 16)

                @blk.tensor
                def _(t):
                    for s_ in range(NS):
                        p = s_ % 2
                        t.wait_ge(sCH[p].sem, ch0[p] + (s_ // 2 + 1) * 16)
                        t.wait_ge(sXL[p].sem, xl0[p] + (s_ // 2 + 1) * 16)
                        if s_ >= NB:
                            t.wait_ge(cDVE.sem, dve_uev[s_ - NB])
                        else:
                            bc, bval = prev_bank[s_ % NB]
                            t.wait_ge(bc.sem, bval)
                        inst = None
                        for sub in range(2):
                            inst = t.matmul(pbank[s_ % NB][:, 0:128], hb_k[:, p, sub, :],
                                            hb_v[:, p, sub, :], start=(sub == 0), stop=(sub == 1))
                        cPE.inc(inst)

                @blk.scalar
                def _(s):
                    s.wait_ge(sSP[0].sem, p1_sp[0])
                    s.wait_ge(sSP[1].sem, p1_sp[1])
                    for s_ in range(NS):
                        h, j = divmod(s_, NJ)
                        p = s_ % 2
                        if s_ >= 2:
                            s.wait_ge(cPE.sem, pe_u[s_ - 2])
                        sCH[p].inc(s.dma_start(
                            out=hb_k[:, p, :, :].rearrange("a b c -> a (b c)"),
                            in_=k_sp[j * 256:(j + 1) * 256, h * 128:(h + 1) * 128]
                            .rearrange("(s p) d -> p s d", s=2)), 16)

                @blk.vector
                def _(v):
                    for s_ in range(NS):
                        v.wait_ge(cPE.sem, pe_u[s_])
                        cDVE.inc(v.tensor_copy(
                            wres[:, s_ // NJ, (s_ % NJ) * 128:(s_ % NJ + 1) * 128],
                            pbank[s_ % NB][:, 0:128]))
                for b_ in range(NB):
                    bank_last[b_] = (cDVE, cDVE.n)
            p3_pe = cPE.n
            p3_dve = cDVE.n

            with nc.Block() as blk:
                @blk.vector
                def _(v):
                    for h in range(NH):
                        v.wait_ge(cDVE.sem, p3_dve - (NH - 1 - h) * NJ)
                        nprev = cDVE.inc(v.tensor_copy(kv_sb[:, :], wres[:, h, 0:128]))
                        for j in range(1, NJ):
                            v.wait_ge(cDVE.sem, nprev)
                            nc_ = cDVE.inc(v.tensor_copy(wres[:, h, (j - 1) * 128:j * 128],
                                                         kv_sb[:, :]))
                            v.wait_ge(cDVE.sem, nc_)
                            n1 = cDVE.inc(v.tensor_single_scalar(kv_sb[:, :], kv_sb[:, :],
                                                                 bdj[:, h * 8 + 1:h * 8 + 2],
                                                                 ALU.mult))
                            v.wait_ge(cDVE.sem, n1)
                            nprev = cDVE.inc(v.tensor_tensor(kv_sb[:, :], kv_sb[:, :],
                                                             wres[:, h, j * 128:(j + 1) * 128],
                                                             ALU.add))
                        v.wait_ge(cDVE.sem, nprev)
                        cDVE.inc(v.tensor_copy(kv_all[:, h * 128:(h + 1) * 128], kv_sb[:, :]))
            p4_dve = cDVE.n

            with nc.Block() as blk:
                @blk.sync
                def _(sync):
                    sync.wait_ge(cDVE.sem, p4_dve)
                    sync.wait_ge(cCC.sem, 4)
                    cCI.inc(sync.dma_start(out=cc2_in[:, :], in_=kv_all[:, :]), 16)
                ci_val = cCI.n

                @blk.gpsimd
                def _(gp):
                    gp.wait_ge(cCI.sem, ci_val)
                    gp.wait_ge(cGA.sem, 4 * 16)
                    cCC.inc(gp.collective_compute(
                        "AllGather", ALU.bypass,
                        replica_groups=[[0, 1], [2, 3], [4, 5], [6, 7]],
                        ins=[cc2_in.ap().opt()], outs=[cc2_out.ap().opt()]))
                    gp.wait_ge(cCC.sem, cCC.n)
                    cGA.inc(gp.dma_start(out=kv_all[:, :], in_=cc2_out[0:128, :]), 16)
                ga_val = cGA.n

                @blk.vector
                def _(v):
                    v.wait_ge(cGA.sem, ga_val)
                    cDVE.inc(v.tensor_single_scalar(kv_all[:, :], kv_all[:, :],
                                                    flag_t[:, 0:1], ALU.mult))
            e3_dve = cDVE.n

            with nc.Block() as blk:
                v0, a0, pe0 = cDVE.n, cACT.n, cPE.n
                xl0 = [sXL[0].n, sXL[1].n]
                xb0 = [sXB[0].n, sXB[1].n]
                st0 = [sST[0].n, sST[1].n]
                prev_bank = list(bank_last)
                dve_fix, dve_msk, dve_ev = {}, {}, {}
                c = v0
                for s_ in range(NS):
                    c += (1 if s_ < NH else 2); dve_fix[s_] = c
                    c += 1; dve_msk[s_] = c
                    if s_ >= 1:
                        c += 1; dve_ev[s_ - 1] = c
                c += 1; dve_ev[NS - 1] = c
                b2b_dve_end = c
                pe_qk, pe_out = {}, {}
                c = pe0
                for s_ in range(NS):
                    c += 2; pe_qk[s_] = c
                    c += 1; pe_out[s_] = c
                b2b_pe_end = c

                @blk.sync
                def _(sync):
                    for j in range(NJ):
                        if j >= 1:
                            sync.wait_ge(cPE.sem, pe_out[j * NH - 1])
                        else:
                            sync.wait_ge(sSP[0].sem, p1_sp[0])
                            sync.wait_ge(sSP[1].sem, p1_sp[1])
                        sXL[j % 2].inc(sync.dma_start(
                            out=qs_stg[:, :, :].rearrange("p s d -> p (s d)"),
                            in_=bv_sp[j * 256:(j + 1) * 256, :]
                            .rearrange("(s p) d -> p s d", s=2)), 16)
                        for h in range(NH):
                            s_ = j * NH + h
                            if s_ >= 1:
                                sm1 = s_ - 1
                                hh, jj = sm1 % NH, sm1 // NH
                                pp = sm1 % 2
                                sync.wait_ge(cDVE.sem, dve_ev[sm1])
                                sST[pp].inc(sync.dma_start(
                                    out=v_sp[hh * 128:(hh + 1) * 128,
                                             jj * 256:(jj + 1) * 256],
                                    in_=ao_stage[:, pp, :, :]
                                    .rearrange("a b c -> a (b c)")), 16)
                    s_ = NS - 1
                    hh, jj = s_ % NH, s_ // NH
                    pp = s_ % 2
                    sync.wait_ge(cDVE.sem, dve_ev[s_])
                    sST[pp].inc(sync.dma_start(
                        out=v_sp[hh * 128:(hh + 1) * 128, jj * 256:(jj + 1) * 256],
                        in_=ao_stage[:, pp, :, :].rearrange("a b c -> a (b c)")), 16)

                @blk.scalar
                def _(s):
                    for s_ in range(NS):
                        h, j = s_ % NH, s_ // NH
                        p = s_ % 2
                        if s_ >= 2:
                            s.wait_ge(cPE.sem, pe_qk[s_ - 2])
                        sXB[p].inc(s.dma_start(
                            out=hb_kf[:, p, :],
                            in_=q_sp[j * 256:(j + 1) * 256, h * 128:(h + 1) * 128],
                            transpose=True), 16)

                @blk.vector
                def _(v):
                    v.wait_ge(cDVE.sem, e3_dve)
                    for s_ in range(NS):
                        h, j = s_ % NH, s_ // NH
                        p = s_ % 2
                        if s_ >= 1:
                            v.wait_ge(cDVE.sem, dve_fix[s_ - 1])
                        if s_ >= 2:
                            v.wait_ge(cPE.sem, pe_out[s_ - 2])
                        if j == 0:
                            cDVE.inc(v.tensor_single_scalar(kv_bf2[:, p, :],
                                                            kv_all[:, h * 128:(h + 1) * 128],
                                                            bdj[:, h * 8:h * 8 + 1], ALU.mult))
                        else:
                            n1 = cDVE.inc(v.tensor_single_scalar(
                                kv_sb[:, :], kv_all[:, h * 128:(h + 1) * 128],
                                bdj[:, h * 8 + j:h * 8 + j + 1], ALU.mult))
                            v.wait_ge(cDVE.sem, n1)
                            cDVE.inc(v.tensor_tensor(kv_bf2[:, p, :], kv_sb[:, :],
                                                     wres[:, h, (j - 1) * 128:j * 128],
                                                     ALU.add))
                        v.wait_ge(cPE.sem, pe_qk[s_])
                        cDVE.inc(v.tensor_tensor(mskd2[:, p, :], pbank[s_ % 4][:, 0:512],
                                                 ind_t[:, :], ALU.mult))
                        if s_ >= 1:
                            sm1 = s_ - 1
                            v.wait_ge(cPE.sem, pe_out[sm1])
                            if sm1 >= 2:
                                v.wait_ge(sST[sm1 % 2].sem,
                                          st0[sm1 % 2] + ((sm1 - 2) // 2 + 1) * 16)
                            cDVE.inc(v.tensor_copy(ao_stage[:, sm1 % 2, :, :]
                                                   .rearrange("a b c -> a (b c)"),
                                                   pbank[4 + sm1 % 3][:, 0:256]))
                    s_ = NS - 1
                    v.wait_ge(cPE.sem, pe_out[s_])
                    v.wait_ge(sST[s_ % 2].sem, st0[s_ % 2] + ((s_ - 2) // 2 + 1) * 16)
                    cDVE.inc(v.tensor_copy(ao_stage[:, s_ % 2, :, :]
                                           .rearrange("a b c -> a (b c)"),
                                           pbank[4 + s_ % 3][:, 0:256]))

                @blk.tensor
                def _(t):
                    t.wait_ge(sXB[0].sem, qs_xb[0])
                    t.wait_ge(sXB[1].sem, qs_xb[1])
                    for s_ in range(NS):
                        h, j = s_ % NH, s_ // NH
                        p = s_ % 2
                        t.wait_ge(sXB[p].sem, xb0[p] + (s_ // 2 + 1) * 16)
                        if s_ >= 4:
                            t.wait_ge(cDVE.sem, dve_msk[s_ - 4])
                        else:
                            bc, bval = prev_bank[s_ % 4]
                            t.wait_ge(bc.sem, bval)
                        for sub2 in range(2):
                            cPE.inc(t.matmul(pbank[s_ % 4][:, sub2 * 256:(sub2 + 1) * 256],
                                             hb_kf[:, p, sub2 * 128:(sub2 + 1) * 128],
                                             mvA[:, h, j * 256:(j + 1) * 256],
                                             start=True, stop=True))
                        t.wait_ge(cDVE.sem, dve_msk[s_])
                        t.wait_ge(sXL[j % 2].sem, xl0[j % 2] + (j // 2 + 1) * 16)
                        if s_ >= 3:
                            t.wait_ge(cDVE.sem, dve_ev[s_ - 3])
                        else:
                            bc, bval = prev_bank[4 + s_ % 3]
                            t.wait_ge(bc.sem, bval)
                        ob = pbank[4 + s_ % 3]
                        t.matmul(ob[:, 0:256], kv_bf2[:, p, :],
                                 mvA[:, h, j * 256:(j + 1) * 256], start=True, stop=False)
                        t.matmul(ob[:, 0:256], qs_stg[:, 0, h * 128:(h + 1) * 128],
                                 mskd2[:, p, 0:256], start=False, stop=False)
                        inst = t.matmul(ob[:, 0:256], qs_stg[:, 1, h * 128:(h + 1) * 128],
                                        mskd2[:, p, 256:512], start=False, stop=True)
                        cPE.inc(inst)
                for b_ in range(NB):
                    bank_last[b_] = (cDVE, b2b_dve_end)
            assert cDVE.n == b2b_dve_end, (cDVE.n, b2b_dve_end)
            assert cPE.n == b2b_pe_end, (cPE.n, b2b_pe_end)
            b2b_st = (sST[0].n, sST[1].n)

            with nc.Block() as blk:
                v0, a0 = cDVE.n, cACT.n
                xl0 = [sXL[0].n, sXL[1].n]
                rg0 = [sRG[0].n, sRG[1].n]
                st0 = [sST[0].n, sST[1].n]

                @blk.sync
                def _(sync):
                    sync.wait_ge(sST[0].sem, b2b_st[0])
                    sync.wait_ge(sST[1].sem, b2b_st[1])
                    for i in range(KC):
                        if i >= 2:
                            sync.wait_ge(cDVE.sem, v0 + (i - 2) * 4 + 4)
                        else:
                            sync.wait_ge(cDVE.sem, v0)
                        sXL[i % 2].inc(sync.dma_start(out=ch_re[:, i % 2, 0:D],
                                                      in_=v_sp[:, i * 128:(i + 1) * 128],
                                                      transpose=True), 16)
                        sRG[i % 2].inc(sync.dma_start(out=mvA[:, i % 2, 0:D],
                                                      in_=gate_sp[i * 128:(i + 1) * 128, :]), 16)
                        if i >= 1:
                            j = i - 1
                            sync.wait_ge(cDVE.sem, v0 + j * 4 + 4)
                            sST[j % 2].inc(sync.dma_start(out=gn_sp[j * 128:(j + 1) * 128, :],
                                                          in_=mvA[:, 2 + j % 2, 0:D]), 16)
                    j = KC - 1
                    sync.wait_ge(cDVE.sem, v0 + j * 4 + 4)
                    sST[j % 2].inc(sync.dma_start(out=gn_sp[j * 128:(j + 1) * 128, :],
                                                  in_=mvA[:, 2 + j % 2, 0:D]), 16)

                @blk.vector
                def _(v):
                    for i in range(KC):
                        pi = (i % 2) * 16
                        at = ch_re[:, i % 2, 0:D]
                        v.wait_ge(sXL[i % 2].sem, xl0[i % 2] + (i // 2 + 1) * 16)
                        v.wait_ge(sRG[i % 2].sem, rg0[i % 2] + (i // 2 + 1) * 16)
                        if i >= 2:
                            v.wait_ge(cACT.sem, a0 + (i - 2) * 2 + 2)
                            v.wait_ge(sST[i % 2].sem, st0[i % 2] + ((i - 2) // 2 + 1) * 16)
                        n1 = cDVE.inc(v.scalar_tensor_tensor(out_t2[:, i % 2, 0:D], at, 1.0, at,
                                                             ALU.mult, ALU.mult,
                                                             accum_out=stt[:, pi + 0:pi + 1]))
                        v.wait_ge(cDVE.sem, n1)
                        cDVE.inc(v.tensor_scalar_mul(stt[:, pi + 1:pi + 2], stt[:, pi + 0:pi + 1],
                                                     1.0 / D))
                        v.wait_ge(cACT.sem, a0 + i * 2 + 2)
                        n3 = cDVE.inc(v.scalar_tensor_tensor(out_t2[:, i % 2, 0:D], at,
                                                             stt[:, pi + 3:pi + 4], g_bc[:, :],
                                                             ALU.mult, ALU.mult))
                        v.wait_ge(cDVE.sem, n3)
                        cDVE.inc(v.tensor_tensor(mvA[:, 2 + i % 2, 0:D], out_t2[:, i % 2, 0:D],
                                                 mvA[:, i % 2, 0:D], ALU.mult))

                @blk.scalar
                def _(s):
                    for i in range(KC):
                        pi = (i % 2) * 16
                        s.wait_ge(cDVE.sem, v0 + i * 4 + 2)
                        na = cACT.inc(s.activation(stt[:, pi + 2:pi + 3], stt[:, pi + 1:pi + 2],
                                                   AF.Ln, bias=dk_t[:, 7:8]))
                        s.wait_ge(cACT.sem, na)
                        cACT.inc(s.activation(stt[:, pi + 3:pi + 4], stt[:, pi + 2:pi + 3],
                                              AF.Exp, scale=-0.5))
            b3_st = (sST[0].n, sST[1].n)

            with nc.Block() as blk:
                @blk.scalar
                def _(s):
                    s.wait_ge(sST[0].sem, b3_st[0])
                    s.wait_ge(sST[1].sem, b3_st[1])
                    for c2 in range(KC):
                        sXB[c2 % 2].inc(s.dma_start(out=mvA[:, c2, :],
                                                    in_=gn_sp[:, c2 * 128:(c2 + 1) * 128],
                                                    transpose=True), 16)
            gn_xb = [(sXB[0].sem, sXB[0].n), (sXB[1].sem, sXB[1].n)]

            emit_optA(wout_in, None, func=None, x_src=x_src, x_dst=x_dst,
                      store_out=store_out, extra_ready=gn_xb)

        emit_ffn(x1_sp, x2_sp, store_out=(N_SUB == 2))
        if N_SUB >= 3:
            emit_attn_b(x2_sp, x3_sp, store_out=(N_SUB == 3))
        if N_SUB >= 4:
            emit_ffn(x3_sp, x2_sp, store_out=True)

    return nc


def _chunked_vec(v):
    return np.ascontiguousarray(v.reshape(KC, 128).T).astype(np.float32)


def _tile_wT(w):
    wt = np.ascontiguousarray(w.T)
    x = wt.reshape(KC, 128, KC, 128)
    return np.ascontiguousarray(np.transpose(x, (2, 1, 0, 3))).reshape(KC, 128, KC * 128).astype(np.float32)


def prepare_in_maps(inputs):
    x_full = np.asarray(inputs["L0_attn_a_hidden_states"], dtype=np.float32)

    wq = _tile_wT(np.asarray(inputs["L0_attn_a_Wq"], dtype=np.float32))
    wk = _tile_wT(np.asarray(inputs["L0_attn_a_Wk"], dtype=np.float32))
    wv = _tile_wT(np.asarray(inputs["L0_attn_a_Wv"], dtype=np.float32))
    wqa = _tile_wT(np.asarray(inputs["L0_attn_a_Wqa"], dtype=np.float32))
    wka = _tile_wT(np.asarray(inputs["L0_attn_a_Wka"], dtype=np.float32))
    bq = _chunked_vec(np.asarray(inputs["L0_attn_a_bq"], dtype=np.float32))
    bk = _chunked_vec(np.asarray(inputs["L0_attn_a_bk"], dtype=np.float32))
    bv = _chunked_vec(np.asarray(inputs["L0_attn_a_bv"], dtype=np.float32))
    bqa = _chunked_vec(np.asarray(inputs["L0_attn_a_bqa"], dtype=np.float32))
    bka = _chunked_vec(np.asarray(inputs["L0_attn_a_bka"], dtype=np.float32))
    g_row = np.asarray(inputs["L0_attn_a_ln_g"], dtype=np.float32).reshape(1, D)
    b_row = np.asarray(inputs["L0_attn_a_ln_b"], dtype=np.float32).reshape(1, D)
    ones_l = np.ones((1, 128), dtype=ml_dtypes.bfloat16)
    idn = np.eye(128, dtype=np.float32).astype(ml_dtypes.bfloat16)

    wbase_T = np.ascontiguousarray(np.asarray(inputs["L0_ffn_W_base"], dtype=np.float32).T)
    wbase_t = np.ascontiguousarray(
        np.transpose(wbase_T.reshape(KC, 128, 2, 1024), (2, 1, 0, 3))).reshape(2, 128, KC * 1024)
    rt = np.transpose(np.ascontiguousarray(
        np.asarray(inputs["L0_ffn_W_router"], dtype=np.float32).T).reshape(KC, 128, 8), (1, 0, 2))
    a_flat = np.asarray(inputs["L0_ffn_A_w"], dtype=np.float32).reshape(64, D)
    at = np.transpose(np.ascontiguousarray(a_flat.T).reshape(KC, 128, 64), (1, 0, 2))
    rat = np.ascontiguousarray(np.concatenate([rt, at], axis=2)).astype(np.float32)
    bt = np.ascontiguousarray(
        np.transpose(np.asarray(inputs["L0_ffn_B_w"], dtype=np.float32), (0, 2, 1)).reshape(64, D))
    bb = np.asarray(inputs["L0_ffn_b_base"], dtype=np.float32).reshape(1, D)

    def _tile_optA(w):
        wt = np.ascontiguousarray(np.asarray(w, dtype=np.float32).T)
        return np.ascontiguousarray(
            np.transpose(wt.reshape(KC, 128, 2, 1024), (2, 1, 0, 3))).reshape(2, 128, KC * 1024)

    wqkv = np.asarray(inputs["L0_attn_b_w_qkv"], dtype=np.float32)
    qkvw_t = np.stack([_tile_optA(wqkv[s * D:(s + 1) * D, :]) for s in range(3)])
    wgate_t = _tile_optA(inputs["L0_attn_b_w_gate"])
    wout_t = _tile_optA(inputs["L0_attn_b_w_out"])
    slope_col = np.ascontiguousarray(np.repeat(
        np.asarray(inputs["L0_attn_b_slope_rate"], dtype=np.float32).reshape(1, 16), 128, axis=0))
    tloc = (np.arange(2)[None, :] * 128 + np.arange(128)[:, None]).astype(np.float32)
    iota1 = np.ascontiguousarray(tloc + 1.0)
    iota2 = np.ascontiguousarray(255.0 - tloc)
    c256 = np.full((128, 1), 256.0, dtype=np.float32)
    iota_bj = np.ascontiguousarray(
        np.repeat((np.arange(8)[None, :] * 256.0).astype(np.float32), 128, axis=0))
    t2i = (np.arange(2)[:, None, None] * 128 + np.arange(128)[None, :, None])
    t1i = np.arange(256)[None, None, :]
    ind_c = np.ascontiguousarray(
        np.transpose((t1i >= t2i).astype(np.float32), (1, 0, 2)).reshape(128, 512)
    ).astype(ml_dtypes.bfloat16)
    t2g = (np.arange(2)[:, None, None] * 128 + np.arange(128)[None, :, None]).astype(np.float32)
    t1g = np.arange(256)[None, None, :].astype(np.float32)
    cmm = np.where(t1g >= t2g, t2g + 1.0, -1e30).astype(np.float32)
    cm = np.ascontiguousarray(np.transpose(cmm, (1, 0, 2)).reshape(128, 512))
    nw = np.asarray(inputs["L0_attn_b_norm_w"], dtype=np.float32).reshape(1, D)

    in_maps = []
    for c in range(NCORES):
        b, half = c // 2, c % 2
        flag = np.full((128, 1), float(half), dtype=np.float32)
        x_sl = np.ascontiguousarray(x_full[b, half * T:(half + 1) * T, :])
        in_maps.append(dict(
            x_in=x_sl, flag=flag, ones_lhsT=ones_l,
            wq_t=wq, wk_t=wk, wv_t=wv, wqa_t=wqa, wka_t=wka,
            bq_c=bq, bk_c=bk, bv_c=bv, bqa_c=bqa, bka_c=bka,
            g_row=g_row, b_row=b_row,
            wbase_t=wbase_t, rat=rat, bt=bt, bb_in=bb, idn=idn,
            qkvw_t=qkvw_t, wgate_t=wgate_t, wout_t=wout_t, slope_col=slope_col,
            iota1=iota1, iota2=iota2, c256=c256, cm=cm, iota_bj=iota_bj, ind_c=ind_c,
            nw_in_row=nw,
        ))
    return in_maps


def kernel(**inputs):
    in_maps = prepare_in_maps(inputs)
    nc = build_program()
    res = run_bass_kernel_spmd(nc, in_maps, core_ids=list(range(NCORES)))
    global LAST_RESULTS
    LAST_RESULTS = res.results
    out = np.empty((4, 4096, D), dtype=np.float32)
    for c in range(NCORES):
        b, half = c // 2, c % 2
        out[b, half * T:(half + 1) * T, :] = np.asarray(res.results[c]["out"], dtype=np.float32)
    return out


# version 11
# speedup vs baseline: 1.0720x; 1.0300x over previous
import numpy as np
import ml_dtypes
from contextlib import ExitStack

import concourse.bass as bass
import concourse.mybir as mybir
from concourse.bass_utils import run_bass_kernel_spmd

F32 = mybir.dt.float32
BF16 = mybir.dt.bfloat16
AF = mybir.ActivationFunctionType
ALU = mybir.AluOpType

NCORES = 8
T = 2048
D = 2048
KC = 16
TBW = 512
NTB = T // TBW
NG = KC * NTB
LN_EPS = 1e-5
ATT_SCALE = 1.0 / float(np.sqrt(np.float32(2048.0)))

DEBUG_TAPS = False
LAST_RESULTS = None
RACE_DETECT = False
N_SUB = 4


class Ctr:
    def __init__(self, sem):
        self.sem = sem
        self.n = 0

    def inc(self, inst, k=1):
        inst.then_inc(self.sem, k)
        self.n += k
        return self.n


def build_program():
    nc = bass.Bass("TRN2", target_bir_lowering=False, detect_race_conditions=RACE_DETECT)
    tap = "ExternalOutput" if DEBUG_TAPS else "Internal"

    x_in = nc.dram_tensor("x_in", [T, D], F32, kind="ExternalInput").ap()
    flag_in = nc.dram_tensor("flag", [128, 1], F32, kind="ExternalInput").ap()
    ones_in = nc.dram_tensor("ones_lhsT", [1, 128], BF16, kind="ExternalInput").ap()
    wq_in = nc.dram_tensor("wq_t", [KC, 128, D], F32, kind="ExternalInput").ap()
    wk_in = nc.dram_tensor("wk_t", [KC, 128, D], F32, kind="ExternalInput").ap()
    wv_in = nc.dram_tensor("wv_t", [KC, 128, D], F32, kind="ExternalInput").ap()
    wqa_in = nc.dram_tensor("wqa_t", [KC, 128, D], F32, kind="ExternalInput").ap()
    wka_in = nc.dram_tensor("wka_t", [KC, 128, D], F32, kind="ExternalInput").ap()
    bq_in = nc.dram_tensor("bq_c", [128, KC], F32, kind="ExternalInput").ap()
    bk_in = nc.dram_tensor("bk_c", [128, KC], F32, kind="ExternalInput").ap()
    bv_in = nc.dram_tensor("bv_c", [128, KC], F32, kind="ExternalInput").ap()
    bqa_in = nc.dram_tensor("bqa_c", [128, KC], F32, kind="ExternalInput").ap()
    bka_in = nc.dram_tensor("bka_c", [128, KC], F32, kind="ExternalInput").ap()
    g_in = nc.dram_tensor("g_row", [1, D], F32, kind="ExternalInput").ap()
    bln_in = nc.dram_tensor("b_row", [1, D], F32, kind="ExternalInput").ap()
    wbase_in = nc.dram_tensor("wbase_t", [2, 128, KC * 1024], F32, kind="ExternalInput").ap()
    rat_in = nc.dram_tensor("rat", [128, KC, 72], F32, kind="ExternalInput").ap()
    bt_in = nc.dram_tensor("bt", [64, D], F32, kind="ExternalInput").ap()
    bb_in = nc.dram_tensor("bb_in", [1, D], F32, kind="ExternalInput").ap()
    idn_in = nc.dram_tensor("idn", [128, 128], BF16, kind="ExternalInput").ap()
    qkvw_in = nc.dram_tensor("qkvw_t", [3, 2, 128, KC * 1024], F32, kind="ExternalInput").ap()
    wgate_in = nc.dram_tensor("wgate_t", [2, 128, KC * 1024], F32, kind="ExternalInput").ap()
    wout_in = nc.dram_tensor("wout_t", [2, 128, KC * 1024], F32, kind="ExternalInput").ap()
    slope_in = nc.dram_tensor("slope_col", [128, 16], F32, kind="ExternalInput").ap()
    iota1_in = nc.dram_tensor("iota1", [128, 2], F32, kind="ExternalInput").ap()
    iota2_in = nc.dram_tensor("iota2", [128, 2], F32, kind="ExternalInput").ap()
    c256_in = nc.dram_tensor("c256", [128, 1], F32, kind="ExternalInput").ap()
    cm_in = nc.dram_tensor("cm", [128, 2 * 256], F32, kind="ExternalInput").ap()
    iotabj_in = nc.dram_tensor("iota_bj", [128, 8], F32, kind="ExternalInput").ap()
    ind_in = nc.dram_tensor("ind_c", [128, 512], BF16, kind="ExternalInput").ap()
    nw_in = nc.dram_tensor("nw_in_row", [1, D], F32, kind="ExternalInput").ap()

    out_ap = nc.dram_tensor("out", [T, D], F32, kind="ExternalOutput").ap()

    u_sp = nc.dram_tensor("u_sp", [T, D], BF16, kind=tap).ap()
    q_sp = nc.dram_tensor("q_sp", [D, T], BF16, kind=tap).ap()
    k_sp = nc.dram_tensor("k_sp", [D, T], BF16, kind=tap).ap()
    v_sp = nc.dram_tensor("v_sp", [D, T], BF16, kind=tap).ap()
    w_sp = nc.dram_tensor("w_sp", [D, T], BF16, kind=tap).ap()
    hi_sp = nc.dram_tensor("hi_sp", [D, T], BF16, kind=tap).ap()
    lo_sp = nc.dram_tensor("lo_sp", [D, T], BF16, kind=tap).ap()
    x1_sp = nc.dram_tensor("x1_sp", [T, D], F32, kind="Internal").ap()
    x2_sp = nc.dram_tensor("x2_sp", [T, D], F32, kind="Internal").ap()
    x3_sp = nc.dram_tensor("x3_sp", [T, D], F32, kind="Internal").ap()
    dbg1 = nc.dram_tensor("dbg1", [128, D], BF16, kind=tap).ap()
    dbg2 = nc.dram_tensor("dbg2", [128, D], BF16, kind=tap).ap()
    dbg3 = nc.dram_tensor("dbg3", [128, 32], F32, kind=tap).ap()
    dbg4 = nc.dram_tensor("dbg4", [128, D], F32, kind=tap).ap()
    dbg5 = nc.dram_tensor("dbg5", [128, D], F32, kind=tap).ap()
    cc_in = nc.dram_tensor("cc_in", [128, KC], F32, kind="Internal")
    cc_out = nc.dram_tensor("cc_out", [256, KC], F32, kind="Internal")
    bq_sp = nc.dram_tensor("bq_sp", [T, D], BF16, kind="Internal").ap()
    bk_sp = nc.dram_tensor("bk_sp", [T, D], BF16, kind="Internal").ap()
    bv_sp = nc.dram_tensor("bv_sp", [T, D], BF16, kind="Internal").ap()
    gate_sp = nc.dram_tensor("gate_sp", [T, D], BF16, kind="Internal").ap()
    ao_sp = nc.dram_tensor("ao_sp", [T, D], BF16, kind="Internal").ap()
    gn_sp = nc.dram_tensor("gn_sp", [T, D], BF16, kind="Internal").ap()
    cc2_in = nc.dram_tensor("cc2_in", [128, D], F32, kind="Internal")
    cc2_out = nc.dram_tensor("cc2_out", [256, D], F32, kind="Internal")

    with ExitStack() as stk:
        e = stk.enter_context

        cPE = Ctr(e(nc.semaphore("pe")))
        cACT = Ctr(e(nc.semaphore("act")))
        cDVE = Ctr(e(nc.semaphore("dve")))
        cPOOL = Ctr(e(nc.semaphore("pool")))
        cCC = Ctr(e(nc.semaphore("cc")))
        cCI = Ctr(e(nc.semaphore("cc_in_st")))
        cGA = Ctr(e(nc.semaphore("gath_ld")))
        sIN = Ctr(e(nc.semaphore("s_init")))
        sXL = [Ctr(e(nc.semaphore(f"s_xl{i}"))) for i in range(2)]
        sST = [Ctr(e(nc.semaphore(f"s_st{i}"))) for i in range(2)]
        sWS = [Ctr(e(nc.semaphore(f"s_ws{i}"))) for i in range(2)]
        sRG = [Ctr(e(nc.semaphore(f"s_rg{i}"))) for i in range(2)]
        sCH = [Ctr(e(nc.semaphore(f"s_ch{i}"))) for i in range(2)]
        sSP = [Ctr(e(nc.semaphore(f"s_sp{i}"))) for i in range(2)]
        sXB = [Ctr(e(nc.semaphore(f"s_xb{i}"))) for i in range(2)]
        cQR = Ctr(e(nc.semaphore("s_qr")))
        sING = Ctr(e(nc.semaphore("s_init_g")))

        flag_t = e(nc.sbuf_tensor("flag_t", [128, 1], F32))
        ones_t = e(nc.sbuf_tensor("ones_t", [1, 128], BF16))
        bias_c = e(nc.sbuf_tensor("bias_c", [128, 5 * KC], F32))
        g_bc = e(nc.sbuf_tensor("g_bc", [128, D], BF16))
        b_bc = e(nc.sbuf_tensor("b_bc", [128, D], BF16))
        zeros_bf = e(nc.sbuf_tensor("zeros_bf", [128, D], BF16))
        stt = e(nc.sbuf_tensor("stt", [128, 32], F32))
        colA = e(nc.sbuf_tensor("colA", [128, KC, NTB], F32))
        colS = e(nc.sbuf_tensor("colS", [128, KC], F32))
        gath = e(nc.sbuf_tensor("gath", [128, KC], F32))
        ceff = e(nc.sbuf_tensor("ceff", [128, 4 * KC], F32))
        eps_t = e(nc.sbuf_tensor("eps_t", [128, 1], F32))
        eps2_t = e(nc.sbuf_tensor("eps2_t", [128, 1], F32))

        x_t2 = e(nc.sbuf_tensor("x_t2", [128, 2, D], F32))
        mvA = e(nc.sbuf_tensor("mvA", [128, KC, T], BF16))
        ch_re = e(nc.sbuf_tensor("ch_re", [128, 2, T], BF16))
        colT = e(nc.sbuf_tensor("colT", [128, KC], F32))
        scr1 = e(nc.sbuf_tensor("scr1", [128, 1], F32))
        out_t2 = e(nc.sbuf_tensor("out_t2", [128, 2, T], F32))
        astk = ExitStack()
        ae = astk.enter_context
        u1_scr = ae(nc.sbuf_tensor("u1_scr", [128, D], F32))
        h1_scr = ae(nc.sbuf_tensor("h1_scr", [128, 2, D], BF16))
        wstrip = ae(nc.sbuf_tensor("wstrip", [128, 2, KC, 128], BF16))
        kv_stage = ae(nc.sbuf_tensor("kv_stage", [128, 2, TBW], BF16))
        ringA = ae(nc.sbuf_tensor("ringA", [128, 2, T], BF16))
        tmp_f = ae(nc.sbuf_tensor("tmp_f", [128, 2, T], F32))
        hi_st = ae(nc.sbuf_tensor("hi_st", [128, 2, T], BF16))
        lo_st = ae(nc.sbuf_tensor("lo_st", [128, 2, T], BF16))
        hi_tm = ae(nc.sbuf_tensor("hi_tm", [128, 2, D], BF16))
        lo_tm = ae(nc.sbuf_tensor("lo_tm", [128, 2, D], BF16))

        pbank = [e(nc.psum_tensor(f"pb{i}", [128, TBW], F32)) for i in range(7)]
        pbank_bf = e(nc.psum_tensor("pbbf", [128, 1024], BF16))
        NB = 7
        bank_last = [(cACT, 0)] * NB
        bf_last = [cACT, 0]
        wres_pe = {"n": 0}

        all_ctrs = ([cPE, cACT, cDVE, cPOOL, cCC, cCI, cGA, sIN, cQR, sING]
                    + sXL + sST + sWS + sRG + sCH + sSP + sXB)
        with nc.Block() as blk:
            @blk.gpsimd
            def _(gp):
                for ctr in all_ctrs:
                    gp.sem_clear(ctr.sem)
        nc.all_engine_barrier()

        with ExitStack() as istk:
            grow_b = istk.enter_context(nc.sbuf_tensor("grow_b", [1, D], BF16))
            brow_b = istk.enter_context(nc.sbuf_tensor("brow_b", [1, D], BF16))

            with nc.Block() as blk:
                @blk.sync
                def _(sync):
                    for dst, src in ((flag_t[:, :], flag_in), (ones_t[:, :], ones_in),
                                     (bias_c[:, 0:KC], bq_in), (bias_c[:, KC:2 * KC], bk_in),
                                     (bias_c[:, 2 * KC:3 * KC], bv_in),
                                     (bias_c[:, 3 * KC:4 * KC], bqa_in),
                                     (bias_c[:, 4 * KC:5 * KC], bka_in)):
                        sIN.inc(sync.dma_start(out=dst, in_=src), 16)

                @blk.gpsimd
                def _(gp):
                    sING.inc(gp.dma_start(out=grow_b[:, :], in_=g_in), 16)
                    sING.inc(gp.dma_start(out=brow_b[:, :], in_=bln_in), 16)
                init_in = sIN.n
                init_ing = sING.n

                @blk.vector
                def _(v):
                    v.memset(zeros_bf[:, :], 0.0)
                    v.memset(eps_t[:, :], LN_EPS)
                    v.memset(eps2_t[:, :], LN_EPS * LN_EPS)
                    v.wait_ge(sIN.sem, init_in)
                    v.wait_ge(sING.sem, init_ing)
                    v.tensor_scalar_mul(bias_c[:, 3 * KC:4 * KC], bias_c[:, 3 * KC:4 * KC], ATT_SCALE)
                    cDVE.inc(v.tensor_scalar_mul(bias_c[:, 4 * KC:5 * KC], bias_c[:, 4 * KC:5 * KC],
                                                 ATT_SCALE))

                @blk.tensor
                def _(t):
                    t.wait_ge(sIN.sem, init_in)
                    t.wait_ge(sING.sem, init_ing)
                    inst = None
                    for j in range(NTB):
                        inst = t.matmul(pbank[j][:, :], ones_t[0:1, :],
                                        grow_b[0:1, j * TBW:(j + 1) * TBW], start=True, stop=True)
                    pe_g = cPE.inc(inst)
                    t.wait_ge(cACT.sem, cACT.n + 1)
                    inst = None
                    for j in range(NTB):
                        inst = t.matmul(pbank[j][:, :], ones_t[0:1, :],
                                        brow_b[0:1, j * TBW:(j + 1) * TBW], start=True, stop=True)
                    cPE.inc(inst)

                @blk.scalar
                def _(s):
                    s.wait_ge(cPE.sem, cPE.n - 1)
                    inst = None
                    for j in range(NTB):
                        inst = s.copy(g_bc[:, j * TBW:(j + 1) * TBW], pbank[j][:, :])
                    cACT.inc(inst)
                    s.wait_ge(cPE.sem, cPE.n)
                    inst = None
                    for j in range(NTB):
                        inst = s.copy(b_bc[:, j * TBW:(j + 1) * TBW], pbank[j][:, :])
                    cACT.inc(inst)
                for b_ in range(NB):
                    bank_last[b_] = (cACT, cACT.n)

        if DEBUG_TAPS:
            with nc.Block() as blk:
                @blk.sync
                def _(sync):
                    sync.wait_ge(cACT.sem, cACT.n)
                    sIN.inc(sync.dma_start(out=dbg1, in_=g_bc[:, :]), 16)
                    sIN.inc(sync.dma_start(out=dbg2, in_=b_bc[:, :]), 16)

        with nc.Block() as blk:
            v0, a0, p0 = cDVE.n, cACT.n, cPOOL.n
            xl0 = [sXL[0].n, sXL[1].n]
            st0 = [sST[0].n, sST[1].n]

            @blk.sync
            def _(sync):
                def store(j):
                    sync.wait_ge(cPOOL.sem, p0 + j + 1)
                    sST[j % 2].inc(sync.dma_start(out=u_sp[j * 128:(j + 1) * 128, :],
                                                  in_=h1_scr[:, j % 2]), 16)
                for i in range(KC):
                    if i >= 2:
                        sync.wait_ge(cDVE.sem, v0 + (i - 2) * 9 + 8)
                    sXL[i % 2].inc(sync.dma_start(out=x_t2[:, i % 2],
                                                  in_=x_in[i * 128:(i + 1) * 128, :]), 16)
                    if i >= 1:
                        store(i - 1)
                store(KC - 1)

            @blk.vector
            def _(v):
                for i in range(KC):
                    pi = (i % 2) * 16
                    xt = x_t2[:, i % 2]
                    hs = h1_scr[:, i % 2]
                    v.wait_ge(sXL[i % 2].sem, xl0[i % 2] + (i // 2 + 1) * 16)
                    if i >= 2:
                        v.wait_ge(cACT.sem, a0 + (i - 2) * 2 + 2)
                        v.wait_ge(cPOOL.sem, p0 + i - 1)
                        v.wait_ge(sST[i % 2].sem, st0[i % 2] + ((i - 2) // 2 + 1) * 16)
                    n = {}
                    def op(key, inst):
                        n[key] = cDVE.inc(inst)
                    op(1, v.tensor_scalar(hs, xt, 1.0, 0.0, ALU.mult, ALU.add,
                                          accum_out=stt[:, pi + 0:pi + 1]))
                    op(2, v.scalar_tensor_tensor(ch_re[:, 0, 0:D], xt, 1.0, xt, ALU.mult, ALU.mult,
                                                 accum_out=stt[:, pi + 1:pi + 2]))
                    v.wait_ge(cDVE.sem, n[1])
                    op(3, v.tensor_scalar_mul(stt[:, pi + 2:pi + 3], stt[:, pi + 0:pi + 1], 1.0 / D))
                    v.wait_ge(cDVE.sem, n[2])
                    op(4, v.tensor_scalar_mul(stt[:, pi + 3:pi + 4], stt[:, pi + 1:pi + 2], 1.0 / D))
                    v.wait_ge(cDVE.sem, n[3])
                    op(5, v.tensor_tensor(stt[:, pi + 4:pi + 5], stt[:, pi + 2:pi + 3],
                                          stt[:, pi + 2:pi + 3], ALU.mult))
                    v.wait_ge(cDVE.sem, n[5])
                    op(6, v.tensor_tensor(stt[:, pi + 5:pi + 6], stt[:, pi + 3:pi + 4],
                                          stt[:, pi + 4:pi + 5], ALU.subtract))
                    v.wait_ge(cDVE.sem, n[6])
                    op(7, v.tensor_scalar_mul(stt[:, pi + 8:pi + 9], stt[:, pi + 5:pi + 6],
                                              1.0 + LN_EPS))
                    v.wait_ge(cACT.sem, a0 + i * 2 + 2)
                    op(8, v.tensor_scalar(u1_scr[:, :], xt, stt[:, pi + 2:pi + 3], stt[:, pi + 6:pi + 7],
                                          ALU.subtract, ALU.mult))
                    v.wait_ge(cDVE.sem, n[8])
                    op(9, v.scalar_tensor_tensor(hs, u1_scr[:, :], 1.0, g_bc[:, :],
                                                 ALU.mult, ALU.mult))

            @blk.scalar
            def _(s):
                for i in range(KC):
                    pi = (i % 2) * 16
                    s.wait_ge(cDVE.sem, v0 + i * 9 + 7)
                    na = cACT.inc(s.activation(stt[:, pi + 7:pi + 8], stt[:, pi + 8:pi + 9], AF.Ln,
                                               bias=eps2_t[:, 0:1]))
                    s.wait_ge(cACT.sem, na)
                    cACT.inc(s.activation(stt[:, pi + 6:pi + 7], stt[:, pi + 7:pi + 8], AF.Exp, scale=-0.5))

            @blk.gpsimd
            def _(gp):
                for i in range(KC):
                    gp.wait_ge(cDVE.sem, v0 + i * 9 + 9)
                    if i >= 2:
                        gp.wait_ge(sST[i % 2].sem, st0[i % 2] + ((i - 2) // 2 + 1) * 16)
                    cPOOL.inc(gp.tensor_tensor(h1_scr[:, i % 2], h1_scr[:, i % 2], b_bc[:, :], ALU.add))

        u_stores = (sST[0].n, sST[1].n)

        def proj_B(w_dram, mov, evac, mov_ready=(), extra_stage_wait=None, post_sync=None):
            pe0, act0 = cPE.n, cACT.n
            ws0 = [sWS[0].n, sWS[1].n]
            prev_bank = list(bank_last)

            with nc.Block() as blk:
                @blk.gpsimd
                def _(gp):
                    for oc in range(KC):
                        if oc >= 2:
                            gp.wait_ge(cPE.sem, pe0 + (oc - 1) * NTB)
                        else:
                            gp.wait_ge(cPE.sem, pe0)
                        sWS[oc % 2].inc(gp.dma_start(
                            out=wstrip[:, oc % 2, :, :].rearrange("p k o -> p (k o)"),
                            in_=w_dram[oc]), 16)

                @blk.tensor
                def _(t):
                    for sem, val in mov_ready:
                        t.wait_ge(sem, val)
                    for oc in range(KC):
                        t.wait_ge(sWS[oc % 2].sem, ws0[oc % 2] + (oc // 2 + 1) * 16)
                        for tb in range(NTB):
                            g = oc * NTB + tb
                            if g >= NB:
                                t.wait_ge(cACT.sem, act0 + g - NB + 1)
                            else:
                                bc, bval = prev_bank[g % NB]
                                t.wait_ge(bc.sem, bval)
                            inst = None
                            for k in range(KC):
                                inst = t.matmul(pbank[g % NB][:, :], wstrip[:, oc % 2, k, :],
                                                mov[:, k, tb * TBW:(tb + 1) * TBW],
                                                start=(k == 0), stop=(k == KC - 1))
                            cPE.inc(inst)

                @blk.scalar
                def _(s):
                    for oc in range(KC):
                        for tb in range(NTB):
                            g = oc * NTB + tb
                            s.wait_ge(cPE.sem, pe0 + g + 1)
                            if extra_stage_wait is not None:
                                extra_stage_wait(s, g)
                            cACT.inc(evac(s, oc, tb, pbank[g % NB]))
                            bank_last[g % NB] = (cACT, cACT.n)

                if post_sync is not None:
                    @blk.sync
                    def _(sync):
                        post_sync(sync)

        def proj_spill(w_dram, mov, spill, bias_off, func, scale, accum, mov_ready=()):
            act0_local = cACT.n
            sp0 = [sSP[0].n, sSP[1].n]

            def evac(s, oc, tb, bank):
                g = oc * NTB + tb
                kw = {}
                if accum:
                    kw["accum_out"] = colA[:, oc, tb:tb + 1]
                return s.activation(kv_stage[:, g % 2], bank[:, :], func,
                                    bias=bias_c[:, bias_off + oc:bias_off + oc + 1],
                                    scale=scale, **kw)

            def stage_wait(s, g):
                if g >= 2:
                    s.wait_ge(sSP[g % 2].sem, sp0[g % 2] + ((g - 2) // 2 + 1) * 16)
                else:
                    s.wait_ge(sSP[g % 2].sem, sp0[g % 2])

            def post_sync(sync):
                for oc in range(KC):
                    for tb in range(NTB):
                        g = oc * NTB + tb
                        sync.wait_ge(cACT.sem, act0_local + g + 1)
                        sSP[g % 2].inc(sync.dma_start(
                            out=spill[oc * 128:(oc + 1) * 128, tb * TBW:(tb + 1) * TBW],
                            in_=kv_stage[:, g % 2]), 16)

            proj_B(w_dram, mov, evac, mov_ready=mov_ready,
                   extra_stage_wait=stage_wait, post_sync=post_sync)
            return (sSP[0].n, sSP[1].n)

        with nc.Block() as blk:
            @blk.scalar
            def _(s):
                s.wait_ge(sST[0].sem, u_stores[0])
                s.wait_ge(sST[1].sem, u_stores[1])
                for c in range(KC):
                    sXB[c % 2].inc(s.dma_start(out=mvA[:, c, :],
                                               in_=u_sp[:, c * 128:(c + 1) * 128], transpose=True), 16)
        h_ready = (sXB[0].n, sXB[1].n)

        hrdy = [(sXB[0].sem, h_ready[0]), (sXB[1].sem, h_ready[1])]
        q_stores = proj_spill(wq_in, mvA, q_sp, 0, AF.Identity, 1.0, False, mov_ready=hrdy)
        k_stores = proj_spill(wk_in, mvA, k_sp, KC, AF.Identity, 1.0, False)
        v_stores = proj_spill(wv_in, mvA, v_sp, 2 * KC, AF.Identity, 1.0, False)
        v_pe_done = cPE.n

        with nc.Block() as blk:
            @blk.sync
            def _(sync):
                sync.wait_ge(cPE.sem, v_pe_done)
                sync.wait_ge(sSP[0].sem, sSP[0].n)
                sync.wait_ge(sSP[1].sem, sSP[1].n)
                for c in range(KC):
                    cQR.inc(sync.dma_start(out=mvA[:, c, :], in_=q_sp[c * 128:(c + 1) * 128, :]), 16)
        q_loaded = cQR.n

        qw_stores = proj_spill(wqa_in, mvA, w_sp, 3 * KC, AF.Exp, ATT_SCALE, True,
                               mov_ready=[(cQR.sem, q_loaded)])
        a2_act = cACT.n

        def exchange(slot, wait_sem_val, from_colA=True):
            with nc.Block() as blk:
                if from_colA:
                    @blk.vector
                    def _(v):
                        v.wait_ge(*wait_sem_val)
                        cDVE.inc(v.tensor_tensor(colS[:, :], colA[:, :, 0], colA[:, :, 1], ALU.add))
                        nx_ = cDVE.inc(v.tensor_tensor(colT[:, :], colA[:, :, 2], colA[:, :, 3], ALU.add))
                        v.wait_ge(cDVE.sem, nx_)
                        cDVE.inc(v.tensor_tensor(colS[:, :], colS[:, :], colT[:, :], ALU.add))
                dve_sum = cDVE.n

                @blk.sync
                def _(sync):
                    if from_colA:
                        sync.wait_ge(cDVE.sem, dve_sum)
                    else:
                        sync.wait_ge(*wait_sem_val)
                    if slot > 0:
                        sync.wait_ge(cCC.sem, slot)
                    cCI.inc(sync.dma_start(out=cc_in[:, :], in_=colS[:, :]), 16)
                ci_val = cCI.n

                @blk.gpsimd
                def _(gp):
                    gp.wait_ge(cCI.sem, ci_val)
                    if slot > 0:
                        gp.wait_ge(cGA.sem, slot * 16)
                    cCC.inc(gp.collective_compute(
                        "AllGather", ALU.bypass,
                        replica_groups=[[0, 1], [2, 3], [4, 5], [6, 7]],
                        ins=[cc_in.ap().opt()], outs=[cc_out.ap().opt()]))
                    gp.wait_ge(cCC.sem, cCC.n)
                    cGA.inc(gp.dma_start(out=gath[:, :], in_=cc_out[0:128, :]), 16)
                ga_val = cGA.n

                @blk.vector
                def _(v):
                    v.wait_ge(cGA.sem, ga_val)
                    cDVE.inc(v.tensor_scalar_mul(ceff[:, slot * KC:(slot + 1) * KC],
                                                 gath[:, :], flag_t[:, 0:1]))
            return cDVE.n

        e1a_dve = exchange(0, (cACT.sem, a2_act))

        def scan_pass(src_sp, src_stores, ceff_slot, start_dve):
            with nc.Block() as blk:
                v0 = cDVE.n
                rg0 = [sRG[0].n, sRG[1].n]

                sp_tot = (sSP[0].n, sSP[1].n)

                @blk.sync
                def _(sync):
                    sync.wait_ge(sSP[0].sem, sp_tot[0])
                    sync.wait_ge(sSP[1].sem, sp_tot[1])
                    for c in range(KC):
                        p = c // 2
                        if p >= 1:
                            sync.wait_ge(cDVE.sem, v0 + p * 8)
                        sRG[c % 2].inc(sync.dma_start(out=ringA[:, c % 2],
                                                      in_=src_sp[c * 128:(c + 1) * 128, :]), 16)

                @blk.vector
                def _(v):
                    v.wait_ge(cDVE.sem, start_dve)
                    v.engine_nop()
                    prev_mul = {}
                    for cp in range(0, KC, 2):
                        ns = {}
                        for c in (cp, cp + 1):
                            v.wait_ge(sRG[c % 2].sem, rg0[c % 2] + (c // 2 + 1) * 16)
                            if c % 2 in prev_mul:
                                v.wait_ge(cDVE.sem, prev_mul[c % 2])
                            ns[("s", c)] = cDVE.inc(v.tensor_tensor_scan(
                                tmp_f[:, c % 2], ringA[:, c % 2], zeros_bf[:, :],
                                ceff[:, ceff_slot * KC + c:ceff_slot * KC + c + 1], ALU.add, ALU.add))
                        for c in (cp, cp + 1):
                            v.wait_ge(cDVE.sem, ns[("s", c)])
                            ns[("r", c)] = cDVE.inc(v.reciprocal(tmp_f[:, c % 2], tmp_f[:, c % 2]))
                        for c in (cp, cp + 1):
                            v.wait_ge(cDVE.sem, ns[("r", c)])
                            ns[("m", c)] = cDVE.inc(v.tensor_tensor(ringA[:, c % 2], ringA[:, c % 2],
                                                                    tmp_f[:, c % 2], ALU.mult))
                            prev_mul[c % 2] = ns[("m", c)]
                        for c in (cp, cp + 1):
                            v.wait_ge(cDVE.sem, ns[("m", c)])
                            cDVE.inc(v.scalar_tensor_tensor(mvA[:, c, :], ringA[:, c % 2], 1.0, mvA[:, c, :],
                                                            ALU.mult, ALU.mult, accum_out=colS[:, c:c + 1]))
            return cDVE.n

        a3_dve = scan_pass(w_sp, qw_stores, 0, e1a_dve)

        e1b_dve = exchange(1, (cDVE.sem, a3_dve), from_colA=False)

        def scan_mul_pass(src_sp, src_stores, ceff_slot, start_dve):
            with nc.Block() as blk:
                v0 = cDVE.n
                ch0 = [sCH[0].n, sCH[1].n]

                sp_tot = (sSP[0].n, sSP[1].n)

                @blk.sync
                def _(sync):
                    sync.wait_ge(sSP[0].sem, sp_tot[0])
                    sync.wait_ge(sSP[1].sem, sp_tot[1])
                    for c in range(KC):
                        p = c // 2
                        if p >= 1:
                            sync.wait_ge(cDVE.sem, v0 + p * 4)
                        sCH[c % 2].inc(sync.dma_start(out=ch_re[:, c % 2],
                                                      in_=src_sp[c * 128:(c + 1) * 128, :]), 16)

                @blk.vector
                def _(v):
                    v.wait_ge(cDVE.sem, start_dve)
                    v.engine_nop()
                    prev_mul = {}
                    for cp in range(0, KC, 2):
                        ns = {}
                        for c in (cp, cp + 1):
                            if c % 2 in prev_mul:
                                v.wait_ge(cDVE.sem, prev_mul[c % 2])
                            ns[c] = cDVE.inc(v.tensor_tensor_scan(
                                tmp_f[:, c % 2], mvA[:, c, :], zeros_bf[:, :],
                                ceff[:, ceff_slot * KC + c:ceff_slot * KC + c + 1], ALU.add, ALU.add))
                        for c in (cp, cp + 1):
                            v.wait_ge(sCH[c % 2].sem, ch0[c % 2] + (c // 2 + 1) * 16)
                            v.wait_ge(cDVE.sem, ns[c])
                            prev_mul[c % 2] = cDVE.inc(v.tensor_tensor(mvA[:, c, :], tmp_f[:, c % 2],
                                                                       ch_re[:, c % 2], ALU.mult))
            return cDVE.n

        a3b_dve = scan_mul_pass(k_sp, k_stores, 1, e1b_dve)

        kw_stores = proj_spill(wka_in, mvA, w_sp, 4 * KC, AF.Exp, ATT_SCALE, True,
                               mov_ready=[(cDVE.sem, a3b_dve)])
        a4_act = cACT.n

        e2a_dve = exchange(2, (cACT.sem, a4_act))

        a5a_dve = scan_pass(w_sp, kw_stores, 2, e2a_dve)

        e2b_dve = exchange(3, (cDVE.sem, a5a_dve), from_colA=False)

        with nc.Block() as blk:
            v0 = cDVE.n
            a0_ = cACT.n
            ch0 = [sCH[0].n, sCH[1].n]
            st0 = [sST[0].n, sST[1].n]

            @blk.sync
            def _(sync):
                def store(j):
                    sync.wait_ge(cACT.sem, a0_ + j + 1)
                    sync.wait_ge(cDVE.sem, v0 + (j // 2) * 6 + 5 + (j % 2))
                    sST[j % 2].inc(sync.dma_start(out=hi_sp[j * 128:(j + 1) * 128, :],
                                                  in_=hi_st[:, j % 2]), 16)
                    sST[j % 2].inc(sync.dma_start(out=lo_sp[j * 128:(j + 1) * 128, :],
                                                  in_=lo_st[:, j % 2]), 16)
                sync.wait_ge(sSP[0].sem, sSP[0].n)
                sync.wait_ge(sSP[1].sem, sSP[1].n)
                for c in range(KC):
                    p = c // 2
                    if p >= 1:
                        sync.wait_ge(cDVE.sem, v0 + p * 6)
                    sCH[c % 2].inc(sync.dma_start(out=ch_re[:, c % 2],
                                                  in_=v_sp[c * 128:(c + 1) * 128, :]), 16)
                    if c >= 1:
                        store(c - 1)
                store(KC - 1)

            @blk.vector
            def _(v):
                v.wait_ge(cDVE.sem, e2b_dve)
                v.engine_nop()
                prev_out = {}
                for cp in range(0, KC, 2):
                    ns = {}
                    for c in (cp, cp + 1):
                        if c % 2 in prev_out:
                            v.wait_ge(cDVE.sem, prev_out[c % 2])
                        ns[("s", c)] = cDVE.inc(v.tensor_tensor_scan(
                            tmp_f[:, c % 2], mvA[:, c, :], zeros_bf[:, :],
                            ceff[:, 3 * KC + c:3 * KC + c + 1], ALU.add, ALU.add))
                    for c in (cp, cp + 1):
                        v.wait_ge(sCH[c % 2].sem, ch0[c % 2] + (c // 2 + 1) * 16)
                        if c >= 2:
                            v.wait_ge(cACT.sem, a0_ + c - 1)
                        v.wait_ge(cDVE.sem, ns[("s", c)])
                        ns[("o", c)] = cDVE.inc(v.tensor_tensor(out_t2[:, c % 2], tmp_f[:, c % 2],
                                                                ch_re[:, c % 2], ALU.mult))
                        prev_out[c % 2] = ns[("o", c)]
                    for c in (cp, cp + 1):
                        v.wait_ge(cACT.sem, a0_ + c + 1)
                        if c >= 2:
                            v.wait_ge(sST[c % 2].sem, st0[c % 2] + ((c - 2) // 2 + 1) * 32)
                        v.wait_ge(cDVE.sem, ns[("o", c)])
                        cDVE.inc(v.tensor_tensor(lo_st[:, c % 2], out_t2[:, c % 2], hi_st[:, c % 2],
                                                 ALU.subtract))

            @blk.scalar
            def _(s):
                for c in range(KC):
                    s.wait_ge(cDVE.sem, v0 + (c // 2) * 6 + 3 + (c % 2))
                    if c >= 2:
                        s.wait_ge(sST[c % 2].sem, st0[c % 2] + ((c - 2) // 2 + 1) * 32)
                    cACT.inc(s.activation(hi_st[:, c % 2], out_t2[:, c % 2], AF.Copy))
        hilo_stores = (sST[0].n, sST[1].n)

        with nc.Block() as blk:
            v0 = cDVE.n
            xl0 = [sXL[0].n, sXL[1].n]
            st0 = [sST[0].n, sST[1].n]
            xb0 = [sXB[0].n, sXB[1].n]

            @blk.scalar
            def _(s):
                s.wait_ge(sST[0].sem, hilo_stores[0])
                s.wait_ge(sST[1].sem, hilo_stores[1])
                for i in range(KC):
                    if i >= 2:
                        s.wait_ge(cDVE.sem, v0 + (i - 2) * 2 + 1)
                    sXB[i % 2].inc(s.dma_start(out=hi_tm[:, i % 2],
                                               in_=hi_sp[:, i * 128:(i + 1) * 128], transpose=True), 16)
                    sXB[i % 2].inc(s.dma_start(out=lo_tm[:, i % 2],
                                               in_=lo_sp[:, i * 128:(i + 1) * 128], transpose=True), 16)

            @blk.sync
            def _(sync):
                def store(j):
                    sync.wait_ge(cDVE.sem, v0 + j * 2 + 2)
                    sST[j % 2].inc(sync.dma_start(out=x1_sp[j * 128:(j + 1) * 128, :],
                                                  in_=out_t2[:, j % 2]), 16)
                    if N_SUB == 1:
                        sST[j % 2].inc(sync.dma_start(out=out_ap[j * 128:(j + 1) * 128, :],
                                                      in_=out_t2[:, j % 2]), 16)
                for i in range(KC):
                    if i >= 2:
                        sync.wait_ge(cDVE.sem, v0 + (i - 2) * 2 + 2)
                    sXL[i % 2].inc(sync.dma_start(out=x_t2[:, i % 2],
                                                  in_=x_in[i * 128:(i + 1) * 128, :]), 16)
                    if i >= 1:
                        store(i - 1)
                store(KC - 1)
                sync.wait_ge(sST[0].sem, sST[0].n)
                sync.wait_ge(sST[1].sem, sST[1].n)

            @blk.vector
            def _(v):
                spt6 = 32 if N_SUB == 1 else 16
                for i in range(KC):
                    v.wait_ge(sXB[i % 2].sem, xb0[i % 2] + (i // 2 + 1) * 32)
                    v.wait_ge(sXL[i % 2].sem, xl0[i % 2] + (i // 2 + 1) * 16)
                    if i >= 2:
                        v.wait_ge(sST[i % 2].sem, st0[i % 2] + ((i - 2) // 2 + 1) * spt6)
                    nn_ = cDVE.inc(v.scalar_tensor_tensor(out_t2[:, i % 2], hi_tm[:, i % 2], 1.0,
                                                          lo_tm[:, i % 2], ALU.mult, ALU.add))
                    v.wait_ge(cDVE.sem, nn_)
                    cDVE.inc(v.tensor_tensor(out_t2[:, i % 2], out_t2[:, i % 2], x_t2[:, i % 2], ALU.add))

        astk.close()

        wres = e(nc.sbuf_tensor("wres", [128, KC, 1024], BF16))
        raf = e(nc.sbuf_tensor("raf", [128, KC, 72], F32))
        hamf = e(nc.sbuf_tensor("hamf", [64, T], BF16))
        bt_res = e(nc.sbuf_tensor("bt_res", [64, D], BF16))
        rat_t = e(nc.sbuf_tensor("rat_t", [128, KC, 72], BF16))
        bb_row = e(nc.sbuf_tensor("bb_row", [1, D], BF16))
        idn_t = e(nc.sbuf_tensor("idn_t", [128, 128], BF16))
        rsc = e(nc.sbuf_tensor("rsc", [128, 64], F32))
        ham_st = e(nc.sbuf_tensor("ham_st", [128, 64], BF16))
        hb_kf = e(nc.sbuf_tensor("hb_kf", [128, 2, 256], BF16))
        mskd2 = e(nc.sbuf_tensor("mskd2", [128, 2, 512], BF16))
        kv_all = e(nc.sbuf_tensor("kv_all", [128, D], F32))
        dk_t = e(nc.sbuf_tensor("dk_t", [128, 8], F32))
        slope_f = e(nc.sbuf_tensor("slope_f", [128, 16], F32))
        nw_row = e(nc.sbuf_tensor("nw_row", [1, D], BF16))
        iota_t = e(nc.sbuf_tensor("iota_t", [128, 5], F32))
        ao_stage = e(nc.sbuf_tensor("ao_stage", [128, 2, 2, 128], BF16))
        qd_all = e(nc.sbuf_tensor("qd_all", [128, 32], F32))
        kd_all = e(nc.sbuf_tensor("kd_all", [128, 32], F32))
        md_all = e(nc.sbuf_tensor("md_all", [128, 32], F32))
        ind_t = e(nc.sbuf_tensor("ind_t", [128, 512], BF16))
        bdj = e(nc.sbuf_tensor("bdj", [128, 128], F32))
        iota_bj_t = e(nc.sbuf_tensor("iota_bj_t", [128, 8], F32))
        qs_stg = e(nc.sbuf_tensor("qs_stg", [128, 2, D], BF16))
        ks_all = e(nc.sbuf_tensor("ks_all", [128, 2, D], BF16))
        kv_bf_all = e(nc.sbuf_tensor("kv_bf_all", [128, D], BF16))

        with nc.Block() as blk:
            @blk.gpsimd
            def _(gp):
                sING.inc(gp.dma_start(out=rat_t[:, :, :], in_=rat_in), 16)
                sING.inc(gp.dma_start(out=bt_res[:, :], in_=bt_in), 16)
                sING.inc(gp.dma_start(out=bb_row[:, :], in_=bb_in), 16)

            @blk.sync
            def _(sync):
                sIN.inc(sync.dma_start(out=idn_t[:, :], in_=idn_in), 16)
        ffn_consts = (sING.n, sIN.n)

        def emit_ln(x_src):
            with nc.Block() as blk:
                v0, a0 = cDVE.n, cACT.n
                xl0 = [sXL[0].n, sXL[1].n]
                st0 = [sST[0].n, sST[1].n]

                @blk.sync
                def _(sync):
                    sync.wait_ge(sST[0].sem, st0[0])
                    sync.wait_ge(sST[1].sem, st0[1])
                    for i in range(KC):
                        if i >= 2:
                            sync.wait_ge(cDVE.sem, v0 + (i - 2) * 7 + 7)
                        else:
                            sync.wait_ge(cDVE.sem, v0)
                        sXL[i % 2].inc(sync.dma_start(out=x_t2[:, i % 2],
                                                      in_=x_src[i * 128:(i + 1) * 128, :]), 16)
                        if i >= 1:
                            j = i - 1
                            sync.wait_ge(cDVE.sem, v0 + j * 7 + 7)
                            sST[j % 2].inc(sync.dma_start(out=u_sp[j * 128:(j + 1) * 128, :],
                                                          in_=ch_re[:, j % 2, 0:D]), 16)
                    j = KC - 1
                    sync.wait_ge(cDVE.sem, v0 + j * 7 + 7)
                    sST[j % 2].inc(sync.dma_start(out=u_sp[j * 128:(j + 1) * 128, :],
                                                  in_=ch_re[:, j % 2, 0:D]), 16)

                @blk.vector
                def _(v):
                    v.wait_ge(sST[0].sem, st0[0])
                    v.wait_ge(sST[1].sem, st0[1])
                    for i in range(KC):
                        pi = (i % 2) * 16
                        xt = x_t2[:, i % 2]
                        ut = ch_re[:, i % 2, 0:D]
                        v.wait_ge(sXL[i % 2].sem, xl0[i % 2] + (i // 2 + 1) * 16)
                        if i >= 2:
                            v.wait_ge(cACT.sem, a0 + (i - 2) * 2 + 2)
                            v.wait_ge(sST[i % 2].sem, st0[i % 2] + ((i - 2) // 2 + 1) * 16)
                        n = {}
                        def op(key, inst):
                            n[key] = cDVE.inc(inst)
                        op(1, v.tensor_scalar(out_t2[:, i % 2], xt, 1.0, 0.0, ALU.mult, ALU.add,
                                              accum_out=stt[:, pi + 0:pi + 1]))
                        op(2, v.scalar_tensor_tensor(out_t2[:, (i + 1) % 2], xt, 1.0, xt,
                                                     ALU.mult, ALU.mult,
                                                     accum_out=stt[:, pi + 1:pi + 2]))
                        v.wait_ge(cDVE.sem, n[1])
                        op(3, v.tensor_scalar_mul(stt[:, pi + 2:pi + 3], stt[:, pi + 0:pi + 1], 1.0 / D))
                        v.wait_ge(cDVE.sem, n[2])
                        op(4, v.tensor_scalar_mul(stt[:, pi + 3:pi + 4], stt[:, pi + 1:pi + 2], 1.0 / D))
                        v.wait_ge(cDVE.sem, n[3])
                        op(5, v.tensor_tensor(stt[:, pi + 4:pi + 5], stt[:, pi + 2:pi + 3],
                                              stt[:, pi + 2:pi + 3], ALU.mult))
                        v.wait_ge(cDVE.sem, n[5])
                        op(6, v.tensor_tensor(stt[:, pi + 5:pi + 6], stt[:, pi + 3:pi + 4],
                                              stt[:, pi + 4:pi + 5], ALU.subtract))
                        v.wait_ge(cACT.sem, a0 + i * 2 + 2)
                        op(7, v.tensor_scalar(ut, xt, stt[:, pi + 2:pi + 3], stt[:, pi + 6:pi + 7],
                                              ALU.subtract, ALU.mult))

                @blk.scalar
                def _(s):
                    for i in range(KC):
                        pi = (i % 2) * 16
                        s.wait_ge(cDVE.sem, v0 + i * 7 + 6)
                        na = cACT.inc(s.activation(stt[:, pi + 7:pi + 8], stt[:, pi + 5:pi + 6],
                                                   AF.Ln, bias=eps_t[:, 0:1]))
                        s.wait_ge(cACT.sem, na)
                        cACT.inc(s.activation(stt[:, pi + 6:pi + 7], stt[:, pi + 7:pi + 8],
                                              AF.Exp, scale=-0.5))
            return (sST[0].n, sST[1].n)

        def emit_xbar(st_tot):
            with nc.Block() as blk:
                @blk.scalar
                def _(s):
                    s.wait_ge(sST[0].sem, st_tot[0])
                    s.wait_ge(sST[1].sem, st_tot[1])
                    for c in range(KC):
                        sXB[c % 2].inc(s.dma_start(out=mvA[:, c, :],
                                                   in_=u_sp[:, c * 128:(c + 1) * 128],
                                                   transpose=True), 16)
            return (sXB[0].n, sXB[1].n)

        def emit_ffn(x_src, x_dst, store_out):
            st_tot = emit_ln(x_src)
            xb_tot = emit_xbar(st_tot)

            with nc.Block() as blk:
                pe0 = cPE.n
                v0 = cDVE.n
                a0 = cACT.n
                prev_bank = list(bank_last)
                mm_n = {}
                tp_n = {}
                dve_evac = {tr: v0 + tr * 19 + 1 for tr in range(KC)}
                dve_ham = {tr: v0 + tr * 19 + 19 for tr in range(KC)}
                dve_m2 = {tr: v0 + tr * 19 + 6 for tr in range(KC)}
                dve_r24 = {tr: v0 + tr * 19 + 10 for tr in range(KC)}
                act_d = {tr: a0 + (2 if tr == 0 else 3 * tr + 1) for tr in range(KC)}
                act_evac = {tr: a0 + (3 * (tr + 1) + 2 if tr < KC - 1 else 3 * KC) for tr in range(KC)}

                @blk.tensor
                def _(t):
                    t.wait_ge(sXB[0].sem, xb_tot[0])
                    t.wait_ge(sXB[1].sem, xb_tot[1])
                    t.wait_ge(sING.sem, ffn_consts[0])
                    t.wait_ge(sIN.sem, ffn_consts[1])

                    def mm(tr):
                        if tr >= NB:
                            t.wait_ge(cDVE.sem, dve_evac[tr - NB])
                        else:
                            bc, bval = prev_bank[tr % NB]
                            t.wait_ge(bc.sem, bval)
                        inst = None
                        for k in range(KC):
                            inst = t.matmul(pbank[tr % NB][:, 0:72],
                                            mvA[:, k, tr * 128:(tr + 1) * 128],
                                            rat_t[:, k, :], start=(k == 0), stop=(k == KC - 1))
                        mm_n[tr] = cPE.inc(inst)

                    def tp(tr):
                        t.wait_ge(cDVE.sem, dve_ham[tr])
                        if tr >= 1:
                            t.wait_ge(cACT.sem, act_evac[tr - 1])
                        tp_n[tr] = cPE.inc(t.transpose(pbank_bf[0:64, 0:128], ham_st[:, :],
                                                       idn_t[:, :]))

                    for tr in range(NB):
                        mm(tr)
                    for tr in range(KC - NB):
                        tp(tr)
                        mm(tr + NB)
                    for tr in range(KC - NB, KC):
                        tp(tr)

                @blk.vector
                def _(v):
                    for tr in range(KC):
                        n = {}
                        def op(key, inst):
                            n[key] = cDVE.inc(inst)
                        v.wait_ge(cPE.sem, pe0 + tr + 1 if tr < NB else mm_n[tr])
                        op(1, v.tensor_copy(raf[:, tr, :], pbank[tr % NB][:, 0:72]))
                        lg = raf[:, tr, 0:8]
                        v.wait_ge(cDVE.sem, n[1])
                        op(2, v.reduce_max(rsc[:, 0:1], lg, mybir.AxisListType.X))
                        v.wait_ge(cDVE.sem, n[2])
                        op(3, v.tensor_scalar_mul(rsc[:, 1:2], rsc[:, 0:1], -1.0))
                        op(4, v.tensor_single_scalar(rsc[:, 8:16], lg, rsc[:, 0:1], ALU.is_ge))
                        v.wait_ge(cDVE.sem, n[4])
                        op(5, v.scalar_tensor_tensor(rsc[:, 16:24], rsc[:, 8:16], -1e30, lg,
                                                     ALU.mult, ALU.add))
                        v.wait_ge(cDVE.sem, n[5])
                        op(6, v.reduce_max(rsc[:, 2:3], rsc[:, 16:24], mybir.AxisListType.X))
                        v.wait_ge(cACT.sem, act_d[tr])
                        op(7, v.tensor_scalar(rsc[:, 4:5], rsc[:, 3:4], 1.0, 0.5, ALU.add, ALU.mult))
                        v.wait_ge(cDVE.sem, n[7])
                        op(8, v.reciprocal(rsc[:, 5:6], rsc[:, 4:5]))
                        op(9, v.tensor_single_scalar(rsc[:, 32:40], lg, rsc[:, 2:3], ALU.is_ge))
                        v.wait_ge(cDVE.sem, n[9])
                        op(10, v.tensor_tensor(rsc[:, 40:48], rsc[:, 24:32], rsc[:, 32:40], ALU.mult))
                        v.wait_ge(cDVE.sem, n[10])
                        op(11, v.tensor_single_scalar(rsc[:, 48:56], rsc[:, 40:48], rsc[:, 5:6],
                                                      ALU.mult))
                        v.wait_ge(cDVE.sem, n[11])
                        if tr >= 1:
                            v.wait_ge(cPE.sem, tp_n[tr - 1])
                        for ei in range(8):
                            cDVE.inc(v.tensor_single_scalar(ham_st[:, ei * 8:(ei + 1) * 8],
                                                            raf[:, tr, 8 + ei * 8:8 + (ei + 1) * 8],
                                                            rsc[:, 48 + ei:49 + ei], ALU.mult))

                @blk.scalar
                def _(s):
                    for tr in range(KC):
                        s.wait_ge(cDVE.sem, dve_m2[tr])
                        if tr >= 1:
                            s.wait_ge(cDVE.sem, dve_r24[tr - 1])
                        cACT.inc(s.activation(rsc[:, 24:32], raf[:, tr, 0:8], AF.Exp,
                                              bias=rsc[:, 1:2]))
                        cACT.inc(s.activation(rsc[:, 3:4], rsc[:, 2:3], AF.Exp, bias=rsc[:, 1:2]))
                        if tr >= 1:
                            s.wait_ge(cPE.sem, tp_n[tr - 1])
                            cACT.inc(s.copy(hamf[0:64, (tr - 1) * 128:tr * 128],
                                            pbank_bf[0:64, 0:128]))
                    s.wait_ge(cPE.sem, tp_n[KC - 1])
                    cACT.inc(s.copy(hamf[0:64, (KC - 1) * 128:KC * 128], pbank_bf[0:64, 0:128]))
                for b_ in range(NB):
                    bank_last[b_] = (cDVE, cDVE.n)
                bf_last[0], bf_last[1] = cACT, cACT.n
            hamf_act = cACT.n

            with nc.Block() as blk:
                pe0 = cPE.n
                v0 = cDVE.n
                ws_0 = sWS[0].n
                xl0 = [sXL[0].n, sXL[1].n]
                st0 = [sST[0].n, sST[1].n]
                prev_bank = list(bank_last)

                @blk.gpsimd
                def _(gp):
                    for oh in range(2):
                        if oh >= 1:
                            gp.wait_ge(cPE.sem, pe0 + 32)
                        elif wres_pe["n"] > 0:
                            gp.wait_ge(cPE.sem, wres_pe["n"])
                        sWS[0].inc(gp.dma_start(out=wres[:, :, :].rearrange("p k o -> p (k o)"),
                                                in_=wbase_in[oh]), 16)

                @blk.tensor
                def _(t):
                    t.wait_ge(cACT.sem, hamf_act)
                    for oh in range(2):
                        t.wait_ge(sWS[0].sem, ws_0 + (oh + 1) * 16)
                        for tr in range(KC):
                            for oc in range(2):
                                g = oh * 32 + tr * 2 + oc
                                if g >= NB:
                                    t.wait_ge(cDVE.sem, v0 + (g - NB) + 1)
                                else:
                                    bc, bval = prev_bank[g % NB]
                                    t.wait_ge(bc.sem, bval)
                                bank = pbank[g % NB]
                                inst = None
                                for k in range(KC):
                                    inst = t.matmul(bank[:, :], mvA[:, k, tr * 128:(tr + 1) * 128],
                                                    wres[:, k, oc * TBW:(oc + 1) * TBW],
                                                    start=(k == 0), stop=False)
                                t.matmul(bank[:, :], ones_t[0:1, :],
                                         bb_row[0:1, oh * 1024 + oc * TBW:oh * 1024 + (oc + 1) * TBW],
                                         start=False, stop=False)
                                inst = t.matmul(bank[:, :], hamf[0:64, tr * 128:(tr + 1) * 128],
                                                bt_res[0:64, oh * 1024 + oc * TBW:oh * 1024 + (oc + 1) * TBW],
                                                start=False, stop=True)
                                cPE.inc(inst)

                @blk.sync
                def _(sync):
                    for oh in range(2):
                        for tr in range(KC):
                            g2 = oh * 16 + tr
                            if g2 >= 2:
                                sync.wait_ge(cDVE.sem, v0 + (g2 - 2) * 2 + 2)
                            else:
                                sync.wait_ge(cDVE.sem, v0)
                            sXL[tr % 2].inc(sync.dma_start(
                                out=x_t2[:, tr % 2, 0:1024],
                                in_=x_src[tr * 128:(tr + 1) * 128, oh * 1024:(oh + 1) * 1024]), 16)
                            if g2 >= 1:
                                j2 = g2 - 1
                                joh, jtr = j2 // 16, j2 % 16
                                sync.wait_ge(cDVE.sem, v0 + j2 * 2 + 2)
                                sST[jtr % 2].inc(sync.dma_start(
                                    out=x_dst[jtr * 128:(jtr + 1) * 128, joh * 1024:(joh + 1) * 1024],
                                    in_=out_t2[:, jtr % 2, 0:1024]), 16)
                                if store_out:
                                    sST[jtr % 2].inc(sync.dma_start(
                                        out=out_ap[jtr * 128:(jtr + 1) * 128, joh * 1024:(joh + 1) * 1024],
                                        in_=out_t2[:, jtr % 2, 0:1024]), 16)
                    j2 = 31
                    joh, jtr = j2 // 16, j2 % 16
                    sync.wait_ge(cDVE.sem, v0 + j2 * 2 + 2)
                    sST[jtr % 2].inc(sync.dma_start(
                        out=x_dst[jtr * 128:(jtr + 1) * 128, joh * 1024:(joh + 1) * 1024],
                        in_=out_t2[:, jtr % 2, 0:1024]), 16)
                    if store_out:
                        sST[jtr % 2].inc(sync.dma_start(
                            out=out_ap[jtr * 128:(jtr + 1) * 128, joh * 1024:(joh + 1) * 1024],
                            in_=out_t2[:, jtr % 2, 0:1024]), 16)
                    if store_out:
                        sync.wait_ge(sST[0].sem, sST[0].n)
                        sync.wait_ge(sST[1].sem, sST[1].n)

                @blk.vector
                def _(v):
                    spt = 32 if store_out else 16
                    for oh in range(2):
                        for tr in range(KC):
                            g2 = oh * 16 + tr
                            v.wait_ge(sXL[tr % 2].sem, xl0[tr % 2] + (g2 // 2 + 1) * 16)
                            if g2 >= 2:
                                v.wait_ge(sST[tr % 2].sem, st0[tr % 2] + ((g2 - 2) // 2 + 1) * spt)
                            nprev = None
                            for oc in range(2):
                                g = oh * 32 + tr * 2 + oc
                                v.wait_ge(cPE.sem, pe0 + g + 1)
                                if nprev is not None:
                                    v.wait_ge(cDVE.sem, nprev)
                                nprev = cDVE.inc(v.scalar_tensor_tensor(
                                    out_t2[:, tr % 2, oc * TBW:(oc + 1) * TBW],
                                    pbank[g % NB][:, :], 1.0,
                                    x_t2[:, tr % 2, oc * TBW:(oc + 1) * TBW],
                                    ALU.mult, ALU.add))
                for b_ in range(NB):
                    bank_last[b_] = (cDVE, cDVE.n)
                wres_pe["n"] = cPE.n

        def emit_optA(wt2, dst_sp, func=None, x_src=None, x_dst=None, store_out=False,
                      extra_ready=()):
            with nc.Block() as blk:
                pe0 = cPE.n
                v0 = cDVE.n
                a0 = cACT.n
                ws_0 = sWS[0].n
                xl0 = [sXL[0].n, sXL[1].n]
                st0 = [sST[0].n, sST[1].n]
                prev_bank = list(bank_last)
                evac_ctr = cACT if func is not None else cDVE

                @blk.gpsimd
                def _(gp):
                    for oh in range(2):
                        if oh >= 1:
                            gp.wait_ge(cPE.sem, pe0 + 32)
                        else:
                            gp.wait_ge(cPE.sem, pe0)
                        sWS[0].inc(gp.dma_start(out=wres[:, :, :].rearrange("p k o -> p (k o)"),
                                                in_=wt2[oh]), 16)

                @blk.tensor
                def _(t):
                    for sem, val in extra_ready:
                        t.wait_ge(sem, val)
                    for oh in range(2):
                        t.wait_ge(sWS[0].sem, ws_0 + (oh + 1) * 16)
                        for tr in range(KC):
                            for oc in range(2):
                                g = oh * 32 + tr * 2 + oc
                                if g >= NB:
                                    t.wait_ge(evac_ctr.sem, v0 + (g - NB) + 1 if func is None
                                              else a0 + (g - NB) + 1)
                                else:
                                    bc, bval = prev_bank[g % NB]
                                    t.wait_ge(bc.sem, bval)
                                bank = pbank[g % NB]
                                inst = None
                                for k in range(KC):
                                    inst = t.matmul(bank[:, :], mvA[:, k, tr * 128:(tr + 1) * 128],
                                                    wres[:, k, oc * TBW:(oc + 1) * TBW],
                                                    start=(k == 0), stop=(k == KC - 1))
                                cPE.inc(inst)

                if func is not None:
                    @blk.scalar
                    def _(s):
                        for oh in range(2):
                            for tr in range(KC):
                                g2 = oh * 16 + tr
                                if g2 >= 2:
                                    s.wait_ge(sST[tr % 2].sem,
                                              st0[tr % 2] + ((g2 - 2) // 2 + 1) * 16)
                                else:
                                    s.wait_ge(sST[tr % 2].sem, st0[tr % 2])
                                for oc in range(2):
                                    g = oh * 32 + tr * 2 + oc
                                    s.wait_ge(cPE.sem, pe0 + g + 1)
                                    cACT.inc(s.activation(
                                        ch_re[:, tr % 2, oc * TBW:(oc + 1) * TBW],
                                        pbank[g % NB][:, :], func))

                    @blk.sync
                    def _(sync):
                        for oh in range(2):
                            for tr in range(KC):
                                g2 = oh * 16 + tr
                                sync.wait_ge(cACT.sem, a0 + g2 * 2 + 2)
                                sST[tr % 2].inc(sync.dma_start(
                                    out=dst_sp[tr * 128:(tr + 1) * 128, oh * 1024:(oh + 1) * 1024],
                                    in_=ch_re[:, tr % 2, 0:1024]), 16)
                    for b_ in range(NB):
                        bank_last[b_] = (cACT, cACT.n)
                else:
                    @blk.sync
                    def _(sync):
                        spt = 32 if store_out else 16
                        nst = 0
                        for oh in range(2):
                            for tr in range(KC):
                                g2 = oh * 16 + tr
                                if g2 >= 2:
                                    sync.wait_ge(cDVE.sem, v0 + (g2 - 2) * 2 + 2)
                                else:
                                    sync.wait_ge(cDVE.sem, v0)
                                sXL[tr % 2].inc(sync.dma_start(
                                    out=x_t2[:, tr % 2, 0:1024],
                                    in_=x_src[tr * 128:(tr + 1) * 128, oh * 1024:(oh + 1) * 1024]), 16)
                                if g2 >= 1:
                                    j2 = g2 - 1
                                    joh, jtr = j2 // 16, j2 % 16
                                    sync.wait_ge(cDVE.sem, v0 + j2 * 2 + 2)
                                    sST[jtr % 2].inc(sync.dma_start(
                                        out=x_dst[jtr * 128:(jtr + 1) * 128,
                                                  joh * 1024:(joh + 1) * 1024],
                                        in_=out_t2[:, jtr % 2, 0:1024]), 16)
                                    if store_out:
                                        sST[jtr % 2].inc(sync.dma_start(
                                            out=out_ap[jtr * 128:(jtr + 1) * 128,
                                                       joh * 1024:(joh + 1) * 1024],
                                            in_=out_t2[:, jtr % 2, 0:1024]), 16)
                        j2 = 31
                        joh, jtr = 1, 15
                        sync.wait_ge(cDVE.sem, v0 + j2 * 2 + 2)
                        sST[jtr % 2].inc(sync.dma_start(
                            out=x_dst[jtr * 128:(jtr + 1) * 128, joh * 1024:(joh + 1) * 1024],
                            in_=out_t2[:, jtr % 2, 0:1024]), 16)
                        if store_out:
                            sST[jtr % 2].inc(sync.dma_start(
                                out=out_ap[jtr * 128:(jtr + 1) * 128, joh * 1024:(joh + 1) * 1024],
                                in_=out_t2[:, jtr % 2, 0:1024]), 16)
                            sync.wait_ge(sST[0].sem, sST[0].n)
                            sync.wait_ge(sST[1].sem, sST[1].n)

                    @blk.vector
                    def _(v):
                        spt = 32 if store_out else 16
                        for oh in range(2):
                            for tr in range(KC):
                                g2 = oh * 16 + tr
                                v.wait_ge(sXL[tr % 2].sem, xl0[tr % 2] + (g2 // 2 + 1) * 16)
                                if g2 >= 2:
                                    v.wait_ge(sST[tr % 2].sem,
                                              st0[tr % 2] + ((g2 - 2) // 2 + 1) * spt)
                                else:
                                    v.wait_ge(sST[tr % 2].sem, st0[tr % 2])
                                nprev = None
                                for oc in range(2):
                                    g = oh * 32 + tr * 2 + oc
                                    v.wait_ge(cPE.sem, pe0 + g + 1)
                                    if nprev is not None:
                                        v.wait_ge(cDVE.sem, nprev)
                                    nprev = cDVE.inc(v.scalar_tensor_tensor(
                                        out_t2[:, tr % 2, oc * TBW:(oc + 1) * TBW],
                                        pbank[g % NB][:, :], 1.0,
                                        x_t2[:, tr % 2, oc * TBW:(oc + 1) * TBW],
                                        ALU.mult, ALU.add))
                    for b_ in range(NB):
                        bank_last[b_] = (cDVE, cDVE.n)

        def emit_attn_b(x_src, x_dst, store_out):
            st_tot = emit_ln(x_src)
            xb_tot = emit_xbar(st_tot)

            with nc.Block() as blk:
                @blk.sync
                def _(sync):
                    sIN.inc(sync.dma_start(out=iota_t[:, 0:2], in_=iota1_in), 16)
                    sIN.inc(sync.dma_start(out=iota_t[:, 2:4], in_=iota2_in), 16)
                    sIN.inc(sync.dma_start(out=iota_t[:, 4:5], in_=c256_in), 16)
                    sIN.inc(sync.dma_start(out=slope_f[:, :], in_=slope_in), 16)
                    sIN.inc(sync.dma_start(out=iota_bj_t[:, :], in_=iotabj_in), 16)
                    sIN.inc(sync.dma_start(out=ind_t[:, :], in_=ind_in), 16)

                @blk.gpsimd
                def _(gp):
                    sING.inc(gp.dma_start(out=nw_row[:, :], in_=nw_in), 16)
                binit_in = (sIN.n, sING.n)

                @blk.tensor
                def _(t):
                    t.wait_ge(sING.sem, binit_in[1])
                    t.wait_ge(cACT.sem, cACT.n)
                    inst = None
                    for j in range(NTB):
                        inst = t.matmul(pbank[j % NB][:, :], ones_t[0:1, :],
                                        nw_row[0:1, j * TBW:(j + 1) * TBW], start=True, stop=True)
                    cPE.inc(inst)
                nwpe = cPE.n

                @blk.scalar
                def _(s):
                    s.wait_ge(cPE.sem, nwpe)
                    inst = None
                    for j in range(NTB):
                        inst = s.copy(g_bc[:, j * TBW:(j + 1) * TBW], pbank[j % NB][:, :])
                    cACT.inc(inst)

                @blk.vector
                def _(v):
                    cDVE.inc(v.memset(dk_t[:, 7:8], 1e-6))
            binit_act = cACT.n
            binit_dve = cDVE.n
            for b_ in range(NTB):
                bank_last[b_] = (cACT, binit_act)

            NH, NJ = 16, 8

            with nc.Block() as blk:
                v0p, a0p = cDVE.n, cACT.n

                @blk.vector
                def _(v):
                    v.wait_ge(cDVE.sem, binit_dve)
                    v.wait_ge(sIN.sem, binit_in[0])
                    for h in range(NH):
                        if h >= 1:
                            v.wait_ge(cACT.sem, a0p + 4 * h)
                        cDVE.inc(v.tensor_scalar_mul(dk_t[:, 6:7], slope_f[:, h:h + 1], -1.0))

                @blk.scalar
                def _(s):
                    for h in range(NH):
                        s.wait_ge(cDVE.sem, v0p + h + 1)
                        cACT.inc(s.activation(qd_all[:, h * 2:h * 2 + 2], iota_t[:, 0:2],
                                              AF.Exp, scale=dk_t[:, 6:7]))
                        cACT.inc(s.activation(kd_all[:, h * 2:h * 2 + 2], iota_t[:, 2:4],
                                              AF.Exp, scale=dk_t[:, 6:7]))
                        cACT.inc(s.activation(bdj[:, h * 8:(h + 1) * 8], iota_bj_t[:, :],
                                              AF.Exp, scale=dk_t[:, 6:7]))
                        cACT.inc(s.activation(md_all[:, h * 2:h * 2 + 2], iota_t[:, 0:2],
                                              AF.Exp, scale=slope_f[:, h:h + 1]))
            p0_act = cACT.n

            def emit_p1_pass(src_sp2, gate, tab, dst_sp2):
                with nc.Block() as blk:
                    v0 = cDVE.n
                    rg0 = [sRG[0].n, sRG[1].n]
                    sp0 = [sSP[0].n, sSP[1].n]

                    @blk.sync
                    def _(sync):
                        def store(j):
                            sync.wait_ge(cDVE.sem, v0 + (j + 1) * NH)
                            sSP[j % 2].inc(sync.dma_start(
                                out=dst_sp2[j * 128:(j + 1) * 128, :],
                                in_=qs_stg[:, j % 2]), 16)
                        for i in range(KC):
                            sync.wait_ge(sST[0].sem, gate[0])
                            sync.wait_ge(sST[1].sem, gate[1])
                            if i >= 2:
                                sync.wait_ge(sSP[i % 2].sem,
                                             sp0[i % 2] + ((i - 2) // 2 + 1) * 16)
                            else:
                                sync.wait_ge(sSP[i % 2].sem, sp0[i % 2])
                            sRG[i % 2].inc(sync.dma_start(
                                out=qs_stg[:, i % 2],
                                in_=src_sp2[i * 128:(i + 1) * 128, :]), 16)
                            if i >= 1:
                                store(i - 1)
                        store(KC - 1)

                    @blk.vector
                    def _(v):
                        v.wait_ge(cACT.sem, p0_act)
                        for i in range(KC):
                            v.wait_ge(sRG[i % 2].sem, rg0[i % 2] + (i // 2 + 1) * 16)
                            for h in range(NH):
                                cDVE.inc(v.tensor_single_scalar(
                                    qs_stg[:, i % 2, h * 128:(h + 1) * 128],
                                    qs_stg[:, i % 2, h * 128:(h + 1) * 128],
                                    tab[:, h * 2 + (i % 2):h * 2 + (i % 2) + 1], ALU.mult))

            xbrdy = [(sXB[0].sem, xb_tot[0]), (sXB[1].sem, xb_tot[1]),
                     (sIN.sem, binit_in[0])]
            emit_optA(qkvw_in[0], bq_sp, func=AF.Silu, extra_ready=xbrdy)
            bq_st = (sST[0].n, sST[1].n)
            emit_optA(qkvw_in[1], bk_sp, func=AF.Silu)
            bk_st = (sST[0].n, sST[1].n)
            emit_p1_pass(bq_sp, bq_st, qd_all, u_sp)
            emit_optA(qkvw_in[2], bv_sp, func=AF.Silu)
            emit_optA(wgate_in, gate_sp, func=AF.Sigmoid)
            emit_p1_pass(bk_sp, bk_st, kd_all, k_sp)
            emit_p1_pass(bk_sp, bk_st, md_all, q_sp)
            b1_st = (sST[0].n, sST[1].n)
            b1_pe = cPE.n
            p1_sp = (sSP[0].n, sSP[1].n)


            with nc.Block() as blk:
                @blk.scalar
                def _(s):
                    s.wait_ge(sSP[0].sem, p1_sp[0])
                    s.wait_ge(sSP[1].sem, p1_sp[1])
                    s.wait_ge(cPE.sem, b1_pe)
                    for h in range(NH):
                        sXB[h % 2].inc(s.dma_start(out=mvA[:, h, :],
                                                   in_=u_sp[:, h * 128:(h + 1) * 128],
                                                   transpose=True), 16)
            qs_xb = (sXB[0].n, sXB[1].n)

            NS = NH * NJ
            with nc.Block() as blk:
                pe0u, v0u = cPE.n, cDVE.n
                rg0 = sRG[0].n
                xl0 = sXL[0].n
                prev_bank = list(bank_last)
                pe_uh = {}
                c = pe0u
                for j in range(NJ):
                    for h in range(NH):
                        c += 1; pe_uh[(j, h)] = c
                p3_pe_end = c
                dve_ev3, dve_chain = {}, {}
                c = v0u
                for j in range(NJ):
                    for g2 in range(4):
                        c += 1; dve_ev3[(j, g2)] = c
                    c += (1 if j == 0 else 18); dve_chain[j] = c
                p3_dve_end = c

                def b3i(j, g2):
                    return (j * 4 + g2) % 7

                @blk.sync
                def _(sync):
                    sync.wait_ge(sST[0].sem, b1_st[0])
                    sync.wait_ge(sST[1].sem, b1_st[1])
                    sync.wait_ge(sSP[0].sem, p1_sp[0])
                    sync.wait_ge(sSP[1].sem, p1_sp[1])
                    for j in range(NJ):
                        if j >= 1:
                            sync.wait_ge(cPE.sem, pe_uh[(j - 1, NH - 1)])
                        sXL[0].inc(sync.dma_start(
                            out=qs_stg[:, :, :].rearrange("p s d -> p (s d)"),
                            in_=bv_sp[j * 256:(j + 1) * 256, :]
                            .rearrange("(s p) d -> p s d", s=2)), 16)
                        sRG[0].inc(sync.dma_start(
                            out=ks_all[:, :, :].rearrange("p s d -> p (s d)"),
                            in_=k_sp[j * 256:(j + 1) * 256, :]
                            .rearrange("(s p) d -> p s d", s=2)), 16)

                @blk.tensor
                def _(t):
                    for j in range(NJ):
                        t.wait_ge(sXL[0].sem, xl0 + (j + 1) * 16)
                        t.wait_ge(sRG[0].sem, rg0 + (j + 1) * 16)
                        for h in range(NH):
                            g2 = h // 4
                            bi = j * 4 + g2
                            if h % 4 == 0:
                                if bi >= 7:
                                    jj2, gg2 = divmod(bi - 7, 4)
                                    t.wait_ge(cDVE.sem, dve_ev3[(jj2, gg2)])
                                else:
                                    bc, bval = prev_bank[bi % 7]
                                    t.wait_ge(bc.sem, bval)
                            inst = None
                            for sub in range(2):
                                inst = t.matmul(
                                    pbank[b3i(j, g2)][:, (h % 4) * 128:(h % 4 + 1) * 128],
                                    ks_all[:, sub, h * 128:(h + 1) * 128],
                                    qs_stg[:, sub, h * 128:(h + 1) * 128],
                                    start=(sub == 0), stop=(sub == 1))
                            cPE.inc(inst)

                @blk.vector
                def _(v):
                    for j in range(NJ):
                        for g2 in range(4):
                            v.wait_ge(cPE.sem, pe_uh[(j, 4 * g2 + 3)])
                            cDVE.inc(v.tensor_copy(
                                wres[:, 4 * g2:4 * g2 + 4, j * 128:(j + 1) * 128],
                                pbank[b3i(j, g2)][:, 0:512]
                                .rearrange("p (h e) -> p h e", h=4)))
                        ual = wres[:, :, j * 128:(j + 1) * 128]
                        kv_v = kv_all[:, :].rearrange("p (h e) -> p h e", h=NH)
                        v.wait_ge(cDVE.sem, dve_ev3[(j, 3)])
                        if j == 0:
                            cDVE.inc(v.tensor_copy(kv_v, ual))
                        else:
                            nl = cDVE.inc(v.tensor_copy(
                                wres[:, :, (j - 1) * 128:j * 128], kv_v))
                            v.wait_ge(cDVE.sem, nl)
                            nm = None
                            for h2 in range(NH):
                                nm = cDVE.inc(v.tensor_single_scalar(
                                    kv_all[:, h2 * 128:(h2 + 1) * 128],
                                    kv_all[:, h2 * 128:(h2 + 1) * 128],
                                    bdj[:, h2 * 8 + 1:h2 * 8 + 2], ALU.mult))
                            v.wait_ge(cDVE.sem, nm)
                            cDVE.inc(v.tensor_tensor(kv_v, kv_v, ual, ALU.add))
                for b_ in range(NB):
                    bank_last[b_] = (cDVE, p3_dve_end)
            p3_pe = cPE.n

            with nc.Block() as blk:
                @blk.sync
                def _(sync):
                    sync.wait_ge(cDVE.sem, p3_dve_end)
                    sync.wait_ge(cCC.sem, 4)
                    cCI.inc(sync.dma_start(out=cc2_in[:, :], in_=kv_all[:, :]), 16)
                ci_val = cCI.n

                @blk.gpsimd
                def _(gp):
                    gp.wait_ge(cCI.sem, ci_val)
                    gp.wait_ge(cGA.sem, 4 * 16)
                    cCC.inc(gp.collective_compute(
                        "AllGather", ALU.bypass,
                        replica_groups=[[0, 1], [2, 3], [4, 5], [6, 7]],
                        ins=[cc2_in.ap().opt()], outs=[cc2_out.ap().opt()]))
                    gp.wait_ge(cCC.sem, cCC.n)
                    cGA.inc(gp.dma_start(out=kv_all[:, :], in_=cc2_out[0:128, :]), 16)
                ga_val = cGA.n

                @blk.vector
                def _(v):
                    v.wait_ge(cGA.sem, ga_val)
                    cDVE.inc(v.tensor_single_scalar(kv_all[:, :], kv_all[:, :],
                                                    flag_t[:, 0:1], ALU.mult))
            e3_dve = cDVE.n

            with nc.Block() as blk:
                v0, a0, pe0 = cDVE.n, cACT.n, cPE.n
                xl0 = [sXL[0].n, sXL[1].n]
                xb0 = [sXB[0].n, sXB[1].n]
                st0 = [sST[0].n, sST[1].n]
                prev_bank = list(bank_last)
                dve_fixj, dve_msk, dve_ev = {}, {}, {}
                c = v0
                for s_ in range(NS):
                    h, j = s_ % NH, s_ // NH
                    if h == 0:
                        c += 1; dve_fixj[j] = c
                        c += NH
                    c += 1; dve_msk[s_] = c
                    if s_ >= 1:
                        c += 1; dve_ev[s_ - 1] = c
                c += 1; dve_ev[NS - 1] = c
                b2b_dve_end = c
                pe_qk, pe_out = {}, {}
                c = pe0
                for s_ in range(NS):
                    c += 2; pe_qk[s_] = c
                    c += 1; pe_out[s_] = c
                b2b_pe_end = c

                @blk.sync
                def _(sync):
                    for j in range(NJ):
                        if j >= 1:
                            sync.wait_ge(cPE.sem, pe_out[j * NH - 1])
                        else:
                            sync.wait_ge(cPE.sem, p3_pe)
                        sXL[j % 2].inc(sync.dma_start(
                            out=qs_stg[:, :, :].rearrange("p s d -> p (s d)"),
                            in_=bv_sp[j * 256:(j + 1) * 256, :]
                            .rearrange("(s p) d -> p s d", s=2)), 16)
                        for h in range(NH):
                            s_ = j * NH + h
                            if s_ >= 1:
                                sm1 = s_ - 1
                                hh, jj = sm1 % NH, sm1 // NH
                                pp = sm1 % 2
                                sync.wait_ge(cDVE.sem, dve_ev[sm1])
                                sST[pp].inc(sync.dma_start(
                                    out=v_sp[hh * 128:(hh + 1) * 128,
                                             jj * 256:(jj + 1) * 256],
                                    in_=ao_stage[:, pp, :, :]
                                    .rearrange("a b c -> a (b c)")), 16)
                    s_ = NS - 1
                    hh, jj = s_ % NH, s_ // NH
                    pp = s_ % 2
                    sync.wait_ge(cDVE.sem, dve_ev[s_])
                    sST[pp].inc(sync.dma_start(
                        out=v_sp[hh * 128:(hh + 1) * 128, jj * 256:(jj + 1) * 256],
                        in_=ao_stage[:, pp, :, :].rearrange("a b c -> a (b c)")), 16)

                @blk.scalar
                def _(s):
                    for s_ in range(NS):
                        h, j = s_ % NH, s_ // NH
                        p = s_ % 2
                        if s_ >= 2:
                            s.wait_ge(cPE.sem, pe_qk[s_ - 2])
                        sXB[p].inc(s.dma_start(
                            out=hb_kf[:, p, :],
                            in_=q_sp[j * 256:(j + 1) * 256, h * 128:(h + 1) * 128],
                            transpose=True), 16)

                @blk.vector
                def _(v):
                    v.wait_ge(cDVE.sem, e3_dve)
                    for s_ in range(NS):
                        h, j = s_ % NH, s_ // NH
                        p = s_ % 2
                        if h == 0:
                            if j == 0:
                                nf = cDVE.inc(v.tensor_copy(kv_bf_all[:, :], kv_all[:, :]))
                            else:
                                v.wait_ge(cPE.sem, pe_out[j * NH - 1])
                                nf = cDVE.inc(v.tensor_tensor(
                                    kv_bf_all[:, :].rearrange("p (h e) -> p h e", h=NH),
                                    kv_all[:, :].rearrange("p (h e) -> p h e", h=NH),
                                    wres[:, :, (j - 1) * 128:j * 128], ALU.add))
                            v.wait_ge(cDVE.sem, nf)
                            for h2 in range(NH):
                                cDVE.inc(v.tensor_single_scalar(
                                    kv_all[:, h2 * 128:(h2 + 1) * 128],
                                    kv_all[:, h2 * 128:(h2 + 1) * 128],
                                    bdj[:, h2 * 8 + 1:h2 * 8 + 2], ALU.mult))
                        if s_ >= 2:
                            v.wait_ge(cPE.sem, pe_out[s_ - 2])
                        v.wait_ge(cPE.sem, pe_qk[s_])
                        cDVE.inc(v.tensor_tensor(mskd2[:, p, :], pbank[s_ % 4][:, 0:512],
                                                 ind_t[:, :], ALU.mult))
                        if s_ >= 1:
                            sm1 = s_ - 1
                            v.wait_ge(cPE.sem, pe_out[sm1])
                            if sm1 >= 2:
                                v.wait_ge(sST[sm1 % 2].sem,
                                          st0[sm1 % 2] + ((sm1 - 2) // 2 + 1) * 16)
                            cDVE.inc(v.tensor_copy(ao_stage[:, sm1 % 2, :, :]
                                                   .rearrange("a b c -> a (b c)"),
                                                   pbank[4 + sm1 % 3][:, 0:256]))
                    s_ = NS - 1
                    v.wait_ge(cPE.sem, pe_out[s_])
                    v.wait_ge(sST[s_ % 2].sem, st0[s_ % 2] + ((s_ - 2) // 2 + 1) * 16)
                    cDVE.inc(v.tensor_copy(ao_stage[:, s_ % 2, :, :]
                                           .rearrange("a b c -> a (b c)"),
                                           pbank[4 + s_ % 3][:, 0:256]))

                @blk.tensor
                def _(t):
                    t.wait_ge(sXB[0].sem, qs_xb[0])
                    t.wait_ge(sXB[1].sem, qs_xb[1])
                    for s_ in range(NS):
                        h, j = s_ % NH, s_ // NH
                        p = s_ % 2
                        t.wait_ge(sXB[p].sem, xb0[p] + (s_ // 2 + 1) * 16)
                        if s_ >= 4:
                            t.wait_ge(cDVE.sem, dve_msk[s_ - 4])
                        else:
                            bc, bval = prev_bank[s_ % 4]
                            t.wait_ge(bc.sem, bval)
                        for sub2 in range(2):
                            cPE.inc(t.matmul(pbank[s_ % 4][:, sub2 * 256:(sub2 + 1) * 256],
                                             hb_kf[:, p, sub2 * 128:(sub2 + 1) * 128],
                                             mvA[:, h, j * 256:(j + 1) * 256],
                                             start=True, stop=True))
                        t.wait_ge(cDVE.sem, dve_msk[s_])
                        t.wait_ge(sXL[j % 2].sem, xl0[j % 2] + (j // 2 + 1) * 16)
                        if s_ >= 3:
                            t.wait_ge(cDVE.sem, dve_ev[s_ - 3])
                        else:
                            bc, bval = prev_bank[4 + s_ % 3]
                            t.wait_ge(bc.sem, bval)
                        ob = pbank[4 + s_ % 3]
                        t.matmul(ob[:, 0:256], kv_bf_all[:, h * 128:(h + 1) * 128],
                                 mvA[:, h, j * 256:(j + 1) * 256], start=True, stop=False)
                        t.matmul(ob[:, 0:256], qs_stg[:, 0, h * 128:(h + 1) * 128],
                                 mskd2[:, p, 0:256], start=False, stop=False)
                        inst = t.matmul(ob[:, 0:256], qs_stg[:, 1, h * 128:(h + 1) * 128],
                                        mskd2[:, p, 256:512], start=False, stop=True)
                        cPE.inc(inst)
                for b_ in range(NB):
                    bank_last[b_] = (cDVE, b2b_dve_end)
            assert cDVE.n == b2b_dve_end, (cDVE.n, b2b_dve_end)
            assert cPE.n == b2b_pe_end, (cPE.n, b2b_pe_end)
            b2b_st = (sST[0].n, sST[1].n)

            with nc.Block() as blk:
                v0, a0 = cDVE.n, cACT.n
                xl0 = [sXL[0].n, sXL[1].n]
                rg0 = [sRG[0].n, sRG[1].n]
                st0 = [sST[0].n, sST[1].n]

                @blk.sync
                def _(sync):
                    sync.wait_ge(sST[0].sem, b2b_st[0])
                    sync.wait_ge(sST[1].sem, b2b_st[1])
                    for i in range(KC):
                        if i >= 2:
                            sync.wait_ge(cDVE.sem, v0 + (i - 2) * 4 + 4)
                        else:
                            sync.wait_ge(cDVE.sem, v0)
                        sXL[i % 2].inc(sync.dma_start(out=ch_re[:, i % 2, 0:D],
                                                      in_=v_sp[:, i * 128:(i + 1) * 128],
                                                      transpose=True), 16)
                        sRG[i % 2].inc(sync.dma_start(out=mvA[:, i % 2, 0:D],
                                                      in_=gate_sp[i * 128:(i + 1) * 128, :]), 16)
                        if i >= 1:
                            j = i - 1
                            sync.wait_ge(cDVE.sem, v0 + j * 4 + 4)
                            sST[j % 2].inc(sync.dma_start(out=gn_sp[j * 128:(j + 1) * 128, :],
                                                          in_=mvA[:, 2 + j % 2, 0:D]), 16)
                    j = KC - 1
                    sync.wait_ge(cDVE.sem, v0 + j * 4 + 4)
                    sST[j % 2].inc(sync.dma_start(out=gn_sp[j * 128:(j + 1) * 128, :],
                                                  in_=mvA[:, 2 + j % 2, 0:D]), 16)

                @blk.vector
                def _(v):
                    for i in range(KC):
                        pi = (i % 2) * 16
                        at = ch_re[:, i % 2, 0:D]
                        v.wait_ge(sXL[i % 2].sem, xl0[i % 2] + (i // 2 + 1) * 16)
                        v.wait_ge(sRG[i % 2].sem, rg0[i % 2] + (i // 2 + 1) * 16)
                        if i >= 2:
                            v.wait_ge(cACT.sem, a0 + (i - 2) * 2 + 2)
                            v.wait_ge(sST[i % 2].sem, st0[i % 2] + ((i - 2) // 2 + 1) * 16)
                        n1 = cDVE.inc(v.scalar_tensor_tensor(out_t2[:, i % 2, 0:D], at, 1.0, at,
                                                             ALU.mult, ALU.mult,
                                                             accum_out=stt[:, pi + 0:pi + 1]))
                        v.wait_ge(cDVE.sem, n1)
                        cDVE.inc(v.tensor_scalar_mul(stt[:, pi + 1:pi + 2], stt[:, pi + 0:pi + 1],
                                                     1.0 / D))
                        v.wait_ge(cACT.sem, a0 + i * 2 + 2)
                        n3 = cDVE.inc(v.scalar_tensor_tensor(out_t2[:, i % 2, 0:D], at,
                                                             stt[:, pi + 3:pi + 4], g_bc[:, :],
                                                             ALU.mult, ALU.mult))
                        v.wait_ge(cDVE.sem, n3)
                        cDVE.inc(v.tensor_tensor(mvA[:, 2 + i % 2, 0:D], out_t2[:, i % 2, 0:D],
                                                 mvA[:, i % 2, 0:D], ALU.mult))

                @blk.scalar
                def _(s):
                    for i in range(KC):
                        pi = (i % 2) * 16
                        s.wait_ge(cDVE.sem, v0 + i * 4 + 2)
                        na = cACT.inc(s.activation(stt[:, pi + 2:pi + 3], stt[:, pi + 1:pi + 2],
                                                   AF.Ln, bias=dk_t[:, 7:8]))
                        s.wait_ge(cACT.sem, na)
                        cACT.inc(s.activation(stt[:, pi + 3:pi + 4], stt[:, pi + 2:pi + 3],
                                              AF.Exp, scale=-0.5))
            b3_st = (sST[0].n, sST[1].n)

            with nc.Block() as blk:
                @blk.scalar
                def _(s):
                    s.wait_ge(sST[0].sem, b3_st[0])
                    s.wait_ge(sST[1].sem, b3_st[1])
                    for c2 in range(KC):
                        sXB[c2 % 2].inc(s.dma_start(out=mvA[:, c2, :],
                                                    in_=gn_sp[:, c2 * 128:(c2 + 1) * 128],
                                                    transpose=True), 16)
            gn_xb = [(sXB[0].sem, sXB[0].n), (sXB[1].sem, sXB[1].n)]

            emit_optA(wout_in, None, func=None, x_src=x_src, x_dst=x_dst,
                      store_out=store_out, extra_ready=gn_xb)

        emit_ffn(x1_sp, x2_sp, store_out=(N_SUB == 2))
        if N_SUB >= 3:
            emit_attn_b(x2_sp, x3_sp, store_out=(N_SUB == 3))
        if N_SUB >= 4:
            emit_ffn(x3_sp, x2_sp, store_out=True)

    return nc


def _chunked_vec(v):
    return np.ascontiguousarray(v.reshape(KC, 128).T).astype(np.float32)


def _tile_wT(w):
    wt = np.ascontiguousarray(w.T)
    x = wt.reshape(KC, 128, KC, 128)
    return np.ascontiguousarray(np.transpose(x, (2, 1, 0, 3))).reshape(KC, 128, KC * 128).astype(np.float32)


def prepare_in_maps(inputs):
    x_full = np.asarray(inputs["L0_attn_a_hidden_states"], dtype=np.float32)

    wq = _tile_wT(np.asarray(inputs["L0_attn_a_Wq"], dtype=np.float32))
    wk = _tile_wT(np.asarray(inputs["L0_attn_a_Wk"], dtype=np.float32))
    wv = _tile_wT(np.asarray(inputs["L0_attn_a_Wv"], dtype=np.float32))
    wqa = _tile_wT(np.asarray(inputs["L0_attn_a_Wqa"], dtype=np.float32))
    wka = _tile_wT(np.asarray(inputs["L0_attn_a_Wka"], dtype=np.float32))
    bq = _chunked_vec(np.asarray(inputs["L0_attn_a_bq"], dtype=np.float32))
    bk = _chunked_vec(np.asarray(inputs["L0_attn_a_bk"], dtype=np.float32))
    bv = _chunked_vec(np.asarray(inputs["L0_attn_a_bv"], dtype=np.float32))
    bqa = _chunked_vec(np.asarray(inputs["L0_attn_a_bqa"], dtype=np.float32))
    bka = _chunked_vec(np.asarray(inputs["L0_attn_a_bka"], dtype=np.float32))
    g_row = np.asarray(inputs["L0_attn_a_ln_g"], dtype=np.float32).reshape(1, D)
    b_row = np.asarray(inputs["L0_attn_a_ln_b"], dtype=np.float32).reshape(1, D)
    ones_l = np.ones((1, 128), dtype=ml_dtypes.bfloat16)
    idn = np.eye(128, dtype=np.float32).astype(ml_dtypes.bfloat16)

    wbase_T = np.ascontiguousarray(np.asarray(inputs["L0_ffn_W_base"], dtype=np.float32).T)
    wbase_t = np.ascontiguousarray(
        np.transpose(wbase_T.reshape(KC, 128, 2, 1024), (2, 1, 0, 3))).reshape(2, 128, KC * 1024)
    rt = np.transpose(np.ascontiguousarray(
        np.asarray(inputs["L0_ffn_W_router"], dtype=np.float32).T).reshape(KC, 128, 8), (1, 0, 2))
    a_flat = np.asarray(inputs["L0_ffn_A_w"], dtype=np.float32).reshape(64, D)
    at = np.transpose(np.ascontiguousarray(a_flat.T).reshape(KC, 128, 64), (1, 0, 2))
    rat = np.ascontiguousarray(np.concatenate([rt, at], axis=2)).astype(np.float32)
    bt = np.ascontiguousarray(
        np.transpose(np.asarray(inputs["L0_ffn_B_w"], dtype=np.float32), (0, 2, 1)).reshape(64, D))
    bb = np.asarray(inputs["L0_ffn_b_base"], dtype=np.float32).reshape(1, D)

    def _tile_optA(w):
        wt = np.ascontiguousarray(np.asarray(w, dtype=np.float32).T)
        return np.ascontiguousarray(
            np.transpose(wt.reshape(KC, 128, 2, 1024), (2, 1, 0, 3))).reshape(2, 128, KC * 1024)

    wqkv = np.asarray(inputs["L0_attn_b_w_qkv"], dtype=np.float32)
    qkvw_t = np.stack([_tile_optA(wqkv[s * D:(s + 1) * D, :]) for s in range(3)])
    wgate_t = _tile_optA(inputs["L0_attn_b_w_gate"])
    wout_t = _tile_optA(inputs["L0_attn_b_w_out"])
    slope_col = np.ascontiguousarray(np.repeat(
        np.asarray(inputs["L0_attn_b_slope_rate"], dtype=np.float32).reshape(1, 16), 128, axis=0))
    tloc = (np.arange(2)[None, :] * 128 + np.arange(128)[:, None]).astype(np.float32)
    iota1 = np.ascontiguousarray(tloc + 1.0)
    iota2 = np.ascontiguousarray(255.0 - tloc)
    c256 = np.full((128, 1), 256.0, dtype=np.float32)
    iota_bj = np.ascontiguousarray(
        np.repeat((np.arange(8)[None, :] * 256.0).astype(np.float32), 128, axis=0))
    t2i = (np.arange(2)[:, None, None] * 128 + np.arange(128)[None, :, None])
    t1i = np.arange(256)[None, None, :]
    ind_c = np.ascontiguousarray(
        np.transpose((t1i >= t2i).astype(np.float32), (1, 0, 2)).reshape(128, 512)
    ).astype(ml_dtypes.bfloat16)
    t2g = (np.arange(2)[:, None, None] * 128 + np.arange(128)[None, :, None]).astype(np.float32)
    t1g = np.arange(256)[None, None, :].astype(np.float32)
    cmm = np.where(t1g >= t2g, t2g + 1.0, -1e30).astype(np.float32)
    cm = np.ascontiguousarray(np.transpose(cmm, (1, 0, 2)).reshape(128, 512))
    nw = np.asarray(inputs["L0_attn_b_norm_w"], dtype=np.float32).reshape(1, D)

    in_maps = []
    for c in range(NCORES):
        b, half = c // 2, c % 2
        flag = np.full((128, 1), float(half), dtype=np.float32)
        x_sl = np.ascontiguousarray(x_full[b, half * T:(half + 1) * T, :])
        in_maps.append(dict(
            x_in=x_sl, flag=flag, ones_lhsT=ones_l,
            wq_t=wq, wk_t=wk, wv_t=wv, wqa_t=wqa, wka_t=wka,
            bq_c=bq, bk_c=bk, bv_c=bv, bqa_c=bqa, bka_c=bka,
            g_row=g_row, b_row=b_row,
            wbase_t=wbase_t, rat=rat, bt=bt, bb_in=bb, idn=idn,
            qkvw_t=qkvw_t, wgate_t=wgate_t, wout_t=wout_t, slope_col=slope_col,
            iota1=iota1, iota2=iota2, c256=c256, cm=cm, iota_bj=iota_bj, ind_c=ind_c,
            nw_in_row=nw,
        ))
    return in_maps


def kernel(**inputs):
    in_maps = prepare_in_maps(inputs)
    nc = build_program()
    res = run_bass_kernel_spmd(nc, in_maps, core_ids=list(range(NCORES)))
    global LAST_RESULTS
    LAST_RESULTS = res.results
    out = np.empty((4, 4096, D), dtype=np.float32)
    for c in range(NCORES):
        b, half = c // 2, c % 2
        out[b, half * T:(half + 1) * T, :] = np.asarray(res.results[c]["out"], dtype=np.float32)
    return out


# version 15 (final)
# speedup vs baseline: 1.1712x; 1.0925x over previous
import numpy as np
import ml_dtypes
from contextlib import ExitStack

import concourse.bass as bass
import concourse.mybir as mybir
from concourse.bass_utils import run_bass_kernel_spmd

F32 = mybir.dt.float32
BF16 = mybir.dt.bfloat16
AF = mybir.ActivationFunctionType
ALU = mybir.AluOpType

NCORES = 8
T = 2048
D = 2048
KC = 16
TBW = 512
NTB = T // TBW
NG = KC * NTB
LN_EPS = 1e-5
ATT_SCALE = 1.0 / float(np.sqrt(np.float32(2048.0)))

DEBUG_TAPS = False
LAST_RESULTS = None
RACE_DETECT = False
N_SUB = 4


class Ctr:
    def __init__(self, sem):
        self.sem = sem
        self.n = 0

    def inc(self, inst, k=1):
        inst.then_inc(self.sem, k)
        self.n += k
        return self.n


def build_program():
    nc = bass.Bass("TRN2", target_bir_lowering=False, detect_race_conditions=RACE_DETECT)
    tap = "ExternalOutput" if DEBUG_TAPS else "Internal"

    x_in = nc.dram_tensor("x_in", [T, D], F32, kind="ExternalInput").ap()
    flag_in = nc.dram_tensor("flag", [128, 1], F32, kind="ExternalInput").ap()
    ones_in = nc.dram_tensor("ones_lhsT", [1, 128], BF16, kind="ExternalInput").ap()
    wq_in = nc.dram_tensor("wq_t", [KC, 128, D], F32, kind="ExternalInput").ap()
    wk_in = nc.dram_tensor("wk_t", [KC, 128, D], F32, kind="ExternalInput").ap()
    wv_in = nc.dram_tensor("wv_t", [KC, 128, D], F32, kind="ExternalInput").ap()
    wqa_in = nc.dram_tensor("wqa_t", [KC, 128, D], F32, kind="ExternalInput").ap()
    wka_in = nc.dram_tensor("wka_t", [KC, 128, D], F32, kind="ExternalInput").ap()
    bq_in = nc.dram_tensor("bq_c", [128, KC], F32, kind="ExternalInput").ap()
    bk_in = nc.dram_tensor("bk_c", [128, KC], F32, kind="ExternalInput").ap()
    bv_in = nc.dram_tensor("bv_c", [128, KC], F32, kind="ExternalInput").ap()
    bqa_in = nc.dram_tensor("bqa_c", [128, KC], F32, kind="ExternalInput").ap()
    bka_in = nc.dram_tensor("bka_c", [128, KC], F32, kind="ExternalInput").ap()
    g_in = nc.dram_tensor("g_row", [1, D], F32, kind="ExternalInput").ap()
    bln_in = nc.dram_tensor("b_row", [1, D], F32, kind="ExternalInput").ap()
    wbase_in = nc.dram_tensor("wbase_t", [2, 128, KC * 1024], F32, kind="ExternalInput").ap()
    rat_in = nc.dram_tensor("rat", [128, KC, 72], F32, kind="ExternalInput").ap()
    bt_in = nc.dram_tensor("bt", [64, D], F32, kind="ExternalInput").ap()
    bb_in = nc.dram_tensor("bb_in", [1, D], F32, kind="ExternalInput").ap()
    idn_in = nc.dram_tensor("idn", [128, 128], BF16, kind="ExternalInput").ap()
    qkvw_in = nc.dram_tensor("qkvw_t", [3, 2, 128, KC * 1024], F32, kind="ExternalInput").ap()
    wgate_in = nc.dram_tensor("wgate_t", [2, 128, KC * 1024], F32, kind="ExternalInput").ap()
    wout_in = nc.dram_tensor("wout_t", [2, 128, KC * 1024], F32, kind="ExternalInput").ap()
    slope_in = nc.dram_tensor("slope_col", [128, 16], F32, kind="ExternalInput").ap()
    iota1_in = nc.dram_tensor("iota1", [128, 2], F32, kind="ExternalInput").ap()
    iota2_in = nc.dram_tensor("iota2", [128, 2], F32, kind="ExternalInput").ap()
    c256_in = nc.dram_tensor("c256", [128, 1], F32, kind="ExternalInput").ap()
    cm_in = nc.dram_tensor("cm", [128, 2 * 256], F32, kind="ExternalInput").ap()
    iotabj_in = nc.dram_tensor("iota_bj", [128, 8], F32, kind="ExternalInput").ap()
    ind_in = nc.dram_tensor("ind_c", [128, 512], BF16, kind="ExternalInput").ap()
    nw_in = nc.dram_tensor("nw_in_row", [1, D], F32, kind="ExternalInput").ap()

    out_ap = nc.dram_tensor("out", [T, D], F32, kind="ExternalOutput").ap()

    u_sp = nc.dram_tensor("u_sp", [T, D], BF16, kind=tap).ap()
    q_sp = nc.dram_tensor("q_sp", [D, T], BF16, kind=tap).ap()
    k_sp = nc.dram_tensor("k_sp", [D, T], BF16, kind=tap).ap()
    v_sp = nc.dram_tensor("v_sp", [D, T], BF16, kind=tap).ap()
    w_sp = nc.dram_tensor("w_sp", [D, T], BF16, kind=tap).ap()
    hi_sp = nc.dram_tensor("hi_sp", [D, T], BF16, kind=tap).ap()
    lo_sp = nc.dram_tensor("lo_sp", [D, T], BF16, kind=tap).ap()
    x1_sp = nc.dram_tensor("x1_sp", [T, D], F32, kind="Internal").ap()
    x2_sp = nc.dram_tensor("x2_sp", [T, D], F32, kind="Internal").ap()
    x3_sp = nc.dram_tensor("x3_sp", [T, D], F32, kind="Internal").ap()
    dbg1 = nc.dram_tensor("dbg1", [128, D], BF16, kind=tap).ap()
    dbg2 = nc.dram_tensor("dbg2", [128, D], BF16, kind=tap).ap()
    dbg3 = nc.dram_tensor("dbg3", [128, 32], F32, kind=tap).ap()
    dbg4 = nc.dram_tensor("dbg4", [128, D], F32, kind=tap).ap()
    dbg5 = nc.dram_tensor("dbg5", [128, D], F32, kind=tap).ap()
    cc_in = nc.dram_tensor("cc_in", [128, KC], F32, kind="Internal")
    cc_out = nc.dram_tensor("cc_out", [256, KC], F32, kind="Internal")
    bq_sp = nc.dram_tensor("bq_sp", [T, D], BF16, kind="Internal").ap()
    bk_sp = nc.dram_tensor("bk_sp", [T, D], BF16, kind="Internal").ap()
    bv_sp = nc.dram_tensor("bv_sp", [T, D], BF16, kind="Internal").ap()
    gate_sp = nc.dram_tensor("gate_sp", [T, D], BF16, kind="Internal").ap()
    ao_sp = nc.dram_tensor("ao_sp", [T, D], BF16, kind="Internal").ap()
    gn_sp = nc.dram_tensor("gn_sp", [T, D], BF16, kind="Internal").ap()
    cc2_in = nc.dram_tensor("cc2_in", [128, D], F32, kind="Internal")
    cc2_out = nc.dram_tensor("cc2_out", [256, D], F32, kind="Internal")

    with ExitStack() as stk:
        e = stk.enter_context

        cPE = Ctr(e(nc.semaphore("pe")))
        cACT = Ctr(e(nc.semaphore("act")))
        cDVE = Ctr(e(nc.semaphore("dve")))
        cPOOL = Ctr(e(nc.semaphore("pool")))
        cCC = Ctr(e(nc.semaphore("cc")))
        cCI = Ctr(e(nc.semaphore("cc_in_st")))
        cGA = Ctr(e(nc.semaphore("gath_ld")))
        sIN = Ctr(e(nc.semaphore("s_init")))
        sXL = [Ctr(e(nc.semaphore(f"s_xl{i}"))) for i in range(2)]
        sST = [Ctr(e(nc.semaphore(f"s_st{i}"))) for i in range(2)]
        sWS = [Ctr(e(nc.semaphore(f"s_ws{i}"))) for i in range(2)]
        sRG = [Ctr(e(nc.semaphore(f"s_rg{i}"))) for i in range(2)]
        sCH = [Ctr(e(nc.semaphore(f"s_ch{i}"))) for i in range(2)]
        sSP = [Ctr(e(nc.semaphore(f"s_sp{i}"))) for i in range(2)]
        sXB = [Ctr(e(nc.semaphore(f"s_xb{i}"))) for i in range(2)]
        cQR = Ctr(e(nc.semaphore("s_qr")))
        sING = Ctr(e(nc.semaphore("s_init_g")))

        flag_t = e(nc.sbuf_tensor("flag_t", [128, 1], F32))
        ones_t = e(nc.sbuf_tensor("ones_t", [1, 128], BF16))
        bias_c = e(nc.sbuf_tensor("bias_c", [128, 5 * KC], F32))
        g_bc = e(nc.sbuf_tensor("g_bc", [128, D], BF16))
        b_bc = e(nc.sbuf_tensor("b_bc", [128, D], BF16))
        zeros_bf = e(nc.sbuf_tensor("zeros_bf", [128, D], BF16))
        stt = e(nc.sbuf_tensor("stt", [128, 32], F32))
        colA = e(nc.sbuf_tensor("colA", [128, KC, NTB], F32))
        colS = e(nc.sbuf_tensor("colS", [128, KC], F32))
        gath = e(nc.sbuf_tensor("gath", [128, KC], F32))
        ceff = e(nc.sbuf_tensor("ceff", [128, 4 * KC], F32))
        eps_t = e(nc.sbuf_tensor("eps_t", [128, 1], F32))
        eps2_t = e(nc.sbuf_tensor("eps2_t", [128, 1], F32))

        x_t2 = e(nc.sbuf_tensor("x_t2", [128, 2, D], F32))
        mvA = e(nc.sbuf_tensor("mvA", [128, KC, T], BF16))
        ch_re = e(nc.sbuf_tensor("ch_re", [128, 2, T], BF16))
        colT = e(nc.sbuf_tensor("colT", [128, KC], F32))
        scr1 = e(nc.sbuf_tensor("scr1", [128, 1], F32))
        out_t2 = e(nc.sbuf_tensor("out_t2", [128, 2, T], F32))
        astk = ExitStack()
        ae = astk.enter_context
        u1_scr = ae(nc.sbuf_tensor("u1_scr", [128, D], F32))
        h1_scr = ae(nc.sbuf_tensor("h1_scr", [128, 2, D], BF16))
        wstrip = ae(nc.sbuf_tensor("wstrip", [128, 2, KC, 128], BF16))
        kv_stage = ae(nc.sbuf_tensor("kv_stage", [128, 2, TBW], BF16))
        ringA = ae(nc.sbuf_tensor("ringA", [128, 2, T], BF16))
        tmp_f = ae(nc.sbuf_tensor("tmp_f", [128, 2, T], F32))
        hi_st = ae(nc.sbuf_tensor("hi_st", [128, 2, T], BF16))
        lo_st = ae(nc.sbuf_tensor("lo_st", [128, 2, T], BF16))
        hi_tm = ae(nc.sbuf_tensor("hi_tm", [128, 2, D], BF16))
        lo_tm = ae(nc.sbuf_tensor("lo_tm", [128, 2, D], BF16))

        pbank = [e(nc.psum_tensor(f"pb{i}", [128, TBW], F32)) for i in range(7)]
        pbank_bf = e(nc.psum_tensor("pbbf", [128, 1024], BF16))
        NB = 7
        bank_last = [(cACT, 0)] * NB
        bf_last = [cACT, 0]
        wres_pe = {"n": 0}

        all_ctrs = ([cPE, cACT, cDVE, cPOOL, cCC, cCI, cGA, sIN, cQR, sING]
                    + sXL + sST + sWS + sRG + sCH + sSP + sXB)
        with nc.Block() as blk:
            @blk.gpsimd
            def _(gp):
                for ctr in all_ctrs:
                    gp.sem_clear(ctr.sem)
        nc.all_engine_barrier()

        with ExitStack() as istk:
            grow_b = istk.enter_context(nc.sbuf_tensor("grow_b", [1, D], BF16))
            brow_b = istk.enter_context(nc.sbuf_tensor("brow_b", [1, D], BF16))

            with nc.Block() as blk:
                @blk.sync
                def _(sync):
                    for dst, src in ((flag_t[:, :], flag_in), (ones_t[:, :], ones_in),
                                     (bias_c[:, 0:KC], bq_in), (bias_c[:, KC:2 * KC], bk_in),
                                     (bias_c[:, 2 * KC:3 * KC], bv_in),
                                     (bias_c[:, 3 * KC:4 * KC], bqa_in),
                                     (bias_c[:, 4 * KC:5 * KC], bka_in)):
                        sIN.inc(sync.dma_start(out=dst, in_=src), 16)

                @blk.gpsimd
                def _(gp):
                    sING.inc(gp.dma_start(out=grow_b[:, :], in_=g_in), 16)
                    sING.inc(gp.dma_start(out=brow_b[:, :], in_=bln_in), 16)
                init_in = sIN.n
                init_ing = sING.n

                @blk.vector
                def _(v):
                    v.memset(zeros_bf[:, :], 0.0)
                    v.memset(eps_t[:, :], LN_EPS)
                    v.memset(eps2_t[:, :], LN_EPS * LN_EPS)
                    v.wait_ge(sIN.sem, init_in)
                    v.wait_ge(sING.sem, init_ing)
                    v.tensor_scalar_mul(bias_c[:, 3 * KC:4 * KC], bias_c[:, 3 * KC:4 * KC], ATT_SCALE)
                    cDVE.inc(v.tensor_scalar_mul(bias_c[:, 4 * KC:5 * KC], bias_c[:, 4 * KC:5 * KC],
                                                 ATT_SCALE))

                @blk.tensor
                def _(t):
                    t.wait_ge(sIN.sem, init_in)
                    t.wait_ge(sING.sem, init_ing)
                    inst = None
                    for j in range(NTB):
                        inst = t.matmul(pbank[j][:, :], ones_t[0:1, :],
                                        grow_b[0:1, j * TBW:(j + 1) * TBW], start=True, stop=True)
                    pe_g = cPE.inc(inst)
                    t.wait_ge(cACT.sem, cACT.n + 1)
                    inst = None
                    for j in range(NTB):
                        inst = t.matmul(pbank[j][:, :], ones_t[0:1, :],
                                        brow_b[0:1, j * TBW:(j + 1) * TBW], start=True, stop=True)
                    cPE.inc(inst)

                @blk.scalar
                def _(s):
                    s.wait_ge(cPE.sem, cPE.n - 1)
                    inst = None
                    for j in range(NTB):
                        inst = s.copy(g_bc[:, j * TBW:(j + 1) * TBW], pbank[j][:, :])
                    cACT.inc(inst)
                    s.wait_ge(cPE.sem, cPE.n)
                    inst = None
                    for j in range(NTB):
                        inst = s.copy(b_bc[:, j * TBW:(j + 1) * TBW], pbank[j][:, :])
                    cACT.inc(inst)
                for b_ in range(NB):
                    bank_last[b_] = (cACT, cACT.n)

        if DEBUG_TAPS:
            with nc.Block() as blk:
                @blk.sync
                def _(sync):
                    sync.wait_ge(cACT.sem, cACT.n)
                    sIN.inc(sync.dma_start(out=dbg1, in_=g_bc[:, :]), 16)
                    sIN.inc(sync.dma_start(out=dbg2, in_=b_bc[:, :]), 16)

        with nc.Block() as blk:
            v0, a0, p0 = cDVE.n, cACT.n, cPOOL.n
            xl0 = [sXL[0].n, sXL[1].n]
            st0 = [sST[0].n, sST[1].n]

            @blk.sync
            def _(sync):
                def store(j):
                    sync.wait_ge(cPOOL.sem, p0 + j + 1)
                    sST[j % 2].inc(sync.dma_start(out=u_sp[j * 128:(j + 1) * 128, :],
                                                  in_=h1_scr[:, j % 2]), 16)
                for i in range(KC):
                    if i >= 2:
                        sync.wait_ge(cDVE.sem, v0 + (i - 2) * 9 + 8)
                    sXL[i % 2].inc(sync.dma_start(out=x_t2[:, i % 2],
                                                  in_=x_in[i * 128:(i + 1) * 128, :]), 16)
                    if i >= 1:
                        store(i - 1)
                store(KC - 1)

            @blk.vector
            def _(v):
                for i in range(KC):
                    pi = (i % 2) * 16
                    xt = x_t2[:, i % 2]
                    hs = h1_scr[:, i % 2]
                    v.wait_ge(sXL[i % 2].sem, xl0[i % 2] + (i // 2 + 1) * 16)
                    if i >= 2:
                        v.wait_ge(cACT.sem, a0 + (i - 2) * 2 + 2)
                        v.wait_ge(cPOOL.sem, p0 + i - 1)
                        v.wait_ge(sST[i % 2].sem, st0[i % 2] + ((i - 2) // 2 + 1) * 16)
                    n = {}
                    def op(key, inst):
                        n[key] = cDVE.inc(inst)
                    op(1, v.tensor_scalar(hs, xt, 1.0, 0.0, ALU.mult, ALU.add,
                                          accum_out=stt[:, pi + 0:pi + 1]))
                    op(2, v.scalar_tensor_tensor(ch_re[:, 0, 0:D], xt, 1.0, xt, ALU.mult, ALU.mult,
                                                 accum_out=stt[:, pi + 1:pi + 2]))
                    v.wait_ge(cDVE.sem, n[1])
                    op(3, v.tensor_scalar_mul(stt[:, pi + 2:pi + 3], stt[:, pi + 0:pi + 1], 1.0 / D))
                    v.wait_ge(cDVE.sem, n[2])
                    op(4, v.tensor_scalar_mul(stt[:, pi + 3:pi + 4], stt[:, pi + 1:pi + 2], 1.0 / D))
                    v.wait_ge(cDVE.sem, n[3])
                    op(5, v.tensor_tensor(stt[:, pi + 4:pi + 5], stt[:, pi + 2:pi + 3],
                                          stt[:, pi + 2:pi + 3], ALU.mult))
                    v.wait_ge(cDVE.sem, n[5])
                    op(6, v.tensor_tensor(stt[:, pi + 5:pi + 6], stt[:, pi + 3:pi + 4],
                                          stt[:, pi + 4:pi + 5], ALU.subtract))
                    v.wait_ge(cDVE.sem, n[6])
                    op(7, v.tensor_scalar_mul(stt[:, pi + 8:pi + 9], stt[:, pi + 5:pi + 6],
                                              1.0 + LN_EPS))
                    v.wait_ge(cACT.sem, a0 + i * 2 + 2)
                    op(8, v.tensor_scalar(u1_scr[:, :], xt, stt[:, pi + 2:pi + 3], stt[:, pi + 6:pi + 7],
                                          ALU.subtract, ALU.mult))
                    v.wait_ge(cDVE.sem, n[8])
                    op(9, v.scalar_tensor_tensor(hs, u1_scr[:, :], 1.0, g_bc[:, :],
                                                 ALU.mult, ALU.mult))

            @blk.scalar
            def _(s):
                for i in range(KC):
                    pi = (i % 2) * 16
                    s.wait_ge(cDVE.sem, v0 + i * 9 + 7)
                    na = cACT.inc(s.activation(stt[:, pi + 7:pi + 8], stt[:, pi + 8:pi + 9], AF.Ln,
                                               bias=eps2_t[:, 0:1]))
                    s.wait_ge(cACT.sem, na)
                    cACT.inc(s.activation(stt[:, pi + 6:pi + 7], stt[:, pi + 7:pi + 8], AF.Exp, scale=-0.5))

            @blk.gpsimd
            def _(gp):
                for i in range(KC):
                    gp.wait_ge(cDVE.sem, v0 + i * 9 + 9)
                    if i >= 2:
                        gp.wait_ge(sST[i % 2].sem, st0[i % 2] + ((i - 2) // 2 + 1) * 16)
                    cPOOL.inc(gp.tensor_tensor(h1_scr[:, i % 2], h1_scr[:, i % 2], b_bc[:, :], ALU.add))

        u_stores = (sST[0].n, sST[1].n)

        def proj_B(w_dram, mov, evac, mov_ready=(), extra_stage_wait=None, post_sync=None):
            pe0, act0 = cPE.n, cACT.n
            ws0 = [sWS[0].n, sWS[1].n]
            prev_bank = list(bank_last)

            with nc.Block() as blk:
                @blk.gpsimd
                def _(gp):
                    for oc in range(KC):
                        if oc >= 2:
                            gp.wait_ge(cPE.sem, pe0 + (oc - 1) * NTB)
                        else:
                            gp.wait_ge(cPE.sem, pe0)
                        sWS[oc % 2].inc(gp.dma_start(
                            out=wstrip[:, oc % 2, :, :].rearrange("p k o -> p (k o)"),
                            in_=w_dram[oc]), 16)

                @blk.tensor
                def _(t):
                    for sem, val in mov_ready:
                        t.wait_ge(sem, val)
                    for oc in range(KC):
                        t.wait_ge(sWS[oc % 2].sem, ws0[oc % 2] + (oc // 2 + 1) * 16)
                        for tb in range(NTB):
                            g = oc * NTB + tb
                            if g >= NB:
                                t.wait_ge(cACT.sem, act0 + g - NB + 1)
                            else:
                                bc, bval = prev_bank[g % NB]
                                t.wait_ge(bc.sem, bval)
                            inst = None
                            for k in range(KC):
                                inst = t.matmul(pbank[g % NB][:, :], wstrip[:, oc % 2, k, :],
                                                mov[:, k, tb * TBW:(tb + 1) * TBW],
                                                start=(k == 0), stop=(k == KC - 1))
                            cPE.inc(inst)

                @blk.scalar
                def _(s):
                    for oc in range(KC):
                        for tb in range(NTB):
                            g = oc * NTB + tb
                            s.wait_ge(cPE.sem, pe0 + g + 1)
                            if extra_stage_wait is not None:
                                extra_stage_wait(s, g)
                            cACT.inc(evac(s, oc, tb, pbank[g % NB]))
                            bank_last[g % NB] = (cACT, cACT.n)

                if post_sync is not None:
                    @blk.sync
                    def _(sync):
                        post_sync(sync)

        def proj_spill(w_dram, mov, spill, bias_off, func, scale, accum, mov_ready=()):
            act0_local = cACT.n
            sp0 = [sSP[0].n, sSP[1].n]

            def evac(s, oc, tb, bank):
                g = oc * NTB + tb
                kw = {}
                if accum:
                    kw["accum_out"] = colA[:, oc, tb:tb + 1]
                return s.activation(kv_stage[:, g % 2], bank[:, :], func,
                                    bias=bias_c[:, bias_off + oc:bias_off + oc + 1],
                                    scale=scale, **kw)

            def stage_wait(s, g):
                if g >= 2:
                    s.wait_ge(sSP[g % 2].sem, sp0[g % 2] + ((g - 2) // 2 + 1) * 16)
                else:
                    s.wait_ge(sSP[g % 2].sem, sp0[g % 2])

            def post_sync(sync):
                for oc in range(KC):
                    for tb in range(NTB):
                        g = oc * NTB + tb
                        sync.wait_ge(cACT.sem, act0_local + g + 1)
                        sSP[g % 2].inc(sync.dma_start(
                            out=spill[oc * 128:(oc + 1) * 128, tb * TBW:(tb + 1) * TBW],
                            in_=kv_stage[:, g % 2]), 16)

            proj_B(w_dram, mov, evac, mov_ready=mov_ready,
                   extra_stage_wait=stage_wait, post_sync=post_sync)
            return (sSP[0].n, sSP[1].n)

        with nc.Block() as blk:
            @blk.scalar
            def _(s):
                s.wait_ge(sST[0].sem, u_stores[0])
                s.wait_ge(sST[1].sem, u_stores[1])
                for c in range(KC):
                    sXB[c % 2].inc(s.dma_start(out=mvA[:, c, :],
                                               in_=u_sp[:, c * 128:(c + 1) * 128], transpose=True), 16)
        h_ready = (sXB[0].n, sXB[1].n)

        hrdy = [(sXB[0].sem, h_ready[0]), (sXB[1].sem, h_ready[1])]
        q_stores = proj_spill(wq_in, mvA, q_sp, 0, AF.Identity, 1.0, False, mov_ready=hrdy)
        k_stores = proj_spill(wk_in, mvA, k_sp, KC, AF.Identity, 1.0, False)
        v_stores = proj_spill(wv_in, mvA, v_sp, 2 * KC, AF.Identity, 1.0, False)
        v_pe_done = cPE.n

        with nc.Block() as blk:
            @blk.sync
            def _(sync):
                sync.wait_ge(cPE.sem, v_pe_done)
                sync.wait_ge(sSP[0].sem, sSP[0].n)
                sync.wait_ge(sSP[1].sem, sSP[1].n)
                for c in range(KC):
                    cQR.inc(sync.dma_start(out=mvA[:, c, :], in_=q_sp[c * 128:(c + 1) * 128, :]), 16)
        q_loaded = cQR.n

        qw_stores = proj_spill(wqa_in, mvA, w_sp, 3 * KC, AF.Exp, ATT_SCALE, True,
                               mov_ready=[(cQR.sem, q_loaded)])
        a2_act = cACT.n

        def exchange(slot, wait_sem_val, from_colA=True):
            with nc.Block() as blk:
                if from_colA:
                    @blk.vector
                    def _(v):
                        v.wait_ge(*wait_sem_val)
                        cDVE.inc(v.tensor_tensor(colS[:, :], colA[:, :, 0], colA[:, :, 1], ALU.add))
                        nx_ = cDVE.inc(v.tensor_tensor(colT[:, :], colA[:, :, 2], colA[:, :, 3], ALU.add))
                        v.wait_ge(cDVE.sem, nx_)
                        cDVE.inc(v.tensor_tensor(colS[:, :], colS[:, :], colT[:, :], ALU.add))
                dve_sum = cDVE.n

                @blk.sync
                def _(sync):
                    if from_colA:
                        sync.wait_ge(cDVE.sem, dve_sum)
                    else:
                        sync.wait_ge(*wait_sem_val)
                    if slot > 0:
                        sync.wait_ge(cCC.sem, slot)
                    cCI.inc(sync.dma_start(out=cc_in[:, :], in_=colS[:, :]), 16)
                ci_val = cCI.n

                @blk.gpsimd
                def _(gp):
                    gp.wait_ge(cCI.sem, ci_val)
                    if slot > 0:
                        gp.wait_ge(cGA.sem, slot * 16)
                    cCC.inc(gp.collective_compute(
                        "AllGather", ALU.bypass,
                        replica_groups=[[0, 1], [2, 3], [4, 5], [6, 7]],
                        ins=[cc_in.ap().opt()], outs=[cc_out.ap().opt()]))
                    gp.wait_ge(cCC.sem, cCC.n)
                    cGA.inc(gp.dma_start(out=gath[:, :], in_=cc_out[0:128, :]), 16)
                ga_val = cGA.n

                @blk.vector
                def _(v):
                    v.wait_ge(cGA.sem, ga_val)
                    cDVE.inc(v.tensor_scalar_mul(ceff[:, slot * KC:(slot + 1) * KC],
                                                 gath[:, :], flag_t[:, 0:1]))
            return cDVE.n

        e1a_dve = exchange(0, (cACT.sem, a2_act))

        def scan_pass(src_sp, src_stores, ceff_slot, start_dve):
            with nc.Block() as blk:
                v0 = cDVE.n
                a0 = cACT.n
                rg0 = [sRG[0].n, sRG[1].n]

                sp_tot = (sSP[0].n, sSP[1].n)

                @blk.sync
                def _(sync):
                    sync.wait_ge(sSP[0].sem, sp_tot[0])
                    sync.wait_ge(sSP[1].sem, sp_tot[1])
                    for c in range(KC):
                        p = c // 2
                        if p >= 1:
                            sync.wait_ge(cDVE.sem, v0 + p * 6)
                        sRG[c % 2].inc(sync.dma_start(out=ringA[:, c % 2],
                                                      in_=src_sp[c * 128:(c + 1) * 128, :]), 16)

                @blk.scalar
                def _(s):
                    for cp in range(0, KC, 2):
                        p = cp // 2
                        for c in (cp, cp + 1):
                            s.wait_ge(cDVE.sem, v0 + p * 6 + (c - cp) + 1)
                            na = cACT.inc(s.activation(u1_scr[:, :], tmp_f[:, c % 2], AF.Ln))
                            s.wait_ge(cACT.sem, na)
                            cACT.inc(s.activation(tmp_f[:, c % 2], u1_scr[:, :],
                                                  AF.Exp, scale=-1.0))

                @blk.vector
                def _(v):
                    v.wait_ge(cDVE.sem, start_dve)
                    v.engine_nop()
                    prev_mul = {}
                    for cp in range(0, KC, 2):
                        p = cp // 2
                        ns = {}
                        for c in (cp, cp + 1):
                            v.wait_ge(sRG[c % 2].sem, rg0[c % 2] + (c // 2 + 1) * 16)
                            if c % 2 in prev_mul:
                                v.wait_ge(cDVE.sem, prev_mul[c % 2])
                            ns[("s", c)] = cDVE.inc(v.tensor_tensor_scan(
                                tmp_f[:, c % 2], ringA[:, c % 2], zeros_bf[:, :],
                                ceff[:, ceff_slot * KC + c:ceff_slot * KC + c + 1], ALU.add, ALU.add))
                        for c in (cp, cp + 1):
                            v.wait_ge(cACT.sem, a0 + p * 4 + (c - cp + 1) * 2)
                            ns[("m", c)] = cDVE.inc(v.tensor_tensor(ringA[:, c % 2], ringA[:, c % 2],
                                                                    tmp_f[:, c % 2], ALU.mult))
                            prev_mul[c % 2] = ns[("m", c)]
                        for c in (cp, cp + 1):
                            v.wait_ge(cDVE.sem, ns[("m", c)])
                            cDVE.inc(v.scalar_tensor_tensor(mvA[:, c, :], ringA[:, c % 2], 1.0, mvA[:, c, :],
                                                            ALU.mult, ALU.mult, accum_out=colS[:, c:c + 1]))
            return cDVE.n

        a3_dve = scan_pass(w_sp, qw_stores, 0, e1a_dve)

        e1b_dve = exchange(1, (cDVE.sem, a3_dve), from_colA=False)

        def scan_mul_pass(src_sp, src_stores, ceff_slot, start_dve):
            with nc.Block() as blk:
                v0 = cDVE.n
                ch0 = [sCH[0].n, sCH[1].n]

                sp_tot = (sSP[0].n, sSP[1].n)

                @blk.sync
                def _(sync):
                    sync.wait_ge(sSP[0].sem, sp_tot[0])
                    sync.wait_ge(sSP[1].sem, sp_tot[1])
                    for c in range(KC):
                        p = c // 2
                        if p >= 1:
                            sync.wait_ge(cDVE.sem, v0 + p * 4)
                        sCH[c % 2].inc(sync.dma_start(out=ch_re[:, c % 2],
                                                      in_=src_sp[c * 128:(c + 1) * 128, :]), 16)

                @blk.vector
                def _(v):
                    v.wait_ge(cDVE.sem, start_dve)
                    v.engine_nop()
                    prev_mul = {}
                    for cp in range(0, KC, 2):
                        ns = {}
                        for c in (cp, cp + 1):
                            if c % 2 in prev_mul:
                                v.wait_ge(cDVE.sem, prev_mul[c % 2])
                            ns[c] = cDVE.inc(v.tensor_tensor_scan(
                                tmp_f[:, c % 2], mvA[:, c, :], zeros_bf[:, :],
                                ceff[:, ceff_slot * KC + c:ceff_slot * KC + c + 1], ALU.add, ALU.add))
                        for c in (cp, cp + 1):
                            v.wait_ge(sCH[c % 2].sem, ch0[c % 2] + (c // 2 + 1) * 16)
                            v.wait_ge(cDVE.sem, ns[c])
                            prev_mul[c % 2] = cDVE.inc(v.tensor_tensor(mvA[:, c, :], tmp_f[:, c % 2],
                                                                       ch_re[:, c % 2], ALU.mult))
            return cDVE.n

        a3b_dve = scan_mul_pass(k_sp, k_stores, 1, e1b_dve)

        kw_stores = proj_spill(wka_in, mvA, w_sp, 4 * KC, AF.Exp, ATT_SCALE, True,
                               mov_ready=[(cDVE.sem, a3b_dve)])
        a4_act = cACT.n

        e2a_dve = exchange(2, (cACT.sem, a4_act))

        a5a_dve = scan_pass(w_sp, kw_stores, 2, e2a_dve)

        e2b_dve = exchange(3, (cDVE.sem, a5a_dve), from_colA=False)

        with nc.Block() as blk:
            v0 = cDVE.n
            a0_ = cACT.n
            ch0 = [sCH[0].n, sCH[1].n]
            st0 = [sST[0].n, sST[1].n]

            @blk.sync
            def _(sync):
                def store(j):
                    sync.wait_ge(cACT.sem, a0_ + j + 1)
                    sync.wait_ge(cDVE.sem, v0 + (j // 2) * 6 + 5 + (j % 2))
                    sST[j % 2].inc(sync.dma_start(out=hi_sp[j * 128:(j + 1) * 128, :],
                                                  in_=hi_st[:, j % 2]), 16)
                    sST[j % 2].inc(sync.dma_start(out=lo_sp[j * 128:(j + 1) * 128, :],
                                                  in_=lo_st[:, j % 2]), 16)
                sync.wait_ge(sSP[0].sem, sSP[0].n)
                sync.wait_ge(sSP[1].sem, sSP[1].n)
                for c in range(KC):
                    p = c // 2
                    if p >= 1:
                        sync.wait_ge(cDVE.sem, v0 + p * 6)
                    sCH[c % 2].inc(sync.dma_start(out=ch_re[:, c % 2],
                                                  in_=v_sp[c * 128:(c + 1) * 128, :]), 16)
                    if c >= 1:
                        store(c - 1)
                store(KC - 1)

            @blk.vector
            def _(v):
                v.wait_ge(cDVE.sem, e2b_dve)
                v.engine_nop()
                prev_out = {}
                for cp in range(0, KC, 2):
                    ns = {}
                    for c in (cp, cp + 1):
                        if c % 2 in prev_out:
                            v.wait_ge(cDVE.sem, prev_out[c % 2])
                        ns[("s", c)] = cDVE.inc(v.tensor_tensor_scan(
                            tmp_f[:, c % 2], mvA[:, c, :], zeros_bf[:, :],
                            ceff[:, 3 * KC + c:3 * KC + c + 1], ALU.add, ALU.add))
                    for c in (cp, cp + 1):
                        v.wait_ge(sCH[c % 2].sem, ch0[c % 2] + (c // 2 + 1) * 16)
                        if c >= 2:
                            v.wait_ge(cACT.sem, a0_ + c - 1)
                        v.wait_ge(cDVE.sem, ns[("s", c)])
                        ns[("o", c)] = cDVE.inc(v.tensor_tensor(out_t2[:, c % 2], tmp_f[:, c % 2],
                                                                ch_re[:, c % 2], ALU.mult))
                        prev_out[c % 2] = ns[("o", c)]
                    for c in (cp, cp + 1):
                        v.wait_ge(cACT.sem, a0_ + c + 1)
                        if c >= 2:
                            v.wait_ge(sST[c % 2].sem, st0[c % 2] + ((c - 2) // 2 + 1) * 32)
                        v.wait_ge(cDVE.sem, ns[("o", c)])
                        cDVE.inc(v.tensor_tensor(lo_st[:, c % 2], out_t2[:, c % 2], hi_st[:, c % 2],
                                                 ALU.subtract))

            @blk.scalar
            def _(s):
                for c in range(KC):
                    s.wait_ge(cDVE.sem, v0 + (c // 2) * 6 + 3 + (c % 2))
                    if c >= 2:
                        s.wait_ge(sST[c % 2].sem, st0[c % 2] + ((c - 2) // 2 + 1) * 32)
                    cACT.inc(s.activation(hi_st[:, c % 2], out_t2[:, c % 2], AF.Copy))
        hilo_stores = (sST[0].n, sST[1].n)

        with nc.Block() as blk:
            v0 = cDVE.n
            xl0 = [sXL[0].n, sXL[1].n]
            st0 = [sST[0].n, sST[1].n]
            xb0 = [sXB[0].n, sXB[1].n]

            @blk.scalar
            def _(s):
                s.wait_ge(sST[0].sem, hilo_stores[0])
                s.wait_ge(sST[1].sem, hilo_stores[1])
                for i in range(KC):
                    if i >= 2:
                        s.wait_ge(cDVE.sem, v0 + (i - 2) * 2 + 1)
                    sXB[i % 2].inc(s.dma_start(out=hi_tm[:, i % 2],
                                               in_=hi_sp[:, i * 128:(i + 1) * 128], transpose=True), 16)
                    sXB[i % 2].inc(s.dma_start(out=lo_tm[:, i % 2],
                                               in_=lo_sp[:, i * 128:(i + 1) * 128], transpose=True), 16)

            @blk.sync
            def _(sync):
                def store(j):
                    sync.wait_ge(cDVE.sem, v0 + j * 2 + 2)
                    sST[j % 2].inc(sync.dma_start(out=x1_sp[j * 128:(j + 1) * 128, :],
                                                  in_=out_t2[:, j % 2]), 16)
                    if N_SUB == 1:
                        sST[j % 2].inc(sync.dma_start(out=out_ap[j * 128:(j + 1) * 128, :],
                                                      in_=out_t2[:, j % 2]), 16)
                for i in range(KC):
                    if i >= 2:
                        sync.wait_ge(cDVE.sem, v0 + (i - 2) * 2 + 2)
                    sXL[i % 2].inc(sync.dma_start(out=x_t2[:, i % 2],
                                                  in_=x_in[i * 128:(i + 1) * 128, :]), 16)
                    if i >= 1:
                        store(i - 1)
                store(KC - 1)
                sync.wait_ge(sST[0].sem, sST[0].n)
                sync.wait_ge(sST[1].sem, sST[1].n)

            @blk.vector
            def _(v):
                spt6 = 32 if N_SUB == 1 else 16
                for i in range(KC):
                    v.wait_ge(sXB[i % 2].sem, xb0[i % 2] + (i // 2 + 1) * 32)
                    v.wait_ge(sXL[i % 2].sem, xl0[i % 2] + (i // 2 + 1) * 16)
                    if i >= 2:
                        v.wait_ge(sST[i % 2].sem, st0[i % 2] + ((i - 2) // 2 + 1) * spt6)
                    nn_ = cDVE.inc(v.scalar_tensor_tensor(out_t2[:, i % 2], hi_tm[:, i % 2], 1.0,
                                                          lo_tm[:, i % 2], ALU.mult, ALU.add))
                    v.wait_ge(cDVE.sem, nn_)
                    cDVE.inc(v.tensor_tensor(out_t2[:, i % 2], out_t2[:, i % 2], x_t2[:, i % 2], ALU.add))

        astk.close()

        wres = e(nc.sbuf_tensor("wres", [128, KC, 1024], BF16))
        raf = e(nc.sbuf_tensor("raf", [128, KC, 72], F32))
        hamf = e(nc.sbuf_tensor("hamf", [64, T], BF16))
        bt_res = e(nc.sbuf_tensor("bt_res", [64, D], BF16))
        rat_t = e(nc.sbuf_tensor("rat_t", [128, KC, 72], BF16))
        bb_row = e(nc.sbuf_tensor("bb_row", [1, D], BF16))
        idn_t = e(nc.sbuf_tensor("idn_t", [128, 128], BF16))
        rsc = e(nc.sbuf_tensor("rsc", [128, 64], F32))
        ham_st = e(nc.sbuf_tensor("ham_st", [128, 64], BF16))
        hb_kf = e(nc.sbuf_tensor("hb_kf", [128, 2, 256], BF16))
        mskd2 = e(nc.sbuf_tensor("mskd2", [128, 2, 512], BF16))
        kv_all = e(nc.sbuf_tensor("kv_all", [128, D], F32))
        dk_t = e(nc.sbuf_tensor("dk_t", [128, 8], F32))
        slope_f = e(nc.sbuf_tensor("slope_f", [128, 16], F32))
        nw_row = e(nc.sbuf_tensor("nw_row", [1, D], BF16))
        iota_t = e(nc.sbuf_tensor("iota_t", [128, 5], F32))
        ao_stage = e(nc.sbuf_tensor("ao_stage", [128, 2, 2, 128], BF16))
        qd_all = e(nc.sbuf_tensor("qd_all", [128, 32], F32))
        kd_all = e(nc.sbuf_tensor("kd_all", [128, 32], F32))
        md_all = e(nc.sbuf_tensor("md_all", [128, 32], F32))
        ind_t = e(nc.sbuf_tensor("ind_t", [128, 512], BF16))
        bdj = e(nc.sbuf_tensor("bdj", [128, 128], F32))
        iota_bj_t = e(nc.sbuf_tensor("iota_bj_t", [128, 8], F32))
        qs_stg = e(nc.sbuf_tensor("qs_stg", [128, 2, D], BF16))
        ks_all = e(nc.sbuf_tensor("ks_all", [128, 2, D], BF16))
        kv_bf_all = e(nc.sbuf_tensor("kv_bf_all", [128, D], BF16))

        with nc.Block() as blk:
            @blk.gpsimd
            def _(gp):
                sING.inc(gp.dma_start(out=rat_t[:, :, :], in_=rat_in), 16)
                sING.inc(gp.dma_start(out=bt_res[:, :], in_=bt_in), 16)
                sING.inc(gp.dma_start(out=bb_row[:, :], in_=bb_in), 16)

            @blk.sync
            def _(sync):
                sIN.inc(sync.dma_start(out=idn_t[:, :], in_=idn_in), 16)
        ffn_consts = (sING.n, sIN.n)

        def emit_ln(x_src):
            with nc.Block() as blk:
                v0, a0 = cDVE.n, cACT.n
                xl0 = [sXL[0].n, sXL[1].n]
                st0 = [sST[0].n, sST[1].n]

                @blk.sync
                def _(sync):
                    sync.wait_ge(sST[0].sem, st0[0])
                    sync.wait_ge(sST[1].sem, st0[1])
                    for i in range(KC):
                        if i >= 2:
                            sync.wait_ge(cDVE.sem, v0 + (i - 2) * 7 + 7)
                        else:
                            sync.wait_ge(cDVE.sem, v0)
                        sXL[i % 2].inc(sync.dma_start(out=x_t2[:, i % 2],
                                                      in_=x_src[i * 128:(i + 1) * 128, :]), 16)
                        if i >= 1:
                            j = i - 1
                            sync.wait_ge(cDVE.sem, v0 + j * 7 + 7)
                            sST[j % 2].inc(sync.dma_start(out=u_sp[j * 128:(j + 1) * 128, :],
                                                          in_=ch_re[:, j % 2, 0:D]), 16)
                    j = KC - 1
                    sync.wait_ge(cDVE.sem, v0 + j * 7 + 7)
                    sST[j % 2].inc(sync.dma_start(out=u_sp[j * 128:(j + 1) * 128, :],
                                                  in_=ch_re[:, j % 2, 0:D]), 16)

                @blk.vector
                def _(v):
                    v.wait_ge(sST[0].sem, st0[0])
                    v.wait_ge(sST[1].sem, st0[1])
                    for i in range(KC):
                        pi = (i % 2) * 16
                        xt = x_t2[:, i % 2]
                        ut = ch_re[:, i % 2, 0:D]
                        v.wait_ge(sXL[i % 2].sem, xl0[i % 2] + (i // 2 + 1) * 16)
                        if i >= 2:
                            v.wait_ge(cACT.sem, a0 + (i - 2) * 2 + 2)
                            v.wait_ge(sST[i % 2].sem, st0[i % 2] + ((i - 2) // 2 + 1) * 16)
                        n = {}
                        def op(key, inst):
                            n[key] = cDVE.inc(inst)
                        op(1, v.tensor_scalar(out_t2[:, i % 2], xt, 1.0, 0.0, ALU.mult, ALU.add,
                                              accum_out=stt[:, pi + 0:pi + 1]))
                        op(2, v.scalar_tensor_tensor(out_t2[:, (i + 1) % 2], xt, 1.0, xt,
                                                     ALU.mult, ALU.mult,
                                                     accum_out=stt[:, pi + 1:pi + 2]))
                        v.wait_ge(cDVE.sem, n[1])
                        op(3, v.tensor_scalar_mul(stt[:, pi + 2:pi + 3], stt[:, pi + 0:pi + 1], 1.0 / D))
                        v.wait_ge(cDVE.sem, n[2])
                        op(4, v.tensor_scalar_mul(stt[:, pi + 3:pi + 4], stt[:, pi + 1:pi + 2], 1.0 / D))
                        v.wait_ge(cDVE.sem, n[3])
                        op(5, v.tensor_tensor(stt[:, pi + 4:pi + 5], stt[:, pi + 2:pi + 3],
                                              stt[:, pi + 2:pi + 3], ALU.mult))
                        v.wait_ge(cDVE.sem, n[5])
                        op(6, v.tensor_tensor(stt[:, pi + 5:pi + 6], stt[:, pi + 3:pi + 4],
                                              stt[:, pi + 4:pi + 5], ALU.subtract))
                        v.wait_ge(cACT.sem, a0 + i * 2 + 2)
                        op(7, v.tensor_scalar(ut, xt, stt[:, pi + 2:pi + 3], stt[:, pi + 6:pi + 7],
                                              ALU.subtract, ALU.mult))

                @blk.scalar
                def _(s):
                    for i in range(KC):
                        pi = (i % 2) * 16
                        s.wait_ge(cDVE.sem, v0 + i * 7 + 6)
                        na = cACT.inc(s.activation(stt[:, pi + 7:pi + 8], stt[:, pi + 5:pi + 6],
                                                   AF.Ln, bias=eps_t[:, 0:1]))
                        s.wait_ge(cACT.sem, na)
                        cACT.inc(s.activation(stt[:, pi + 6:pi + 7], stt[:, pi + 7:pi + 8],
                                              AF.Exp, scale=-0.5))
            return (sST[0].n, sST[1].n)

        def emit_xbar(st_tot):
            with nc.Block() as blk:
                @blk.scalar
                def _(s):
                    s.wait_ge(sST[0].sem, st_tot[0])
                    s.wait_ge(sST[1].sem, st_tot[1])
                    for c in range(KC):
                        sXB[c % 2].inc(s.dma_start(out=mvA[:, c, :],
                                                   in_=u_sp[:, c * 128:(c + 1) * 128],
                                                   transpose=True), 16)
            return (sXB[0].n, sXB[1].n)

        def emit_ffn(x_src, x_dst, store_out):
            st_tot = emit_ln(x_src)
            xb_tot = emit_xbar(st_tot)

            with nc.Block() as blk:
                pe0 = cPE.n
                v0 = cDVE.n
                a0 = cACT.n
                prev_bank = list(bank_last)
                mm_n = {}
                tp_n = {}
                dve_evac = {tr: v0 + tr * 19 + 1 for tr in range(KC)}
                dve_ham = {tr: v0 + tr * 19 + 19 for tr in range(KC)}
                dve_m2 = {tr: v0 + tr * 19 + 6 for tr in range(KC)}
                dve_r24 = {tr: v0 + tr * 19 + 10 for tr in range(KC)}
                act_d = {tr: a0 + (2 if tr == 0 else 3 * tr + 1) for tr in range(KC)}
                act_evac = {tr: a0 + (3 * (tr + 1) + 2 if tr < KC - 1 else 3 * KC) for tr in range(KC)}

                @blk.tensor
                def _(t):
                    t.wait_ge(sXB[0].sem, xb_tot[0])
                    t.wait_ge(sXB[1].sem, xb_tot[1])
                    t.wait_ge(sING.sem, ffn_consts[0])
                    t.wait_ge(sIN.sem, ffn_consts[1])

                    def mm(tr):
                        if tr >= NB:
                            t.wait_ge(cDVE.sem, dve_evac[tr - NB])
                        else:
                            bc, bval = prev_bank[tr % NB]
                            t.wait_ge(bc.sem, bval)
                        inst = None
                        for k in range(KC):
                            inst = t.matmul(pbank[tr % NB][:, 0:72],
                                            mvA[:, k, tr * 128:(tr + 1) * 128],
                                            rat_t[:, k, :], start=(k == 0), stop=(k == KC - 1))
                        mm_n[tr] = cPE.inc(inst)

                    def tp(tr):
                        t.wait_ge(cDVE.sem, dve_ham[tr])
                        if tr >= 1:
                            t.wait_ge(cACT.sem, act_evac[tr - 1])
                        tp_n[tr] = cPE.inc(t.transpose(pbank_bf[0:64, 0:128], ham_st[:, :],
                                                       idn_t[:, :]))

                    for tr in range(NB):
                        mm(tr)
                    for tr in range(KC - NB):
                        tp(tr)
                        mm(tr + NB)
                    for tr in range(KC - NB, KC):
                        tp(tr)

                @blk.vector
                def _(v):
                    for tr in range(KC):
                        n = {}
                        def op(key, inst):
                            n[key] = cDVE.inc(inst)
                        v.wait_ge(cPE.sem, pe0 + tr + 1 if tr < NB else mm_n[tr])
                        op(1, v.tensor_copy(raf[:, tr, :], pbank[tr % NB][:, 0:72]))
                        lg = raf[:, tr, 0:8]
                        v.wait_ge(cDVE.sem, n[1])
                        op(2, v.reduce_max(rsc[:, 0:1], lg, mybir.AxisListType.X))
                        v.wait_ge(cDVE.sem, n[2])
                        op(3, v.tensor_scalar_mul(rsc[:, 1:2], rsc[:, 0:1], -1.0))
                        op(4, v.tensor_single_scalar(rsc[:, 8:16], lg, rsc[:, 0:1], ALU.is_ge))
                        v.wait_ge(cDVE.sem, n[4])
                        op(5, v.scalar_tensor_tensor(rsc[:, 16:24], rsc[:, 8:16], -1e30, lg,
                                                     ALU.mult, ALU.add))
                        v.wait_ge(cDVE.sem, n[5])
                        op(6, v.reduce_max(rsc[:, 2:3], rsc[:, 16:24], mybir.AxisListType.X))
                        v.wait_ge(cACT.sem, act_d[tr])
                        op(7, v.tensor_scalar(rsc[:, 4:5], rsc[:, 3:4], 1.0, 0.5, ALU.add, ALU.mult))
                        v.wait_ge(cDVE.sem, n[7])
                        op(8, v.reciprocal(rsc[:, 5:6], rsc[:, 4:5]))
                        op(9, v.tensor_single_scalar(rsc[:, 32:40], lg, rsc[:, 2:3], ALU.is_ge))
                        v.wait_ge(cDVE.sem, n[9])
                        op(10, v.tensor_tensor(rsc[:, 40:48], rsc[:, 24:32], rsc[:, 32:40], ALU.mult))
                        v.wait_ge(cDVE.sem, n[10])
                        op(11, v.tensor_single_scalar(rsc[:, 48:56], rsc[:, 40:48], rsc[:, 5:6],
                                                      ALU.mult))
                        v.wait_ge(cDVE.sem, n[11])
                        if tr >= 1:
                            v.wait_ge(cPE.sem, tp_n[tr - 1])
                        for ei in range(8):
                            cDVE.inc(v.tensor_single_scalar(ham_st[:, ei * 8:(ei + 1) * 8],
                                                            raf[:, tr, 8 + ei * 8:8 + (ei + 1) * 8],
                                                            rsc[:, 48 + ei:49 + ei], ALU.mult))

                @blk.scalar
                def _(s):
                    for tr in range(KC):
                        s.wait_ge(cDVE.sem, dve_m2[tr])
                        if tr >= 1:
                            s.wait_ge(cDVE.sem, dve_r24[tr - 1])
                        cACT.inc(s.activation(rsc[:, 24:32], raf[:, tr, 0:8], AF.Exp,
                                              bias=rsc[:, 1:2]))
                        cACT.inc(s.activation(rsc[:, 3:4], rsc[:, 2:3], AF.Exp, bias=rsc[:, 1:2]))
                        if tr >= 1:
                            s.wait_ge(cPE.sem, tp_n[tr - 1])
                            cACT.inc(s.copy(hamf[0:64, (tr - 1) * 128:tr * 128],
                                            pbank_bf[0:64, 0:128]))
                    s.wait_ge(cPE.sem, tp_n[KC - 1])
                    cACT.inc(s.copy(hamf[0:64, (KC - 1) * 128:KC * 128], pbank_bf[0:64, 0:128]))
                for b_ in range(NB):
                    bank_last[b_] = (cDVE, cDVE.n)
                bf_last[0], bf_last[1] = cACT, cACT.n
            hamf_act = cACT.n

            with nc.Block() as blk:
                pe0 = cPE.n
                v0 = cDVE.n
                ws_0 = sWS[0].n
                xl0 = [sXL[0].n, sXL[1].n]
                st0 = [sST[0].n, sST[1].n]
                prev_bank = list(bank_last)

                @blk.gpsimd
                def _(gp):
                    for oh in range(2):
                        if oh >= 1:
                            gp.wait_ge(cPE.sem, pe0 + 32)
                        elif wres_pe["n"] > 0:
                            gp.wait_ge(cPE.sem, wres_pe["n"])
                        sWS[0].inc(gp.dma_start(out=wres[:, :, :].rearrange("p k o -> p (k o)"),
                                                in_=wbase_in[oh]), 16)

                @blk.tensor
                def _(t):
                    t.wait_ge(cACT.sem, hamf_act)
                    for oh in range(2):
                        t.wait_ge(sWS[0].sem, ws_0 + (oh + 1) * 16)
                        for tr in range(KC):
                            for oc in range(2):
                                g = oh * 32 + tr * 2 + oc
                                if g >= NB:
                                    t.wait_ge(cDVE.sem, v0 + (g - NB) + 1)
                                else:
                                    bc, bval = prev_bank[g % NB]
                                    t.wait_ge(bc.sem, bval)
                                bank = pbank[g % NB]
                                inst = None
                                for k in range(KC):
                                    inst = t.matmul(bank[:, :], mvA[:, k, tr * 128:(tr + 1) * 128],
                                                    wres[:, k, oc * TBW:(oc + 1) * TBW],
                                                    start=(k == 0), stop=False)
                                t.matmul(bank[:, :], ones_t[0:1, :],
                                         bb_row[0:1, oh * 1024 + oc * TBW:oh * 1024 + (oc + 1) * TBW],
                                         start=False, stop=False)
                                inst = t.matmul(bank[:, :], hamf[0:64, tr * 128:(tr + 1) * 128],
                                                bt_res[0:64, oh * 1024 + oc * TBW:oh * 1024 + (oc + 1) * TBW],
                                                start=False, stop=True)
                                cPE.inc(inst)

                @blk.sync
                def _(sync):
                    for oh in range(2):
                        for tr in range(KC):
                            g2 = oh * 16 + tr
                            if g2 >= 2:
                                sync.wait_ge(cDVE.sem, v0 + (g2 - 2) * 2 + 2)
                            else:
                                sync.wait_ge(cDVE.sem, v0)
                            sXL[tr % 2].inc(sync.dma_start(
                                out=x_t2[:, tr % 2, 0:1024],
                                in_=x_src[tr * 128:(tr + 1) * 128, oh * 1024:(oh + 1) * 1024]), 16)
                            if g2 >= 1:
                                j2 = g2 - 1
                                joh, jtr = j2 // 16, j2 % 16
                                sync.wait_ge(cDVE.sem, v0 + j2 * 2 + 2)
                                sST[jtr % 2].inc(sync.dma_start(
                                    out=x_dst[jtr * 128:(jtr + 1) * 128, joh * 1024:(joh + 1) * 1024],
                                    in_=out_t2[:, jtr % 2, 0:1024]), 16)
                                if store_out:
                                    sST[jtr % 2].inc(sync.dma_start(
                                        out=out_ap[jtr * 128:(jtr + 1) * 128, joh * 1024:(joh + 1) * 1024],
                                        in_=out_t2[:, jtr % 2, 0:1024]), 16)
                    j2 = 31
                    joh, jtr = j2 // 16, j2 % 16
                    sync.wait_ge(cDVE.sem, v0 + j2 * 2 + 2)
                    sST[jtr % 2].inc(sync.dma_start(
                        out=x_dst[jtr * 128:(jtr + 1) * 128, joh * 1024:(joh + 1) * 1024],
                        in_=out_t2[:, jtr % 2, 0:1024]), 16)
                    if store_out:
                        sST[jtr % 2].inc(sync.dma_start(
                            out=out_ap[jtr * 128:(jtr + 1) * 128, joh * 1024:(joh + 1) * 1024],
                            in_=out_t2[:, jtr % 2, 0:1024]), 16)
                    if store_out:
                        sync.wait_ge(sST[0].sem, sST[0].n)
                        sync.wait_ge(sST[1].sem, sST[1].n)

                @blk.vector
                def _(v):
                    spt = 32 if store_out else 16
                    for oh in range(2):
                        for tr in range(KC):
                            g2 = oh * 16 + tr
                            v.wait_ge(sXL[tr % 2].sem, xl0[tr % 2] + (g2 // 2 + 1) * 16)
                            if g2 >= 2:
                                v.wait_ge(sST[tr % 2].sem, st0[tr % 2] + ((g2 - 2) // 2 + 1) * spt)
                            nprev = None
                            for oc in range(2):
                                g = oh * 32 + tr * 2 + oc
                                v.wait_ge(cPE.sem, pe0 + g + 1)
                                if nprev is not None:
                                    v.wait_ge(cDVE.sem, nprev)
                                nprev = cDVE.inc(v.scalar_tensor_tensor(
                                    out_t2[:, tr % 2, oc * TBW:(oc + 1) * TBW],
                                    pbank[g % NB][:, :], 1.0,
                                    x_t2[:, tr % 2, oc * TBW:(oc + 1) * TBW],
                                    ALU.mult, ALU.add))
                for b_ in range(NB):
                    bank_last[b_] = (cDVE, cDVE.n)
                wres_pe["n"] = cPE.n

        def emit_optA(wt2, dst_sp, func=None, x_src=None, x_dst=None, store_out=False,
                      extra_ready=()):
            with nc.Block() as blk:
                pe0 = cPE.n
                v0 = cDVE.n
                a0 = cACT.n
                ws_0 = sWS[0].n
                xl0 = [sXL[0].n, sXL[1].n]
                st0 = [sST[0].n, sST[1].n]
                prev_bank = list(bank_last)
                evac_ctr = cACT if func is not None else cDVE

                @blk.gpsimd
                def _(gp):
                    for oh in range(2):
                        if oh >= 1:
                            gp.wait_ge(cPE.sem, pe0 + 32)
                        else:
                            gp.wait_ge(cPE.sem, pe0)
                        sWS[0].inc(gp.dma_start(out=wres[:, :, :].rearrange("p k o -> p (k o)"),
                                                in_=wt2[oh]), 16)

                @blk.tensor
                def _(t):
                    for sem, val in extra_ready:
                        t.wait_ge(sem, val)
                    for oh in range(2):
                        t.wait_ge(sWS[0].sem, ws_0 + (oh + 1) * 16)
                        for tr in range(KC):
                            for oc in range(2):
                                g = oh * 32 + tr * 2 + oc
                                if g >= NB:
                                    t.wait_ge(evac_ctr.sem, v0 + (g - NB) + 1 if func is None
                                              else a0 + (g - NB) + 1)
                                else:
                                    bc, bval = prev_bank[g % NB]
                                    t.wait_ge(bc.sem, bval)
                                bank = pbank[g % NB]
                                inst = None
                                for k in range(KC):
                                    inst = t.matmul(bank[:, :], mvA[:, k, tr * 128:(tr + 1) * 128],
                                                    wres[:, k, oc * TBW:(oc + 1) * TBW],
                                                    start=(k == 0), stop=(k == KC - 1))
                                cPE.inc(inst)

                if func is not None:
                    @blk.scalar
                    def _(s):
                        for oh in range(2):
                            for tr in range(KC):
                                g2 = oh * 16 + tr
                                if g2 >= 2:
                                    s.wait_ge(sST[tr % 2].sem,
                                              st0[tr % 2] + ((g2 - 2) // 2 + 1) * 16)
                                else:
                                    s.wait_ge(sST[tr % 2].sem, st0[tr % 2])
                                for oc in range(2):
                                    g = oh * 32 + tr * 2 + oc
                                    s.wait_ge(cPE.sem, pe0 + g + 1)
                                    cACT.inc(s.activation(
                                        ch_re[:, tr % 2, oc * TBW:(oc + 1) * TBW],
                                        pbank[g % NB][:, :], func))

                    @blk.sync
                    def _(sync):
                        for oh in range(2):
                            for tr in range(KC):
                                g2 = oh * 16 + tr
                                sync.wait_ge(cACT.sem, a0 + g2 * 2 + 2)
                                sST[tr % 2].inc(sync.dma_start(
                                    out=dst_sp[tr * 128:(tr + 1) * 128, oh * 1024:(oh + 1) * 1024],
                                    in_=ch_re[:, tr % 2, 0:1024]), 16)
                    for b_ in range(NB):
                        bank_last[b_] = (cACT, cACT.n)
                else:
                    @blk.sync
                    def _(sync):
                        spt = 32 if store_out else 16
                        nst = 0
                        for oh in range(2):
                            for tr in range(KC):
                                g2 = oh * 16 + tr
                                if g2 >= 2:
                                    sync.wait_ge(cDVE.sem, v0 + (g2 - 2) * 2 + 2)
                                else:
                                    sync.wait_ge(cDVE.sem, v0)
                                sXL[tr % 2].inc(sync.dma_start(
                                    out=x_t2[:, tr % 2, 0:1024],
                                    in_=x_src[tr * 128:(tr + 1) * 128, oh * 1024:(oh + 1) * 1024]), 16)
                                if g2 >= 1:
                                    j2 = g2 - 1
                                    joh, jtr = j2 // 16, j2 % 16
                                    sync.wait_ge(cDVE.sem, v0 + j2 * 2 + 2)
                                    sST[jtr % 2].inc(sync.dma_start(
                                        out=x_dst[jtr * 128:(jtr + 1) * 128,
                                                  joh * 1024:(joh + 1) * 1024],
                                        in_=out_t2[:, jtr % 2, 0:1024]), 16)
                                    if store_out:
                                        sST[jtr % 2].inc(sync.dma_start(
                                            out=out_ap[jtr * 128:(jtr + 1) * 128,
                                                       joh * 1024:(joh + 1) * 1024],
                                            in_=out_t2[:, jtr % 2, 0:1024]), 16)
                        j2 = 31
                        joh, jtr = 1, 15
                        sync.wait_ge(cDVE.sem, v0 + j2 * 2 + 2)
                        sST[jtr % 2].inc(sync.dma_start(
                            out=x_dst[jtr * 128:(jtr + 1) * 128, joh * 1024:(joh + 1) * 1024],
                            in_=out_t2[:, jtr % 2, 0:1024]), 16)
                        if store_out:
                            sST[jtr % 2].inc(sync.dma_start(
                                out=out_ap[jtr * 128:(jtr + 1) * 128, joh * 1024:(joh + 1) * 1024],
                                in_=out_t2[:, jtr % 2, 0:1024]), 16)
                            sync.wait_ge(sST[0].sem, sST[0].n)
                            sync.wait_ge(sST[1].sem, sST[1].n)

                    @blk.vector
                    def _(v):
                        spt = 32 if store_out else 16
                        for oh in range(2):
                            for tr in range(KC):
                                g2 = oh * 16 + tr
                                v.wait_ge(sXL[tr % 2].sem, xl0[tr % 2] + (g2 // 2 + 1) * 16)
                                if g2 >= 2:
                                    v.wait_ge(sST[tr % 2].sem,
                                              st0[tr % 2] + ((g2 - 2) // 2 + 1) * spt)
                                else:
                                    v.wait_ge(sST[tr % 2].sem, st0[tr % 2])
                                nprev = None
                                for oc in range(2):
                                    g = oh * 32 + tr * 2 + oc
                                    v.wait_ge(cPE.sem, pe0 + g + 1)
                                    if nprev is not None:
                                        v.wait_ge(cDVE.sem, nprev)
                                    nprev = cDVE.inc(v.scalar_tensor_tensor(
                                        out_t2[:, tr % 2, oc * TBW:(oc + 1) * TBW],
                                        pbank[g % NB][:, :], 1.0,
                                        x_t2[:, tr % 2, oc * TBW:(oc + 1) * TBW],
                                        ALU.mult, ALU.add))
                    for b_ in range(NB):
                        bank_last[b_] = (cDVE, cDVE.n)

        def emit_attn_b(x_src, x_dst, store_out):
            st_tot = emit_ln(x_src)
            xb_tot = emit_xbar(st_tot)

            with nc.Block() as blk:
                @blk.sync
                def _(sync):
                    sIN.inc(sync.dma_start(out=iota_t[:, 0:2], in_=iota1_in), 16)
                    sIN.inc(sync.dma_start(out=iota_t[:, 2:4], in_=iota2_in), 16)
                    sIN.inc(sync.dma_start(out=iota_t[:, 4:5], in_=c256_in), 16)
                    sIN.inc(sync.dma_start(out=slope_f[:, :], in_=slope_in), 16)
                    sIN.inc(sync.dma_start(out=iota_bj_t[:, :], in_=iotabj_in), 16)
                    sIN.inc(sync.dma_start(out=ind_t[:, :], in_=ind_in), 16)

                @blk.gpsimd
                def _(gp):
                    sING.inc(gp.dma_start(out=nw_row[:, :], in_=nw_in), 16)
                binit_in = (sIN.n, sING.n)

                @blk.tensor
                def _(t):
                    t.wait_ge(sING.sem, binit_in[1])
                    t.wait_ge(cACT.sem, cACT.n)
                    inst = None
                    for j in range(NTB):
                        inst = t.matmul(pbank[j % NB][:, :], ones_t[0:1, :],
                                        nw_row[0:1, j * TBW:(j + 1) * TBW], start=True, stop=True)
                    cPE.inc(inst)
                nwpe = cPE.n

                @blk.scalar
                def _(s):
                    s.wait_ge(cPE.sem, nwpe)
                    inst = None
                    for j in range(NTB):
                        inst = s.copy(g_bc[:, j * TBW:(j + 1) * TBW], pbank[j % NB][:, :])
                    cACT.inc(inst)

                @blk.vector
                def _(v):
                    cDVE.inc(v.memset(dk_t[:, 7:8], 1e-6))
            binit_act = cACT.n
            binit_dve = cDVE.n
            for b_ in range(NTB):
                bank_last[b_] = (cACT, binit_act)

            NH, NJ = 16, 8

            with nc.Block() as blk:
                v0p, a0p = cDVE.n, cACT.n

                @blk.vector
                def _(v):
                    v.wait_ge(cDVE.sem, binit_dve)
                    v.wait_ge(sIN.sem, binit_in[0])
                    for h in range(NH):
                        if h >= 1:
                            v.wait_ge(cACT.sem, a0p + 4 * h)
                        cDVE.inc(v.tensor_scalar_mul(dk_t[:, 6:7], slope_f[:, h:h + 1], -1.0))

                @blk.scalar
                def _(s):
                    for h in range(NH):
                        s.wait_ge(cDVE.sem, v0p + h + 1)
                        cACT.inc(s.activation(qd_all[:, h * 2:h * 2 + 2], iota_t[:, 0:2],
                                              AF.Exp, scale=dk_t[:, 6:7]))
                        cACT.inc(s.activation(kd_all[:, h * 2:h * 2 + 2], iota_t[:, 2:4],
                                              AF.Exp, scale=dk_t[:, 6:7]))
                        cACT.inc(s.activation(bdj[:, h * 8:(h + 1) * 8], iota_bj_t[:, :],
                                              AF.Exp, scale=dk_t[:, 6:7]))
                        cACT.inc(s.activation(md_all[:, h * 2:h * 2 + 2], iota_t[:, 0:2],
                                              AF.Exp, scale=slope_f[:, h:h + 1]))
            p0_act = cACT.n

            def emit_p1_pass(src_sp2, gate, tab, dst_sp2):
                with nc.Block() as blk:
                    v0 = cDVE.n
                    rg0 = [sRG[0].n, sRG[1].n]
                    sp0 = [sSP[0].n, sSP[1].n]

                    @blk.sync
                    def _(sync):
                        def store(j):
                            sync.wait_ge(cDVE.sem, v0 + (j + 1) * NH)
                            sSP[j % 2].inc(sync.dma_start(
                                out=dst_sp2[j * 128:(j + 1) * 128, :],
                                in_=qs_stg[:, j % 2]), 16)
                        for i in range(KC):
                            sync.wait_ge(sST[0].sem, gate[0])
                            sync.wait_ge(sST[1].sem, gate[1])
                            if i >= 2:
                                sync.wait_ge(sSP[i % 2].sem,
                                             sp0[i % 2] + ((i - 2) // 2 + 1) * 16)
                            else:
                                sync.wait_ge(sSP[i % 2].sem, sp0[i % 2])
                            sRG[i % 2].inc(sync.dma_start(
                                out=qs_stg[:, i % 2],
                                in_=src_sp2[i * 128:(i + 1) * 128, :]), 16)
                            if i >= 1:
                                store(i - 1)
                        store(KC - 1)

                    @blk.vector
                    def _(v):
                        v.wait_ge(cACT.sem, p0_act)
                        for i in range(KC):
                            v.wait_ge(sRG[i % 2].sem, rg0[i % 2] + (i // 2 + 1) * 16)
                            for h in range(NH):
                                cDVE.inc(v.tensor_single_scalar(
                                    qs_stg[:, i % 2, h * 128:(h + 1) * 128],
                                    qs_stg[:, i % 2, h * 128:(h + 1) * 128],
                                    tab[:, h * 2 + (i % 2):h * 2 + (i % 2) + 1], ALU.mult))

            xbrdy = [(sXB[0].sem, xb_tot[0]), (sXB[1].sem, xb_tot[1]),
                     (sIN.sem, binit_in[0])]
            emit_optA(qkvw_in[0], bq_sp, func=AF.Silu, extra_ready=xbrdy)
            bq_st = (sST[0].n, sST[1].n)
            emit_optA(qkvw_in[1], bk_sp, func=AF.Silu)
            bk_st = (sST[0].n, sST[1].n)
            emit_p1_pass(bq_sp, bq_st, qd_all, u_sp)
            emit_optA(qkvw_in[2], bv_sp, func=AF.Silu)
            emit_optA(wgate_in, gate_sp, func=AF.Sigmoid)
            emit_p1_pass(bk_sp, bk_st, kd_all, k_sp)
            emit_p1_pass(bk_sp, bk_st, md_all, q_sp)
            b1_st = (sST[0].n, sST[1].n)
            b1_pe = cPE.n
            p1_sp = (sSP[0].n, sSP[1].n)


            with nc.Block() as blk:
                @blk.scalar
                def _(s):
                    s.wait_ge(sSP[0].sem, p1_sp[0])
                    s.wait_ge(sSP[1].sem, p1_sp[1])
                    s.wait_ge(cPE.sem, b1_pe)
                    for h in range(NH):
                        sXB[h % 2].inc(s.dma_start(out=mvA[:, h, :],
                                                   in_=u_sp[:, h * 128:(h + 1) * 128],
                                                   transpose=True), 16)
            qs_xb = (sXB[0].n, sXB[1].n)

            NS = NH * NJ
            with nc.Block() as blk:
                pe0u, v0u = cPE.n, cDVE.n
                rg0 = sRG[0].n
                xl0 = sXL[0].n
                prev_bank = list(bank_last)
                pe_uh = {}
                c = pe0u
                for j in range(NJ):
                    for h in range(NH):
                        c += 1; pe_uh[(j, h)] = c
                p3_pe_end = c
                dve_ev3, dve_chain = {}, {}
                c = v0u
                for j in range(NJ):
                    for g2 in range(4):
                        c += 1; dve_ev3[(j, g2)] = c
                    c += (1 if j == 0 else 18); dve_chain[j] = c
                p3_dve_end = c

                def b3i(j, g2):
                    return (j * 4 + g2) % 7

                @blk.sync
                def _(sync):
                    sync.wait_ge(sST[0].sem, b1_st[0])
                    sync.wait_ge(sST[1].sem, b1_st[1])
                    sync.wait_ge(sSP[0].sem, p1_sp[0])
                    sync.wait_ge(sSP[1].sem, p1_sp[1])
                    for j in range(NJ):
                        if j >= 1:
                            sync.wait_ge(cPE.sem, pe_uh[(j - 1, NH - 1)])
                        sXL[0].inc(sync.dma_start(
                            out=qs_stg[:, :, :].rearrange("p s d -> p (s d)"),
                            in_=bv_sp[j * 256:(j + 1) * 256, :]
                            .rearrange("(s p) d -> p s d", s=2)), 16)
                        sRG[0].inc(sync.dma_start(
                            out=ks_all[:, :, :].rearrange("p s d -> p (s d)"),
                            in_=k_sp[j * 256:(j + 1) * 256, :]
                            .rearrange("(s p) d -> p s d", s=2)), 16)

                @blk.tensor
                def _(t):
                    for j in range(NJ):
                        t.wait_ge(sXL[0].sem, xl0 + (j + 1) * 16)
                        t.wait_ge(sRG[0].sem, rg0 + (j + 1) * 16)
                        for h in range(NH):
                            g2 = h // 4
                            bi = j * 4 + g2
                            if h % 4 == 0:
                                if bi >= 7:
                                    jj2, gg2 = divmod(bi - 7, 4)
                                    t.wait_ge(cDVE.sem, dve_ev3[(jj2, gg2)])
                                else:
                                    bc, bval = prev_bank[bi % 7]
                                    t.wait_ge(bc.sem, bval)
                            inst = None
                            for sub in range(2):
                                inst = t.matmul(
                                    pbank[b3i(j, g2)][:, (h % 4) * 128:(h % 4 + 1) * 128],
                                    ks_all[:, sub, h * 128:(h + 1) * 128],
                                    qs_stg[:, sub, h * 128:(h + 1) * 128],
                                    start=(sub == 0), stop=(sub == 1))
                            cPE.inc(inst)

                @blk.vector
                def _(v):
                    for j in range(NJ):
                        for g2 in range(4):
                            v.wait_ge(cPE.sem, pe_uh[(j, 4 * g2 + 3)])
                            cDVE.inc(v.tensor_copy(
                                wres[:, 4 * g2:4 * g2 + 4, j * 128:(j + 1) * 128],
                                pbank[b3i(j, g2)][:, 0:512]
                                .rearrange("p (h e) -> p h e", h=4)))
                        ual = wres[:, :, j * 128:(j + 1) * 128]
                        kv_v = kv_all[:, :].rearrange("p (h e) -> p h e", h=NH)
                        v.wait_ge(cDVE.sem, dve_ev3[(j, 3)])
                        if j == 0:
                            cDVE.inc(v.tensor_copy(kv_v, ual))
                        else:
                            nl = cDVE.inc(v.tensor_copy(
                                wres[:, :, (j - 1) * 128:j * 128], kv_v))
                            v.wait_ge(cDVE.sem, nl)
                            nm = None
                            for h2 in range(NH):
                                nm = cDVE.inc(v.tensor_single_scalar(
                                    kv_all[:, h2 * 128:(h2 + 1) * 128],
                                    kv_all[:, h2 * 128:(h2 + 1) * 128],
                                    bdj[:, h2 * 8 + 1:h2 * 8 + 2], ALU.mult))
                            v.wait_ge(cDVE.sem, nm)
                            cDVE.inc(v.tensor_tensor(kv_v, kv_v, ual, ALU.add))
                for b_ in range(NB):
                    bank_last[b_] = (cDVE, p3_dve_end)
            p3_pe = cPE.n

            with nc.Block() as blk:
                @blk.sync
                def _(sync):
                    sync.wait_ge(cDVE.sem, p3_dve_end)
                    sync.wait_ge(cCC.sem, 4)
                    cCI.inc(sync.dma_start(out=cc2_in[:, :], in_=kv_all[:, :]), 16)
                ci_val = cCI.n

                @blk.gpsimd
                def _(gp):
                    gp.wait_ge(cCI.sem, ci_val)
                    gp.wait_ge(cGA.sem, 4 * 16)
                    cCC.inc(gp.collective_compute(
                        "AllGather", ALU.bypass,
                        replica_groups=[[0, 1], [2, 3], [4, 5], [6, 7]],
                        ins=[cc2_in.ap().opt()], outs=[cc2_out.ap().opt()]))
                    gp.wait_ge(cCC.sem, cCC.n)
                    cGA.inc(gp.dma_start(out=kv_all[:, :], in_=cc2_out[0:128, :]), 16)
                ga_val = cGA.n

                @blk.vector
                def _(v):
                    v.wait_ge(cGA.sem, ga_val)
                    cDVE.inc(v.tensor_single_scalar(kv_all[:, :], kv_all[:, :],
                                                    flag_t[:, 0:1], ALU.mult))
            e3_dve = cDVE.n

            with nc.Block() as blk:
                v0, a0, pe0 = cDVE.n, cACT.n, cPE.n
                xl0 = [sXL[0].n, sXL[1].n]
                xb0 = [sXB[0].n, sXB[1].n]
                st0 = [sST[0].n, sST[1].n]
                prev_bank = list(bank_last)
                dve_fixj, dve_msk, dve_ev = {}, {}, {}
                c = v0
                for s_ in range(NS):
                    h, j = s_ % NH, s_ // NH
                    if h == 0:
                        c += 1; dve_fixj[j] = c
                        c += NH
                    c += 1; dve_msk[s_] = c
                    if s_ >= 1:
                        c += 1; dve_ev[s_ - 1] = c
                c += 1; dve_ev[NS - 1] = c
                b2b_dve_end = c
                pe_qk, pe_out = {}, {}
                c = pe0
                for s_ in range(NS):
                    c += 2; pe_qk[s_] = c
                    c += 1; pe_out[s_] = c
                b2b_pe_end = c

                @blk.sync
                def _(sync):
                    for j in range(NJ):
                        if j >= 1:
                            sync.wait_ge(cPE.sem, pe_out[j * NH - 1])
                        else:
                            sync.wait_ge(cPE.sem, p3_pe)
                        sXL[j % 2].inc(sync.dma_start(
                            out=qs_stg[:, :, :].rearrange("p s d -> p (s d)"),
                            in_=bv_sp[j * 256:(j + 1) * 256, :]
                            .rearrange("(s p) d -> p s d", s=2)), 16)
                        for h in range(NH):
                            s_ = j * NH + h
                            if s_ >= 1:
                                sm1 = s_ - 1
                                hh, jj = sm1 % NH, sm1 // NH
                                pp = sm1 % 2
                                sync.wait_ge(cDVE.sem, dve_ev[sm1])
                                sST[pp].inc(sync.dma_start(
                                    out=v_sp[hh * 128:(hh + 1) * 128,
                                             jj * 256:(jj + 1) * 256],
                                    in_=ao_stage[:, pp, :, :]
                                    .rearrange("a b c -> a (b c)")), 16)
                    s_ = NS - 1
                    hh, jj = s_ % NH, s_ // NH
                    pp = s_ % 2
                    sync.wait_ge(cDVE.sem, dve_ev[s_])
                    sST[pp].inc(sync.dma_start(
                        out=v_sp[hh * 128:(hh + 1) * 128, jj * 256:(jj + 1) * 256],
                        in_=ao_stage[:, pp, :, :].rearrange("a b c -> a (b c)")), 16)

                @blk.scalar
                def _(s):
                    for s_ in range(NS):
                        h, j = s_ % NH, s_ // NH
                        p = s_ % 2
                        if s_ >= 2:
                            s.wait_ge(cPE.sem, pe_qk[s_ - 2])
                        sXB[p].inc(s.dma_start(
                            out=hb_kf[:, p, :],
                            in_=q_sp[j * 256:(j + 1) * 256, h * 128:(h + 1) * 128],
                            transpose=True), 16)

                @blk.vector
                def _(v):
                    v.wait_ge(cDVE.sem, e3_dve)
                    for s_ in range(NS):
                        h, j = s_ % NH, s_ // NH
                        p = s_ % 2
                        if h == 0:
                            if j == 0:
                                nf = cDVE.inc(v.tensor_copy(kv_bf_all[:, :], kv_all[:, :]))
                            else:
                                v.wait_ge(cPE.sem, pe_out[j * NH - 1])
                                nf = cDVE.inc(v.tensor_tensor(
                                    kv_bf_all[:, :].rearrange("p (h e) -> p h e", h=NH),
                                    kv_all[:, :].rearrange("p (h e) -> p h e", h=NH),
                                    wres[:, :, (j - 1) * 128:j * 128], ALU.add))
                            v.wait_ge(cDVE.sem, nf)
                            for h2 in range(NH):
                                cDVE.inc(v.tensor_single_scalar(
                                    kv_all[:, h2 * 128:(h2 + 1) * 128],
                                    kv_all[:, h2 * 128:(h2 + 1) * 128],
                                    bdj[:, h2 * 8 + 1:h2 * 8 + 2], ALU.mult))
                        if s_ >= 2:
                            v.wait_ge(cPE.sem, pe_out[s_ - 2])
                        v.wait_ge(cPE.sem, pe_qk[s_])
                        cDVE.inc(v.tensor_tensor(mskd2[:, p, :], pbank[s_ % 4][:, 0:512],
                                                 ind_t[:, :], ALU.mult))
                        if s_ >= 1:
                            sm1 = s_ - 1
                            v.wait_ge(cPE.sem, pe_out[sm1])
                            if sm1 >= 2:
                                v.wait_ge(sST[sm1 % 2].sem,
                                          st0[sm1 % 2] + ((sm1 - 2) // 2 + 1) * 16)
                            cDVE.inc(v.tensor_copy(ao_stage[:, sm1 % 2, :, :]
                                                   .rearrange("a b c -> a (b c)"),
                                                   pbank[4 + sm1 % 3][:, 0:256]))
                    s_ = NS - 1
                    v.wait_ge(cPE.sem, pe_out[s_])
                    v.wait_ge(sST[s_ % 2].sem, st0[s_ % 2] + ((s_ - 2) // 2 + 1) * 16)
                    cDVE.inc(v.tensor_copy(ao_stage[:, s_ % 2, :, :]
                                           .rearrange("a b c -> a (b c)"),
                                           pbank[4 + s_ % 3][:, 0:256]))

                @blk.tensor
                def _(t):
                    t.wait_ge(sXB[0].sem, qs_xb[0])
                    t.wait_ge(sXB[1].sem, qs_xb[1])
                    for s_ in range(NS):
                        h, j = s_ % NH, s_ // NH
                        p = s_ % 2
                        t.wait_ge(sXB[p].sem, xb0[p] + (s_ // 2 + 1) * 16)
                        if s_ >= 4:
                            t.wait_ge(cDVE.sem, dve_msk[s_ - 4])
                        else:
                            bc, bval = prev_bank[s_ % 4]
                            t.wait_ge(bc.sem, bval)
                        for sub2 in range(2):
                            cPE.inc(t.matmul(pbank[s_ % 4][:, sub2 * 256:(sub2 + 1) * 256],
                                             hb_kf[:, p, sub2 * 128:(sub2 + 1) * 128],
                                             mvA[:, h, j * 256:(j + 1) * 256],
                                             start=True, stop=True))
                        t.wait_ge(cDVE.sem, dve_msk[s_])
                        t.wait_ge(sXL[j % 2].sem, xl0[j % 2] + (j // 2 + 1) * 16)
                        if s_ >= 3:
                            t.wait_ge(cDVE.sem, dve_ev[s_ - 3])
                        else:
                            bc, bval = prev_bank[4 + s_ % 3]
                            t.wait_ge(bc.sem, bval)
                        ob = pbank[4 + s_ % 3]
                        t.matmul(ob[:, 0:256], kv_bf_all[:, h * 128:(h + 1) * 128],
                                 mvA[:, h, j * 256:(j + 1) * 256], start=True, stop=False)
                        t.matmul(ob[:, 0:256], qs_stg[:, 0, h * 128:(h + 1) * 128],
                                 mskd2[:, p, 0:256], start=False, stop=False)
                        inst = t.matmul(ob[:, 0:256], qs_stg[:, 1, h * 128:(h + 1) * 128],
                                        mskd2[:, p, 256:512], start=False, stop=True)
                        cPE.inc(inst)
                for b_ in range(NB):
                    bank_last[b_] = (cDVE, b2b_dve_end)
            assert cDVE.n == b2b_dve_end, (cDVE.n, b2b_dve_end)
            assert cPE.n == b2b_pe_end, (cPE.n, b2b_pe_end)
            b2b_st = (sST[0].n, sST[1].n)

            with nc.Block() as blk:
                v0, a0 = cDVE.n, cACT.n
                xl0 = [sXL[0].n, sXL[1].n]
                rg0 = [sRG[0].n, sRG[1].n]
                st0 = [sST[0].n, sST[1].n]

                @blk.sync
                def _(sync):
                    sync.wait_ge(sST[0].sem, b2b_st[0])
                    sync.wait_ge(sST[1].sem, b2b_st[1])
                    for i in range(KC):
                        if i >= 2:
                            sync.wait_ge(cDVE.sem, v0 + (i - 2) * 4 + 4)
                        else:
                            sync.wait_ge(cDVE.sem, v0)
                        sXL[i % 2].inc(sync.dma_start(out=ch_re[:, i % 2, 0:D],
                                                      in_=v_sp[:, i * 128:(i + 1) * 128],
                                                      transpose=True), 16)
                        sRG[i % 2].inc(sync.dma_start(out=mvA[:, i % 2, 0:D],
                                                      in_=gate_sp[i * 128:(i + 1) * 128, :]), 16)
                        if i >= 1:
                            j = i - 1
                            sync.wait_ge(cDVE.sem, v0 + j * 4 + 4)
                            sST[j % 2].inc(sync.dma_start(out=gn_sp[j * 128:(j + 1) * 128, :],
                                                          in_=mvA[:, 2 + j % 2, 0:D]), 16)
                    j = KC - 1
                    sync.wait_ge(cDVE.sem, v0 + j * 4 + 4)
                    sST[j % 2].inc(sync.dma_start(out=gn_sp[j * 128:(j + 1) * 128, :],
                                                  in_=mvA[:, 2 + j % 2, 0:D]), 16)

                @blk.vector
                def _(v):
                    for i in range(KC):
                        pi = (i % 2) * 16
                        at = ch_re[:, i % 2, 0:D]
                        v.wait_ge(sXL[i % 2].sem, xl0[i % 2] + (i // 2 + 1) * 16)
                        v.wait_ge(sRG[i % 2].sem, rg0[i % 2] + (i // 2 + 1) * 16)
                        if i >= 2:
                            v.wait_ge(cACT.sem, a0 + (i - 2) * 2 + 2)
                            v.wait_ge(sST[i % 2].sem, st0[i % 2] + ((i - 2) // 2 + 1) * 16)
                        n1 = cDVE.inc(v.scalar_tensor_tensor(out_t2[:, i % 2, 0:D], at, 1.0, at,
                                                             ALU.mult, ALU.mult,
                                                             accum_out=stt[:, pi + 0:pi + 1]))
                        v.wait_ge(cDVE.sem, n1)
                        cDVE.inc(v.tensor_scalar_mul(stt[:, pi + 1:pi + 2], stt[:, pi + 0:pi + 1],
                                                     1.0 / D))
                        v.wait_ge(cACT.sem, a0 + i * 2 + 2)
                        n3 = cDVE.inc(v.scalar_tensor_tensor(out_t2[:, i % 2, 0:D], at,
                                                             stt[:, pi + 3:pi + 4], g_bc[:, :],
                                                             ALU.mult, ALU.mult))
                        v.wait_ge(cDVE.sem, n3)
                        cDVE.inc(v.tensor_tensor(mvA[:, 2 + i % 2, 0:D], out_t2[:, i % 2, 0:D],
                                                 mvA[:, i % 2, 0:D], ALU.mult))

                @blk.scalar
                def _(s):
                    for i in range(KC):
                        pi = (i % 2) * 16
                        s.wait_ge(cDVE.sem, v0 + i * 4 + 2)
                        na = cACT.inc(s.activation(stt[:, pi + 2:pi + 3], stt[:, pi + 1:pi + 2],
                                                   AF.Ln, bias=dk_t[:, 7:8]))
                        s.wait_ge(cACT.sem, na)
                        cACT.inc(s.activation(stt[:, pi + 3:pi + 4], stt[:, pi + 2:pi + 3],
                                              AF.Exp, scale=-0.5))
            b3_st = (sST[0].n, sST[1].n)

            with nc.Block() as blk:
                @blk.scalar
                def _(s):
                    s.wait_ge(sST[0].sem, b3_st[0])
                    s.wait_ge(sST[1].sem, b3_st[1])
                    for c2 in range(KC):
                        sXB[c2 % 2].inc(s.dma_start(out=mvA[:, c2, :],
                                                    in_=gn_sp[:, c2 * 128:(c2 + 1) * 128],
                                                    transpose=True), 16)
            gn_xb = [(sXB[0].sem, sXB[0].n), (sXB[1].sem, sXB[1].n)]

            emit_optA(wout_in, None, func=None, x_src=x_src, x_dst=x_dst,
                      store_out=store_out, extra_ready=gn_xb)

        emit_ffn(x1_sp, x2_sp, store_out=(N_SUB == 2))
        if N_SUB >= 3:
            emit_attn_b(x2_sp, x3_sp, store_out=(N_SUB == 3))
        if N_SUB >= 4:
            emit_ffn(x3_sp, x2_sp, store_out=True)

    return nc


def _chunked_vec(v):
    return np.ascontiguousarray(v.reshape(KC, 128).T).astype(np.float32)


def _tile_wT(w):
    wt = np.ascontiguousarray(w.T)
    x = wt.reshape(KC, 128, KC, 128)
    return np.ascontiguousarray(np.transpose(x, (2, 1, 0, 3))).reshape(KC, 128, KC * 128).astype(np.float32)


def prepare_in_maps(inputs):
    x_full = np.asarray(inputs["L0_attn_a_hidden_states"], dtype=np.float32)

    wq = _tile_wT(np.asarray(inputs["L0_attn_a_Wq"], dtype=np.float32))
    wk = _tile_wT(np.asarray(inputs["L0_attn_a_Wk"], dtype=np.float32))
    wv = _tile_wT(np.asarray(inputs["L0_attn_a_Wv"], dtype=np.float32))
    wqa = _tile_wT(np.asarray(inputs["L0_attn_a_Wqa"], dtype=np.float32))
    wka = _tile_wT(np.asarray(inputs["L0_attn_a_Wka"], dtype=np.float32))
    bq = _chunked_vec(np.asarray(inputs["L0_attn_a_bq"], dtype=np.float32))
    bk = _chunked_vec(np.asarray(inputs["L0_attn_a_bk"], dtype=np.float32))
    bv = _chunked_vec(np.asarray(inputs["L0_attn_a_bv"], dtype=np.float32))
    bqa = _chunked_vec(np.asarray(inputs["L0_attn_a_bqa"], dtype=np.float32))
    bka = _chunked_vec(np.asarray(inputs["L0_attn_a_bka"], dtype=np.float32))
    g_row = np.asarray(inputs["L0_attn_a_ln_g"], dtype=np.float32).reshape(1, D)
    b_row = np.asarray(inputs["L0_attn_a_ln_b"], dtype=np.float32).reshape(1, D)
    ones_l = np.ones((1, 128), dtype=ml_dtypes.bfloat16)
    idn = np.eye(128, dtype=np.float32).astype(ml_dtypes.bfloat16)

    wbase_T = np.ascontiguousarray(np.asarray(inputs["L0_ffn_W_base"], dtype=np.float32).T)
    wbase_t = np.ascontiguousarray(
        np.transpose(wbase_T.reshape(KC, 128, 2, 1024), (2, 1, 0, 3))).reshape(2, 128, KC * 1024)
    rt = np.transpose(np.ascontiguousarray(
        np.asarray(inputs["L0_ffn_W_router"], dtype=np.float32).T).reshape(KC, 128, 8), (1, 0, 2))
    a_flat = np.asarray(inputs["L0_ffn_A_w"], dtype=np.float32).reshape(64, D)
    at = np.transpose(np.ascontiguousarray(a_flat.T).reshape(KC, 128, 64), (1, 0, 2))
    rat = np.ascontiguousarray(np.concatenate([rt, at], axis=2)).astype(np.float32)
    bt = np.ascontiguousarray(
        np.transpose(np.asarray(inputs["L0_ffn_B_w"], dtype=np.float32), (0, 2, 1)).reshape(64, D))
    bb = np.asarray(inputs["L0_ffn_b_base"], dtype=np.float32).reshape(1, D)

    def _tile_optA(w):
        wt = np.ascontiguousarray(np.asarray(w, dtype=np.float32).T)
        return np.ascontiguousarray(
            np.transpose(wt.reshape(KC, 128, 2, 1024), (2, 1, 0, 3))).reshape(2, 128, KC * 1024)

    wqkv = np.asarray(inputs["L0_attn_b_w_qkv"], dtype=np.float32)
    qkvw_t = np.stack([_tile_optA(wqkv[s * D:(s + 1) * D, :]) for s in range(3)])
    wgate_t = _tile_optA(inputs["L0_attn_b_w_gate"])
    wout_t = _tile_optA(inputs["L0_attn_b_w_out"])
    slope_col = np.ascontiguousarray(np.repeat(
        np.asarray(inputs["L0_attn_b_slope_rate"], dtype=np.float32).reshape(1, 16), 128, axis=0))
    tloc = (np.arange(2)[None, :] * 128 + np.arange(128)[:, None]).astype(np.float32)
    iota1 = np.ascontiguousarray(tloc + 1.0)
    iota2 = np.ascontiguousarray(255.0 - tloc)
    c256 = np.full((128, 1), 256.0, dtype=np.float32)
    iota_bj = np.ascontiguousarray(
        np.repeat((np.arange(8)[None, :] * 256.0).astype(np.float32), 128, axis=0))
    t2i = (np.arange(2)[:, None, None] * 128 + np.arange(128)[None, :, None])
    t1i = np.arange(256)[None, None, :]
    ind_c = np.ascontiguousarray(
        np.transpose((t1i >= t2i).astype(np.float32), (1, 0, 2)).reshape(128, 512)
    ).astype(ml_dtypes.bfloat16)
    t2g = (np.arange(2)[:, None, None] * 128 + np.arange(128)[None, :, None]).astype(np.float32)
    t1g = np.arange(256)[None, None, :].astype(np.float32)
    cmm = np.where(t1g >= t2g, t2g + 1.0, -1e30).astype(np.float32)
    cm = np.ascontiguousarray(np.transpose(cmm, (1, 0, 2)).reshape(128, 512))
    nw = np.asarray(inputs["L0_attn_b_norm_w"], dtype=np.float32).reshape(1, D)

    in_maps = []
    for c in range(NCORES):
        b, half = c // 2, c % 2
        flag = np.full((128, 1), float(half), dtype=np.float32)
        x_sl = np.ascontiguousarray(x_full[b, half * T:(half + 1) * T, :])
        in_maps.append(dict(
            x_in=x_sl, flag=flag, ones_lhsT=ones_l,
            wq_t=wq, wk_t=wk, wv_t=wv, wqa_t=wqa, wka_t=wka,
            bq_c=bq, bk_c=bk, bv_c=bv, bqa_c=bqa, bka_c=bka,
            g_row=g_row, b_row=b_row,
            wbase_t=wbase_t, rat=rat, bt=bt, bb_in=bb, idn=idn,
            qkvw_t=qkvw_t, wgate_t=wgate_t, wout_t=wout_t, slope_col=slope_col,
            iota1=iota1, iota2=iota2, c256=c256, cm=cm, iota_bj=iota_bj, ind_c=ind_c,
            nw_in_row=nw,
        ))
    return in_maps


def kernel(**inputs):
    in_maps = prepare_in_maps(inputs)
    nc = build_program()
    res = run_bass_kernel_spmd(nc, in_maps, core_ids=list(range(NCORES)))
    global LAST_RESULTS
    LAST_RESULTS = res.results
    out = np.empty((4, 4096, D), dtype=np.float32)
    for c in range(NCORES):
        b, half = c // 2, c % 2
        out[b, half * T:(half + 1) * T, :] = np.asarray(res.results[c]["out"], dtype=np.float32)
    return out
